# version 01
# baseline (speedup 1.0000x reference)
import numpy as np
import contextlib
import concourse.bass as bass
import concourse.mybir as mybir

F32 = mybir.dt.float32
BF16 = mybir.dt.bfloat16
AF = mybir.ActivationFunctionType
ALU = mybir.AluOpType

H, DK, DV = 16, 64, 64
D = 1024
NHID = 4096
CEXP = 12.0
EPS = 1e-5


class Sem:
    def __init__(self, nc, st, name):
        self.h = st.enter_context(nc.semaphore(name))
        self.n = 0

    def inc(self, ins, k=1):
        ins.then_inc(self.h, k)
        self.n += k
        return ins


def host_constants():
    ident = np.eye(128, dtype=np.float32)
    m = np.zeros((128, 128), dtype=np.float32)
    for c2 in range(16):
        for t2 in range(8):
            for c1 in range(16):
                for t1 in range(8):
                    m[8 * c2 + t2, 8 * c1 + t1] = 1.0 if (16 * t2 + c2) <= (16 * t1 + c1) else 0.0
    return {"ident": ident, "diagmask": m}


def build(nb=4, nsegs=4, debug=(), dbg_tile=None, skip_ffn=False, skip_wo=False, stages=None, detect=False):
    nc = bass.Bass("TRN2", target_bir_lowering=False, detect_race_conditions=detect)
    S = 2048
    if dbg_tile is None:
        dbg_tile = (0, nsegs - 1)

    xs = nc.dram_tensor("xs", [nb, S, D], F32, kind="ExternalInput").ap()
    Wq = nc.dram_tensor("Wq", [D, D], F32, kind="ExternalInput").ap()
    Wk = nc.dram_tensor("Wk", [D, D], F32, kind="ExternalInput").ap()
    Wv = nc.dram_tensor("Wv", [D, D], F32, kind="ExternalInput").ap()
    Wo = nc.dram_tensor("Wo", [D, D], F32, kind="ExternalInput").ap()
    w1 = nc.dram_tensor("w1", [D, NHID], F32, kind="ExternalInput").ap()
    b1 = nc.dram_tensor("b1", [NHID], F32, kind="ExternalInput").ap()
    w2 = nc.dram_tensor("w2", [NHID, D], F32, kind="ExternalInput").ap()
    b2 = nc.dram_tensor("b2", [D], F32, kind="ExternalInput").ap()
    betas = nc.dram_tensor("betas", [H, DV], F32, kind="ExternalInput").ap()
    m0_arr = nc.dram_tensor("m0_arr", [128, 512], F32, kind="ExternalInput").ap()
    z0_arr = nc.dram_tensor("z0_arr", [128, 8], F32, kind="ExternalInput").ap()
    ys = nc.dram_tensor("ys", [nb, S, D], F32, kind="ExternalOutput").ap()

    consts = host_constants()
    ident_dram = nc.inline_tensor(consts["ident"], "identc")
    mask_dram = nc.inline_tensor(consts["diagmask"], "maskc")

    w1_scr = nc.dram_tensor("w1_scr", [128, 32, 8, 128], BF16, kind="Internal").ap()
    w2_scr = nc.dram_tensor("w2_scr", [32, 128, D], BF16, kind="Internal").ap()
    gate_scr = nc.dram_tensor("gate_scr", [H, 65], F32, kind="Internal").ap()
    omg_scr = nc.dram_tensor("omg_scr", [H, 65], F32, kind="Internal").ap()

    dbg_specs = {
        "xn_fm": ([128, 8 * 512], BF16), "qd2": ([128, 4096], BF16),
        "kd2": ([128, 4096], BF16), "vd2": ([128, 4096], BF16),
        "vposm": ([128, 4160], BF16), "vs": ([128, 4160], BF16),
        "skp": ([128, 4096], BF16), "pt": ([128, 4096], BF16),
        "att": ([128, 512], BF16), "atto": ([128, 4096], BF16),
        "acc": ([128, 4096], F32), "accw": ([128, 4096], F32), "zmf": ([128, 520], F32),
        "sq": ([128, 512], BF16), "hsb": ([128, 2048], BF16),
        "xn_tm": ([128, 4096], BF16),
    }
    dbg_drams = {k: nc.dram_tensor("dbg_" + k, *dbg_specs[k], kind="ExternalOutput").ap()
                 for k in debug}

    with contextlib.ExitStack() as st:
        e = st.enter_context

        def sb(name, shape, dt):
            return e(nc.sbuf_tensor(name, shape, dt))

        Wq_sb = sb("Wq_sb", [128, 8 * 1024], BF16)
        Wk_sb = sb("Wk_sb", [128, 8 * 1024], BF16)
        Wv_sb = sb("Wv_sb", [128, 8 * 1024], BF16)
        Wo_sb = sb("Wo_sb", [128, 8 * 1024], BF16)
        w1b = sb("w1b", [128, 3 * 1024], BF16)
        w2b = sb("w2b", [128, 3 * 1024], BF16)
        acc = sb("acc", [128, 4 * 1024], F32)
        xn_tm = sb("xn_tm", [128, 4 * 1024], BF16)
        xn_fm = sb("xn_fm", [128, 8 * 512], BF16)
        qd2 = sb("qd2", [128, 4096], BF16)
        kd2 = sb("kd2", [128, 4096], BF16)
        vd2 = sb("vd2", [128, 4096], BF16)
        vposm = sb("vposm", [128, 4160], BF16)
        vs = sb("vs", [128, 4160], BF16)
        skp = sb("skp", [128, 4096], BF16)
        pt = vd2
        att_p = sb("att_p", [128, 512], BF16)
        memn = sb("memn", [128, 512], BF16)
        sq_p = sb("sq_p", [128, 512], BF16)
        elu_t = sb("elu_t", [128, 1024], BF16)
        atto = xn_tm
        hsb = sb("hsb", [128, 32 * 512], BF16)
        sscr = sb("sscr", [128, 65], F32)
        zmf = sb("zmf", [128, 8 * 65], F32)
        zmb = sb("zmb", [128, 8 * 65], BF16)
        gmp = sb("gmp", [128, 8 * 65], F32)
        omg130 = sb("omg130", [128, 8 * 130], BF16)
        b2bc = sb("b2bc", [128, 1024], F32)
        b1_sb = sb("b1_sb", [128, 32], F32)
        mask8 = sb("mask8", [128, 128], BF16)
        identb = sb("identb", [128, 128], BF16)
        gstage = sb("gstage", [H, 65], F32)
        stats = sb("stats", [128, 16], F32)
        rd4 = sb("rd4", [128, 16], F32)
        eps_sb = sb("eps_sb", [128, 1], F32)
        negc_sb = sb("negc_sb", [128, 1], F32)
        tiny_sb = sb("tiny_sb", [128, 1], F32)
        onesb = sb("onesb", [1, 1], F32)

        wstage = hsb[:, 0:2 * 4096].bitcast(BF16) if False else hsb

        pa = e(nc.psum_tensor("pa", [128, 1024], F32))
        pb = e(nc.psum_tensor("pb", [128, 1024], F32))
        pc = e(nc.psum_tensor("pc", [128, 1024], F32))
        pd = e(nc.psum_tensor("pd", [128, 1024], F32))
        pc_bf = pc.ap().bitcast(BF16)
        pd_bf = pd.ap().bitcast(BF16)

        s_ld = Sem(nc, st, "s_ld")
        s_ldx = Sem(nc, st, "s_ldx")
        s_state = Sem(nc, st, "s_state")
        s_w1s = [Sem(nc, st, f"s_w1s{i}") for i in range(3)]
        s_w2s = [Sem(nc, st, f"s_w2s{i}") for i in range(3)]
        s_stg = [Sem(nc, st, f"s_stg{i}") for i in range(2)]
        s_sts = [Sem(nc, st, f"s_sts{i}") for i in range(2)]
        s_wf1 = Sem(nc, st, "s_wf1")
        s_wf2 = Sem(nc, st, "s_wf2")
        s_mm = Sem(nc, st, "s_mm")
        s_ev = Sem(nc, st, "s_ev")
        s_sc = Sem(nc, st, "s_sc")
        s_st = Sem(nc, st, "s_st")
        block = e(nc.Block())

        T, V, SC, SY, G = block.tensor, block.vector, block.scalar, block.sync, block.gpsimd

        def dump(name, ap_src, waits):
            if name not in dbg_drams:
                return
            dst = dbg_drams[name]

            def _f(sy, dst=dst, ap_src=ap_src, waits=tuple(waits)):
                for sem, n in waits:
                    sy.wait_ge(sem.h, n)
                s_st.inc(sy.dma_start(out=dst[:, :], in_=ap_src), 16)
            SY(_f)

        def p0_gpsimd(g):
            for W, Wsb in ((Wq, Wq_sb), (Wk, Wk_sb), (Wv, Wv_sb), (Wo, Wo_sb)):
                for kc in range(8):
                    s_ld.inc(g.dma_start(out=Wsb[:, kc * 1024:(kc + 1) * 1024],
                                         in_=W[128 * kc:128 * (kc + 1), :]), 16)
            s_ld.inc(g.dma_start(out=identb[:, :], in_=ident_dram.ap()[:, :]), 16)
            s_ld.inc(g.dma_start(out=mask8[:, 0:128], in_=mask_dram.ap()[:, :]), 16)
            s_ld.inc(g.dma_start(out=b2bc[:, :],
                                 in_=b2.unsqueeze(0).partition_broadcast(128)), 16)
            s_ld.inc(g.dma_start(out=gstage[:, 1:65], in_=betas[:, :]), 16)
            s_ld.inc(g.dma_start(out=acc[0:1, 0:4096], in_=b1.unsqueeze(0)), 16)
        G(p0_gpsimd)
        ld0 = s_ld.n

        def p0_vector(v):
            v.memset(eps_sb[:, :], EPS)
            v.memset(negc_sb[:, :], -CEXP)
            v.memset(tiny_sb[:, :], 1e-35)
            v.memset(onesb[:, :], 1.0)
            v.memset(gstage[:, 0:1], 1.0)
            v.memset(vposm[:, :].rearrange("p (b c) -> p b c", c=65)[:, :, 0], 1.0)
            v.memset(vs[:, :].rearrange("p (b c) -> p b c", c=65)[:, :, 0], 1.0)
            v.wait_ge(s_ld.h, ld0)
            s_ev.inc(v.memset(rd4[:, 0:1], 0.0))
        V(p0_vector)
        ev0 = s_ev.n

        def p0_scalar(s):
            s.wait_ge(s_ld.h, ld0)
            s.wait_ge(s_ev.h, ev0)
            s_sc.inc(s.activation(out=gstage[:, 1:65], in_=gstage[:, 1:65], func=AF.Sigmoid))
        SC(p0_scalar)
        sc_gate = s_sc.n

        def p0_vec2(v):
            v.wait_ge(s_sc.h, sc_gate)
            v.tensor_scalar(out=sscr[0:H, 0:65], in0=gstage[:, 0:65],
                            scalar1=1.0, scalar2=-1.0, op0=ALU.subtract, op1=ALU.mult)
            v.drain()
            s_ev.inc(v.memset(sscr[0:H, 0:1], 1.0))
        V(p0_vec2)
        ev_omg = s_ev.n

        def p0_b1_t(t):
            t.wait_ge(s_ld.h, ld0)
            t.wait_ge(s_ev.h, ev0)
            ins = None
            for mc in range(32):
                ins = t.matmul(pa[0:128, mc:mc + 1], acc[0:1, 128 * mc:128 * (mc + 1)],
                               onesb[0:1, 0:1], start=True, stop=True)
            s_mm.inc(ins)
        T(p0_b1_t)
        mm_b1 = s_mm.n

        def p0_b1_v(v):
            v.wait_ge(s_mm.h, mm_b1)
            s_ev.inc(v.tensor_copy(out=b1_sb[:, :], in_=pa[:, 0:32]))
        V(p0_b1_v)
        ev_b1 = s_ev.n

        def p0_gates_sy(sy):
            sy.wait_ge(s_sc.h, sc_gate)
            s_st.inc(sy.dma_start(out=gate_scr[:, :], in_=gstage[:, :]), 16)
            sy.wait_ge(s_ev.h, ev_omg)
            s_st.inc(sy.dma_start(out=omg_scr[:, :], in_=sscr[0:H, 0:65]), 16)
            sy.wait_ge(s_st.h, s_st.n)
            for g_ in range(8):
                for hp in range(2):
                    s_state.inc(sy.dma_start(
                        out=gmp[64 * hp:64 * (hp + 1), g_ * 65:(g_ + 1) * 65],
                        in_=gate_scr[2 * g_ + hp:2 * g_ + hp + 1, :].partition_broadcast(64)), 16)
        SY(p0_gates_sy)

        def p0_omg_g(g):
            g.wait_ge(s_st.h, s_st.n)
            for g_ in range(8):
                for hp in range(2):
                    s_ld.inc(g.dma_start(
                        out=omg130[:, (g_ * 2 + hp) * 65:(g_ * 2 + hp + 1) * 65],
                        in_=omg_scr[2 * g_ + hp:2 * g_ + hp + 1, :].partition_broadcast(128)), 16)
        G(p0_omg_g)

        def p0_w1_g(g):
            for kc in range(8):
                if kc >= 2:
                    g.wait_ge(s_sts[kc % 2].h, 16 * (kc // 2))
                s_stg[kc % 2].inc(g.dma_start(
                    out=wstage[:, (kc % 2) * 4096:(kc % 2) * 4096 + 4096],
                    in_=w1[128 * kc:128 * (kc + 1), :]), 16)
        G(p0_w1_g)

        def p0_w1_sy(sy):
            for kc in range(8):
                sy.wait_ge(s_stg[kc % 2].h, 16 * (kc // 2 + 1))
                src = wstage[:, (kc % 2) * 4096:(kc % 2) * 4096 + 4096].rearrange(
                    "p (mc c) -> p mc c", mc=32)
                dst = w1_scr[:, :, kc, :]
                s_sts[kc % 2].inc(sy.dma_start(out=dst, in_=src), 16)
        SY(p0_w1_sy)

        w1_sts_base = [s.n for s in s_sts]
        w1_stg_base = [s.n for s in s_stg]

        def p0_w2_g(g):
            for kc in range(32):
                if kc == 0:
                    g.wait_ge(s_sts[0].h, w1_sts_base[0])
                    g.wait_ge(s_sts[1].h, w1_sts_base[1])
                if kc >= 2:
                    g.wait_ge(s_sts[kc % 2].h, w1_sts_base[kc % 2] + 16 * (kc // 2))
                s_stg[kc % 2].inc(g.dma_start(
                    out=wstage[:, (kc % 2) * 4096:(kc % 2) * 4096 + 1024],
                    in_=w2[128 * kc:128 * (kc + 1), :]), 16)
        G(p0_w2_g)

        def p0_w2_sy(sy):
            for kc in range(32):
                sy.wait_ge(s_stg[kc % 2].h, w1_stg_base[kc % 2] + 16 * (kc // 2 + 1))
                s_sts[kc % 2].inc(sy.dma_start(
                    out=w2_scr[kc, :, :],
                    in_=wstage[:, (kc % 2) * 4096:(kc % 2) * 4096 + 1024]), 16)
        SY(p0_w2_sy)

        P0_LD = s_ld.n
        P0_ST = s_st.n
        P0_MM = s_mm.n

        for b in range(nb):
            if stages == "xn":
                pass
            def init_state_sy(sy, b=b, mmn=s_mm.n, evn=s_ev.n):
                sy.wait_ge(s_st.h, P0_ST)
                if b > 0:
                    sy.wait_ge(s_mm.h, mmn)
                    sy.wait_ge(s_ev.h, evn)
                s_state.inc(sy.dma_start(
                    out=zmf[:, :].rearrange("p (g c) -> p g c", g=8)[:, :, 0],
                    in_=z0_arr[:, :], allow_slow_non_contiguous=True), 16)
                s_state.inc(sy.dma_start(
                    out=zmf[:, :].rearrange("p (g c) -> p g c", g=8)[:, :, 1:65],
                    in_=m0_arr[:, :].rearrange("p (g v) -> p g v", g=8)), 16)
            SY(init_state_sy)
            ld_state = s_state.n

            def init_state_v(v, b=b):
                v.wait_ge(s_state.h, ld_state)
                if b == 0:
                    v.wait_ge(s_ld.h, P0_LD)
                s_ev.inc(v.tensor_copy(out=zmb[:, :], in_=zmf[:, :]))
            V(init_state_v)
            ev_state0 = s_ev.n

            for seg in range(nsegs):
                is_dbg = (b, seg) == dbg_tile and dbg_drams

                def x_load(sy, b=b, seg=seg, stn=s_st.n, mmn=P0_MM):
                    sy.wait_ge(s_st.h, stn)
                    if b == 0 and seg == 0:
                        sy.wait_ge(s_mm.h, mmn)
                    for ic in range(4):
                        s_ldx.inc(sy.dma_start(
                            out=acc[:, ic * 1024:(ic + 1) * 1024],
                            in_=xs[b, seg * 512 + 128 * ic: seg * 512 + 128 * (ic + 1), :]), 16)
                SY(x_load)
                ld_x = s_ldx.n

                wf1_base = s_wf1.n
                wf2_base = s_wf2.n
                w1_ready_at = {}
                w2_ready_at = {}

                def w_prefetch(sy, w1_ready_at=w1_ready_at, w2_ready_at=w2_ready_at):
                    for i in range(2):
                        sy.wait_ge(s_sts[i].h, s_sts[i].n)
                    for mc in range(32):
                        need = wf1_base + mc - 2
                        if need > 0:
                            sy.wait_ge(s_wf1.h, need)
                        slot = mc % 3
                        ins = sy.dma_start(
                            out=w1b[:, slot * 1024:(slot + 1) * 1024],
                            in_=w1_scr[:, mc, :, :])
                        s_w1s[slot].inc(ins, 16)
                        w1_ready_at[mc] = (slot, s_w1s[slot].n)
                    for p_ in range(2):
                        for kc in range(32):
                            need = wf2_base + 32 * p_ + kc - 2
                            if need > 0:
                                sy.wait_ge(s_wf2.h, need)
                            slot = (32 * p_ + kc) % 3
                            ins = sy.dma_start(
                                out=w2b[:, slot * 1024:(slot + 1) * 1024],
                                in_=w2_scr[kc, :, :])
                            s_w2s[slot].inc(ins, 16)
                            w2_ready_at[(p_, kc)] = (slot, s_w2s[slot].n)
                if stages != "xn":
                    SY(w_prefetch)

                if seg > 0:
                    def zmb_cast(v):
                        v.drain()
                        s_ev.inc(v.tensor_copy(out=zmb[:, :], in_=zmf[:, :]))
                    V(zmb_cast)

                def ln_v(v):
                    v.wait_ge(s_ldx.h, ld_x)
                    ins = None
                    for ic in range(4):
                        ins = v.reduce_sum(out=stats[:, ic:ic + 1],
                                           in_=acc[:, ic * 1024:(ic + 1) * 1024],
                                           axis=mybir.AxisListType.X)
                    s_ev.inc(ins)
                V(ln_v)
                ev_ln1 = s_ev.n

                def ln_s(s, b=b, seg=seg, sts_tot=[x.n for x in s_sts]):
                    if b == 0 and seg == 0:
                        s.wait_ge(s_sts[0].h, sts_tot[0])
                        s.wait_ge(s_sts[1].h, sts_tot[1])
                    s.wait_ge(s_ldx.h, ld_x)
                    ins = None
                    for ic in range(4):
                        ins = s.activation(out=hsb[:, 0:1024], in_=acc[:, ic * 1024:(ic + 1) * 1024],
                                           func=AF.Square, accum_out=stats[:, 4 + ic:5 + ic])
                    s_sc.inc(ins)
                SC(ln_s)
                sc_ln = s_sc.n

                def ln_v2(v):
                    v.wait_ge(s_sc.h, sc_ln)
                    v.wait_ge(s_ev.h, ev_ln1)
                    v.tensor_scalar(out=stats[:, 8:12], in0=stats[:, 0:4], scalar1=1.0 / D,
                                    scalar2=None, op0=ALU.mult)
                    v.tensor_scalar(out=stats[:, 4:8], in0=stats[:, 4:8], scalar1=1.0 / D,
                                    scalar2=None, op0=ALU.mult)
                    v.drain()
                    v.tensor_tensor(out=stats[:, 12:16], in0=stats[:, 8:12],
                                    in1=stats[:, 8:12], op=ALU.mult)
                    v.drain()
                    s_ev.inc(v.tensor_tensor(out=stats[:, 12:16], in0=stats[:, 4:8],
                                             in1=stats[:, 12:16], op=ALU.subtract))
                V(ln_v2)
                ev_var = s_ev.n

                def ln_s2(s):
                    s.wait_ge(s_ev.h, ev_var)
                    s.activation(out=stats[:, 12:16], in_=stats[:, 12:16],
                                 func=AF.Ln, bias=eps_sb[:, :])
                    s.drain()
                    s_sc.inc(s.activation(out=stats[:, 12:16], in_=stats[:, 12:16],
                                          func=AF.Exp, scale=-0.5))
                SC(ln_s2)
                sc_rstd = s_sc.n

                def ln_v3(v):
                    v.wait_ge(s_sc.h, sc_rstd)
                    for ic in range(4):
                        s_ev.inc(v.tensor_scalar(
                            out=xn_tm[:, ic * 1024:(ic + 1) * 1024],
                            in0=acc[:, ic * 1024:(ic + 1) * 1024],
                            scalar1=stats[:, 8 + ic:9 + ic], scalar2=stats[:, 12 + ic:13 + ic],
                            op0=ALU.subtract, op1=ALU.mult))
                    v.drain()
                    ins = None
                    for ic in range(4):
                        ins = v.tensor_tensor(out=acc[:, ic * 1024:(ic + 1) * 1024],
                                              in0=acc[:, ic * 1024:(ic + 1) * 1024],
                                              in1=b2bc[:, :], op=ALU.add)
                    s_ev.inc(ins)
                V(ln_v3)
                ev_xn = s_ev.n

                if is_dbg:
                    dump("xn_tm", xn_tm[:, :], [(s_ev, ev_xn)])

                mm_xnT = s_mm.n
                ev_xnT = s_ev.n

                sc_now = s_sc.n

                def xnT(t, sc_now=sc_now):
                    t.wait_ge(s_ev.h, ev_xn)
                    t.wait_ge(s_sc.h, sc_now)
                    t.wait_ge(s_ld.h, P0_LD)
                    for jc in range(8):
                        half = jc % 2
                        if jc >= 2:
                            t.wait_ge(s_ev.h, ev_xnT + (jc - 1))
                        ins = None
                        for ic in range(4):
                            ins = t.transpose(
                                pc_bf[:, half * 1024 + ic * 128: half * 1024 + (ic + 1) * 128],
                                xn_tm[:, ic * 1024 + 128 * jc: ic * 1024 + 128 * (jc + 1)],
                                identb[:, :])
                        s_mm.inc(ins)
                T(xnT)

                def xnT_ev(v):
                    for jc in range(8):
                        half = jc % 2
                        v.wait_ge(s_mm.h, mm_xnT + jc + 1)
                        s_ev.inc(v.tensor_copy(out=xn_fm[:, jc * 512:(jc + 1) * 512],
                                               in_=pc_bf[:, half * 1024:half * 1024 + 512]))
                V(xnT_ev)
                ev_xnfm = s_ev.n

                if is_dbg:
                    dump("xn_fm", xn_fm[:, :], [(s_ev, ev_xnfm)])
                if stages == "xn":
                    continue

                mm_proj = s_mm.n
                ev_proj_b = s_ev.n
                sc_proj_b = s_sc.n
                projs = [(Wq_sb, qd2), (Wk_sb, kd2), (Wv_sb, vd2)]

                def proj_t(t):
                    t.wait_ge(s_ev.h, ev_xnfm)
                    cnt = 0
                    for Wsb, dst in projs:
                        for jc in range(8):
                            p = pa if cnt % 2 == 0 else pb
                            if cnt >= 2:
                                t.wait_ge(s_ev.h, ev_proj_b + 2 * (cnt - 1))
                                t.wait_ge(s_sc.h, sc_proj_b + 2 * (cnt - 1))
                            ins = None
                            for kc in range(8):
                                ins = t.matmul(
                                    p[:, 0:512],
                                    Wsb[:, kc * 1024 + 128 * jc: kc * 1024 + 128 * (jc + 1)],
                                    xn_fm[:, kc * 512:(kc + 1) * 512],
                                    start=(kc == 0), stop=(kc == 7))
                            s_mm.inc(ins)
                            cnt += 1
                T(proj_t)

                def proj_ev(who):
                    def f(en):
                        cnt = 0
                        for Wsb, dst in projs:
                            for jc in range(8):
                                p = pa if cnt % 2 == 0 else pb
                                en.wait_ge(s_mm.h, mm_proj + cnt + 1)
                                cpar = who
                                for hp in range(2):
                                    src = p[64 * cpar:64 * (cpar + 1), 0:512].rearrange(
                                        "p (g two n tw) -> p g two n tw",
                                        g=8, two=2, n=4)[:, :, hp, :, :]
                                    dstv = dst[64 * hp:64 * (hp + 1), :].rearrange(
                                        "p (g n c tw) -> p g n c tw",
                                        g=8, n=4, c=16)[:, :, :, 2 * jc + cpar, :]
                                    if who == 0:
                                        ins = en.tensor_copy(out=dstv, in_=src)
                                        s_ev.inc(ins)
                                    else:
                                        ins = en.copy(out=dstv, in_=src)
                                        s_sc.inc(ins)
                                cnt += 1
                    return f
                V(proj_ev(0))
                SC(proj_ev(1))
                ev_proj = s_ev.n
                sc_proj = s_sc.n

                if is_dbg:
                    dump("qd2", qd2[:, :], [(s_ev, ev_proj), (s_sc, sc_proj)])
                    dump("kd2", kd2[:, :], [(s_ev, ev_proj), (s_sc, sc_proj)])
                    dump("vd2", vd2[:, :], [(s_ev, ev_proj), (s_sc, sc_proj)])

                mm_vpos = s_mm.n
                ev_vpos_b = s_ev.n
                sc_vpos_b = s_sc.n

                def vpos_t(t):
                    t.wait_ge(s_ev.h, ev_proj)
                    t.wait_ge(s_sc.h, sc_proj)
                    for g_ in range(8):
                        half = g_ % 2
                        if g_ >= 2:
                            t.wait_ge(s_ev.h, ev_vpos_b + (g_ - 1))
                            t.wait_ge(s_sc.h, sc_vpos_b + 4 * (g_ - 1))
                        ins = None
                        for n in range(4):
                            ins = t.transpose(
                                pc_bf[:, half * 1024 + n * 128: half * 1024 + (n + 1) * 128],
                                vd2[:, g_ * 512 + 128 * n: g_ * 512 + 128 * (n + 1)],
                                identb[:, :])
                        s_mm.inc(ins)
                T(vpos_t)

                def vpos_ev(v):
                    for g_ in range(8):
                        half = g_ % 2
                        v.wait_ge(s_mm.h, mm_vpos + g_ + 1)
                        src = pc_bf[:, half * 1024: half * 1024 + 512].rearrange(
                            "p (n hp v) -> p n hp v", n=4, hp=2)
                        dst = vposm[:, g_ * 520:(g_ + 1) * 520].rearrange(
                            "p (n hp c) -> p n hp c", n=4, hp=2)[:, :, :, 1:65]
                        s_ev.inc(v.tensor_copy(out=dst, in_=src))
                        for n in range(4):
                            src2 = pc_bf[:, half * 1024 + n * 128: half * 1024 + n * 128 + 128].rearrange(
                                "p (hp v) -> p hp v", hp=2)
                            dst2 = vs[:, g_ * 520 + n * 130: g_ * 520 + (n + 1) * 130].rearrange(
                                "p (hp c) -> p hp c", hp=2)[:, :, 1:65]
                            omg = omg130[:, :].rearrange(
                                "p (g2 hp c) -> p g2 hp c", g2=8, hp=2)[:, g_, :, 1:65]
                            s_sc.inc(v.tensor_tensor(out=dst2, in0=src2, in1=omg, op=ALU.mult))
                V(vpos_ev)
                ev_vpos = s_ev.n
                sc_vs = s_sc.n

                mm_kpos = s_mm.n
                ev_kpos_b = s_ev.n
                sc_kpos_b = s_sc.n

                def kpos_t(t):
                    t.wait_ge(s_ev.h, ev_vpos)
                    t.wait_ge(s_sc.h, sc_vs)
                    for g_ in range(8):
                        half = g_ % 2
                        if g_ >= 2:
                            t.wait_ge(s_ev.h, ev_kpos_b + 2 * (g_ - 1))
                        ins = None
                        for n in range(4):
                            ins = t.transpose(
                                pd_bf[:, half * 1024 + n * 128: half * 1024 + (n + 1) * 128],
                                kd2[:, g_ * 512 + 128 * n: g_ * 512 + 128 * (n + 1)],
                                identb[:, :])
                        s_mm.inc(ins)
                T(kpos_t)

                def kpos_v(v):
                    for g_ in range(8):
                        half = g_ % 2
                        v.wait_ge(s_mm.h, mm_kpos + g_ + 1)
                        el = elu_t[:, half * 512:(half + 1) * 512]
                        s_ev.inc(v.tensor_scalar(
                            out=el, in0=pd_bf[:, half * 1024:half * 1024 + 512],
                            scalar1=0.0, scalar2=None, op0=ALU.min))
                        v.wait_ge(s_sc.h, sc_kpos_b + g_ + 1)
                        s_ev.inc(v.scalar_tensor_tensor(
                            out=skp[:, g_ * 512:(g_ + 1) * 512],
                            in0=pd_bf[:, half * 1024:half * 1024 + 512],
                            scalar=0.0, in1=el, op0=ALU.max, op1=ALU.add))
                V(kpos_v)

                def kpos_s(s):
                    for g_ in range(8):
                        half = g_ % 2
                        el = elu_t[:, half * 512:(half + 1) * 512]
                        s.wait_ge(s_ev.h, ev_kpos_b + 2 * g_ + 1)
                        s_sc.inc(s.activation(out=el, in_=el, func=AF.Exp))
                SC(kpos_s)
                ev_kpos = s_ev.n
                sc_kpos = s_sc.n

                if is_dbg:
                    dump("vposm", vposm[:, :], [(s_ev, ev_kpos), (s_sc, sc_kpos)])
                    dump("vs", vs[:, :], [(s_ev, ev_kpos), (s_sc, sc_kpos)])
                    dump("skp", skp[:, :], [(s_ev, ev_kpos), (s_sc, sc_kpos)])

                last_ev_attO = None
                last_sc_attO = None
                for g_ in range(8):
                    mm_sc_b = s_mm.n
                    sc_exp_b = s_sc.n

                    def pair_scores(t, g_=g_, first=(g_ == 0), sc_exp_b=sc_exp_b):
                        if first:
                            t.wait_ge(s_ev.h, ev_kpos)
                            t.wait_ge(s_sc.h, sc_kpos)
                        for n in range(4):
                            psc = pc if n % 2 == 0 else pd
                            if n >= 2:
                                t.wait_ge(s_sc.h, sc_exp_b + (n - 1))
                            ins = None
                            for hp in range(2):
                                ins = t.matmul(
                                    psc[:, hp * 512 + 128 * n: hp * 512 + 512],
                                    kd2[64 * hp:64 * (hp + 1),
                                        g_ * 512 + 128 * n: g_ * 512 + 128 * (n + 1)],
                                    qd2[64 * hp:64 * (hp + 1),
                                        g_ * 512 + 128 * n: (g_ + 1) * 512],
                                    start=True, stop=True, tile_position=(64 * hp, 0))
                            s_mm.inc(ins)
                    T(pair_scores)

                    def pair_exp(s, g_=g_, mm_sc_b=mm_sc_b):
                        for n in range(4):
                            psc = pc if n % 2 == 0 else pd
                            s.wait_ge(s_mm.h, mm_sc_b + n + 1)
                            src = psc[:, :].rearrange("p (hp c) -> p hp c", hp=2)[:, :, 128 * n:512]
                            dst = pt[:, :].rearrange("p (hp n c) -> p hp n c",
                                                     hp=2, n=4)[:, :, n, 128 * n:512]
                            s_sc.inc(s.activation(out=dst, in_=src, func=AF.Exp,
                                                  bias=negc_sb[:, :], scale=0.125))
                    SC(pair_exp)
                    sc_exp = s_sc.n

                    def pair_mask(v, sc_exp=sc_exp):
                        v.wait_ge(s_sc.h, sc_exp)
                        ins = None
                        for hp in range(2):
                            for n in range(4):
                                base = hp * 2048 + n * 640
                                ins = v.tensor_tensor(
                                    out=pt[:, base:base + 128], in0=pt[:, base:base + 128],
                                    in1=mask8[:, 0:128], op=ALU.mult)
                        s_ev.inc(ins)
                    V(pair_mask)
                    ev_mask = s_ev.n

                    def pair_sq_min(v, g_=g_):
                        el = elu_t[:, (g_ % 2) * 512:(g_ % 2 + 1) * 512]
                        s_ev.inc(v.tensor_scalar(out=el,
                                                 in0=qd2[:, g_ * 512:(g_ + 1) * 512],
                                                 scalar1=0.0, scalar2=None, op0=ALU.min))
                    V(pair_sq_min)
                    ev_sqmin = s_ev.n

                    def pair_sq_exp(s, ev_sqmin=ev_sqmin, g_=g_):
                        el = elu_t[:, (g_ % 2) * 512:(g_ % 2 + 1) * 512]
                        s.wait_ge(s_ev.h, ev_sqmin)
                        s_sc.inc(s.activation(out=el, in_=el, func=AF.Exp))
                    SC(pair_sq_exp)
                    sc_sqexp = s_sc.n

                    def pair_sq_fin(v, g_=g_, sc_sqexp=sc_sqexp):
                        el = elu_t[:, (g_ % 2) * 512:(g_ % 2 + 1) * 512]
                        v.wait_ge(s_sc.h, sc_sqexp)
                        s_ev.inc(v.scalar_tensor_tensor(
                            out=sq_p[:, :], in0=qd2[:, g_ * 512:(g_ + 1) * 512],
                            scalar=0.0, in1=el, op0=ALU.max, op1=ALU.add))
                    V(pair_sq_fin)
                    ev_sq = s_ev.n

                    mm_mem_b = s_mm.n

                    def pair_mem(t, g_=g_, ev_sq=ev_sq, sc_exp=sc_exp):
                        t.wait_ge(s_ev.h, ev_sq)
                        t.wait_ge(s_sc.h, sc_exp)
                        for hp in range(2):
                            ins = None
                            for m in range(4):
                                ins = t.matmul(
                                    pd[:, hp * 512 + m * 65: hp * 512 + (m + 1) * 65],
                                    sq_p[64 * hp:64 * (hp + 1), 128 * m:128 * (m + 1)],
                                    zmb[64 * hp:64 * (hp + 1), g_ * 65:(g_ + 1) * 65],
                                    start=True, stop=True, tile_position=(64 * hp, 0))
                            s_mm.inc(ins)
                    T(pair_mem)
                    mm_mem = s_mm.n

                    mm_dot_b = s_mm.n

                    def pair_dot(t, g_=g_, ev_mask=ev_mask):
                        t.wait_ge(s_ev.h, ev_mask)
                        for hp in range(2):
                            for m in range(4):
                                ins = None
                                for n in range(m + 1):
                                    ins = t.matmul(
                                        pc[:, hp * 512 + m * 65: hp * 512 + (m + 1) * 65],
                                        pt[:, hp * 2048 + n * 512 + 128 * m:
                                           hp * 2048 + n * 512 + 128 * (m + 1)],
                                        vs[:, g_ * 520 + n * 130 + hp * 65:
                                           g_ * 520 + n * 130 + (hp + 1) * 65],
                                        start=(n == 0), stop=(n == m))
                                s_mm.inc(ins)
                    T(pair_dot)
                    mm_dot = s_mm.n

                    def pair_recip(v, mm_mem=mm_mem, mm_dot=mm_dot):
                        v.wait_ge(s_mm.h, mm_mem)
                        den = pd[:, :].rearrange("p (hp q) -> p hp q", hp=2)[:, :, 0:260]
                        den = den.rearrange("p hp (x c) -> p hp x c", c=65)[:, :, :, 0]
                        v.tensor_scalar(out=rd4[:, 0:8].rearrange("p (hp x) -> p hp x", hp=2),
                                        in0=den, scalar1=tiny_sb[:, :],
                                        scalar2=None, op0=ALU.add)
                        v.drain()
                        v.reciprocal(out=rd4[:, 0:8], in_=rd4[:, 0:8])
                        v.wait_ge(s_mm.h, mm_dot)
                        sden = pc[:, :].rearrange("p (hp q) -> p hp q", hp=2)[:, :, 0:260]
                        sden = sden.rearrange("p hp (x c) -> p hp x c", c=65)[:, :, :, 0]
                        s_ev.inc(v.reciprocal(out=rd4[:, 8:16].rearrange("p (hp x) -> p hp x", hp=2),
                                              in_=sden))
                    V(pair_recip)
                    ev_recip = s_ev.n

                    def pair_memn(s, ev_recip=ev_recip):
                        s.wait_ge(s_ev.h, ev_recip)
                        ins = None
                        for hp in range(2):
                            for m in range(4):
                                x = hp * 4 + m
                                ins = s.activation(
                                    out=memn[:, x * 64:(x + 1) * 64],
                                    in_=pd[:, hp * 512 + m * 65 + 1: hp * 512 + (m + 1) * 65],
                                    func=AF.Copy, scale=rd4[:, x:x + 1])
                        s_sc.inc(ins)
                    SC(pair_memn)
                    sc_memn = s_sc.n

                    def pair_comb(v, sc_memn=sc_memn):
                        v.wait_ge(s_sc.h, sc_memn)
                        ins = None
                        for hp in range(2):
                            for m in range(4):
                                x = hp * 4 + m
                                ins = v.scalar_tensor_tensor(
                                    out=att_p[:, x * 64:(x + 1) * 64],
                                    in0=pc[:, hp * 512 + m * 65 + 1: hp * 512 + (m + 1) * 65],
                                    scalar=rd4[:, 8 + x:9 + x],
                                    in1=memn[:, x * 64:(x + 1) * 64],
                                    op0=ALU.mult, op1=ALU.add)
                        s_ev.inc(ins)
                    V(pair_comb)
                    ev_att = s_ev.n

                    if is_dbg and g_ == 0:
                        dump("pt", pt[:, :], [(s_ev, ev_mask)])
                        dump("sq", sq_p[:, :], [(s_ev, ev_sq)])
                        dump("att", att_p[:, :], [(s_ev, ev_att)])

                    mm_mupd_b = s_mm.n

                    def pair_mupd(t, g_=g_, sc_memn=sc_memn):
                        t.wait_ge(s_sc.h, sc_memn)
                        for hp in range(2):
                            ins = None
                            for n in range(4):
                                ins = t.matmul(
                                    pd[64 * hp:64 * (hp + 1), 772:837],
                                    skp[:, g_ * 512 + n * 128 + hp * 64:
                                        g_ * 512 + n * 128 + (hp + 1) * 64],
                                    vposm[:, g_ * 520 + n * 130 + hp * 65:
                                          g_ * 520 + n * 130 + (hp + 1) * 65],
                                    start=(n == 0), stop=(n == 3))
                            s_mm.inc(ins)
                    T(pair_mupd)
                    mm_mupd = s_mm.n

                    def pair_state(v, g_=g_, mm_mupd=mm_mupd):
                        v.wait_ge(s_mm.h, mm_mupd)
                        v.tensor_tensor(out=sscr[:, 0:65], in0=pd[:, 772:837],
                                        in1=gmp[:, g_ * 65:(g_ + 1) * 65], op=ALU.mult)
                        v.drain()
                        s_ev.inc(v.tensor_tensor(out=zmf[:, g_ * 65:(g_ + 1) * 65],
                                                 in0=zmf[:, g_ * 65:(g_ + 1) * 65],
                                                 in1=sscr[:, 0:65], op=ALU.add))
                    V(pair_state)

                    mm_attT_b = s_mm.n

                    def pair_attT(t, ev_att=ev_att):
                        t.wait_ge(s_ev.h, ev_att)
                        ins = None
                        for hp in range(2):
                            for m in range(4):
                                x = hp * 4 + m
                                ins = t.transpose(
                                    pc_bf[64 * hp:64 * (hp + 1), 1280 + m * 128: 1280 + (m + 1) * 128],
                                    att_p[:, x * 64:(x + 1) * 64],
                                    identb[:, :])
                        s_mm.inc(ins)
                    T(pair_attT)
                    mm_attT = s_mm.n

                    def pair_attO(is_v, g_=g_, mm_attT=mm_attT):
                        def f(en):
                            en.wait_ge(s_mm.h, mm_attT)
                            for hp in range(2):
                                h_ = 2 * g_ + hp
                                for cpar in range(2):
                                    if (hp + cpar) % 2 != (0 if is_v else 1):
                                        continue
                                    src = pc_bf[64 * hp:64 * (hp + 1), 1280:1792].rearrange(
                                        "p (m a t2) -> p m a t2",
                                        m=4, a=8)[:, :, :, cpar * 8:(cpar + 1) * 8]
                                    src = src.transpose([0, 2, 1, 3])
                                    dst = atto[64 * cpar:64 * (cpar + 1), :].rearrange(
                                        "p (a i) -> p a i", a=8)[:, :, 32 * h_:32 * h_ + 32]
                                    dst = dst.rearrange("p a (m tw) -> p a m tw", m=4)
                                    if is_v:
                                        ins = en.tensor_copy(out=dst, in_=src)
                                        s_ev.inc(ins)
                                    else:
                                        ins = en.copy(out=dst, in_=src)
                                        s_sc.inc(ins)
                        return f
                    V(pair_attO(True))
                    SC(pair_attO(False))
                    last_ev_attO = s_ev.n
                    last_sc_attO = s_sc.n

                    def pair_guard(t, a=last_ev_attO, b_=last_sc_attO):
                        t.wait_ge(s_ev.h, a)
                        t.wait_ge(s_sc.h, b_)
                    T(pair_guard)

                if is_dbg:
                    dump("atto", atto[:, :], [(s_ev, last_ev_attO), (s_sc, last_sc_attO)])
                    dump("zmf", zmf[:, :], [(s_ev, s_ev.n)])

                mm_wo_b = s_mm.n
                ev_wo_b = s_ev.n

                def wo_t(t):
                    for ic in range(4):
                        p = pa if ic % 2 == 0 else pb
                        if ic >= 2:
                            t.wait_ge(s_ev.h, ev_wo_b + (ic - 1))
                        ins = None
                        for half in range(2):
                            for a in range(8):
                                ins = t.matmul(
                                    p[:, half * 512:(half + 1) * 512],
                                    atto[:, a * 512 + 128 * ic: a * 512 + 128 * (ic + 1)],
                                    Wo_sb[:, a * 1024 + half * 512: a * 1024 + (half + 1) * 512],
                                    start=(a == 0), stop=(a == 7))
                        s_mm.inc(ins)
                T(wo_t)

                def wo_ev(v, mm_wo_b=mm_wo_b):
                    for ic in range(4):
                        p = pa if ic % 2 == 0 else pb
                        v.wait_ge(s_mm.h, mm_wo_b + ic + 1)
                        if skip_wo:
                            s_ev.inc(v.tensor_copy(out=rd4[:, 1:2], in_=rd4[:, 1:2]))
                        else:
                            s_ev.inc(v.tensor_tensor(out=acc[:, ic * 1024:(ic + 1) * 1024],
                                                     in0=acc[:, ic * 1024:(ic + 1) * 1024],
                                                     in1=p[:, 0:1024], op=ALU.add))
                V(wo_ev)
                ev_wo = s_ev.n

                if is_dbg:
                    dump("accw", acc[:, :], [(s_ev, ev_wo)])

                mm_f1_b = s_mm.n
                if skip_ffn:
                    ev_f2 = s_ev.n

                def f1_t(t, wf1_base=wf1_base):
                    if skip_ffn:
                        t.wait_ge(s_ev.h, ev_wo)
                        for mc in range(32):
                            sl, nn = w1_ready_at[mc]
                            t.wait_ge(s_w1s[sl].h, nn)
                        for p_ in range(2):
                            for kc in range(32):
                                sl, nn = w2_ready_at[(p_, kc)]
                                t.wait_ge(s_w2s[sl].h, nn)
                        return
                    t.wait_ge(s_ev.h, ev_wo)
                    for mc in range(32):
                        p = pa if mc % 2 == 0 else pb
                        sl, nn = w1_ready_at[mc]
                        t.wait_ge(s_w1s[sl].h, nn)
                        if mc >= 2:
                            t.wait_ge(s_wf1.h, wf1_base + (mc - 1))
                        ins = None
                        for kc in range(8):
                            ins = t.matmul(
                                p[:, 0:512],
                                w1b[:, (mc % 3) * 1024 + kc * 128: (mc % 3) * 1024 + (kc + 1) * 128],
                                xn_fm[:, kc * 512:(kc + 1) * 512],
                                start=(kc == 0), stop=(kc == 7))
                        s_mm.inc(ins)
                T(f1_t)

                def f1_ev(s, mm_f1_b=mm_f1_b):
                    if skip_ffn:
                        for mc in range(32):
                            s_wf1.inc(s.engine_nop() if False else s.copy(out=elu_t[:, 0:1], in_=elu_t[:, 0:1]))
                        return
                    for mc in range(32):
                        p = pa if mc % 2 == 0 else pb
                        s.wait_ge(s_mm.h, mm_f1_b + mc + 1)
                        s_wf1.inc(s.activation(out=hsb[:, mc * 512:(mc + 1) * 512],
                                               in_=p[:, 0:512], func=AF.Gelu,
                                               bias=b1_sb[:, mc:mc + 1]))
                SC(f1_ev)
                wf1_done = s_wf1.n

                if is_dbg:
                    dump("hsb", hsb[:, 0:2048], [(s_wf1, wf1_done)])

                mm_f2_b = s_mm.n
                ev_f2_b = s_ev.n

                def f2_t(t, wf1_done=wf1_done):
                    if skip_ffn:
                        return
                    t.wait_ge(s_wf1.h, wf1_done)
                    for p_ in range(2):
                        if p_ == 1:
                            t.wait_ge(s_ev.h, ev_f2_b + 2)
                        for kc in range(32):
                            sl, nn = w2_ready_at[(p_, kc)]
                            t.wait_ge(s_w2s[sl].h, nn)
                            ins = None
                            for ic2 in range(2):
                                ic = 2 * p_ + ic2
                                p = pa if ic2 == 0 else pb
                                for half in range(2):
                                    ins = t.matmul(
                                        p[:, half * 512:(half + 1) * 512],
                                        hsb[:, kc * 512 + 128 * ic: kc * 512 + 128 * (ic + 1)],
                                        w2b[:, ((32 * p_ + kc) % 3) * 1024 + half * 512:
                                            ((32 * p_ + kc) % 3) * 1024 + (half + 1) * 512],
                                        start=(kc == 0), stop=(kc == 31))
                            s_wf2.inc(ins)
                T(f2_t)

                def f2_ev(v, wf2_base=wf2_base):
                    if skip_ffn:
                        ins = None
                        for i_ in range(64):
                            ins = v.tensor_copy(out=rd4[:, 0:1], in_=rd4[:, 0:1])
                            s_wf2.inc(ins)
                        return
                    for p_ in range(2):
                        v.wait_ge(s_wf2.h, wf2_base + 32 * (p_ + 1))
                        for ic2 in range(2):
                            ic = 2 * p_ + ic2
                            p = pa if ic2 == 0 else pb
                            s_ev.inc(v.tensor_tensor(
                                out=acc[:, ic * 1024:(ic + 1) * 1024],
                                in0=acc[:, ic * 1024:(ic + 1) * 1024],
                                in1=p[:, 0:1024], op=ALU.add))
                V(f2_ev)
                ev_f2 = s_ev.n

                if is_dbg:
                    dump("acc", acc[:, :], [(s_ev, ev_f2)])

                def store(sy, b=b, seg=seg, ev_f2=ev_f2):
                    sy.wait_ge(s_ev.h, ev_f2)
                    for ic in range(4):
                        s_st.inc(sy.dma_start(
                            out=ys[b, seg * 512 + 128 * ic: seg * 512 + 128 * (ic + 1), :],
                            in_=acc[:, ic * 1024:(ic + 1) * 1024]), 16)
                SY(store)

        def fin(sy):
            sy.wait_ge(s_st.h, s_st.n)
        SY(fin)

    return nc


def host_arrange_inputs(inp, nb=4, core=0):
    x = np.ascontiguousarray(inp["L0_attn_x"][core * nb:(core + 1) * nb])
    betas = np.ascontiguousarray(inp["L0_attn_betas"].reshape(H, DV))
    m0 = inp["L0_attn_memory0"].reshape(H, DK, DV)
    z0 = inp["L0_attn_z0"].reshape(H, DK)
    m0_arr = np.zeros((128, 512), dtype=np.float32)
    z0_arr = np.zeros((128, 8), dtype=np.float32)
    for g in range(8):
        for hp in range(2):
            h = 2 * g + hp
            m0_arr[64 * hp:64 * (hp + 1), g * 64:(g + 1) * 64] = m0[h]
            z0_arr[64 * hp:64 * (hp + 1), g] = z0[h]
    return {
        "xs": x, "Wq": inp["L0_attn_Wq"], "Wk": inp["L0_attn_Wk"],
        "Wv": inp["L0_attn_Wv"], "Wo": inp["L0_attn_Wo"],
        "w1": inp["L0_ffn_w1"], "b1": inp["L0_ffn_b1"],
        "w2": inp["L0_ffn_w2"], "b2": inp["L0_ffn_b2"],
        "betas": betas, "m0_arr": m0_arr, "z0_arr": z0_arr,
    }


_NCORES = 8


def kernel(**inputs):
    from concourse.bass_utils import run_bass_kernel_spmd
    in_maps = [host_arrange_inputs(inputs, nb=4, core=c) for c in range(_NCORES)]
    nc = build(nb=4, nsegs=4)
    res = run_bass_kernel_spmd(nc, in_maps, core_ids=list(range(_NCORES)))
    out = np.concatenate([res.results[c]["ys"] for c in range(_NCORES)], axis=0)
    return out.astype(np.float32)


# version 02
# speedup vs baseline: 1.0367x; 1.0367x over previous
import numpy as np
import contextlib
import concourse.bass as bass
import concourse.mybir as mybir

F32 = mybir.dt.float32
BF16 = mybir.dt.bfloat16
AF = mybir.ActivationFunctionType
ALU = mybir.AluOpType

H, DK, DV = 16, 64, 64
D = 1024
NHID = 4096
CEXP = 12.0
EPS = 1e-5


class Sem:
    def __init__(self, nc, st, name):
        self.h = st.enter_context(nc.semaphore(name))
        self.n = 0

    def inc(self, ins, k=1):
        ins.then_inc(self.h, k)
        self.n += k
        return ins


def host_constants():
    ident = np.eye(128, dtype=np.float32)
    m = np.zeros((128, 128), dtype=np.float32)
    for c2 in range(16):
        for t2 in range(8):
            for c1 in range(16):
                for t1 in range(8):
                    m[8 * c2 + t2, 8 * c1 + t1] = 1.0 if (16 * t2 + c2) <= (16 * t1 + c1) else 0.0
    return {"ident": ident, "diagmask": m}


def build(nb=4, nsegs=4, debug=(), dbg_tile=None, skip_ffn=False, skip_wo=False, stages=None, detect=False):
    nc = bass.Bass("TRN2", target_bir_lowering=False, detect_race_conditions=detect)
    S = 2048
    if dbg_tile is None:
        dbg_tile = (0, nsegs - 1)

    xs = nc.dram_tensor("xs", [nb, S, D], F32, kind="ExternalInput").ap()
    Wq = nc.dram_tensor("Wq", [D, D], F32, kind="ExternalInput").ap()
    Wk = nc.dram_tensor("Wk", [D, D], F32, kind="ExternalInput").ap()
    Wv = nc.dram_tensor("Wv", [D, D], F32, kind="ExternalInput").ap()
    Wo = nc.dram_tensor("Wo", [D, D], F32, kind="ExternalInput").ap()
    w1 = nc.dram_tensor("w1", [D, NHID], F32, kind="ExternalInput").ap()
    b1 = nc.dram_tensor("b1", [NHID], F32, kind="ExternalInput").ap()
    w2 = nc.dram_tensor("w2", [NHID, D], F32, kind="ExternalInput").ap()
    b2 = nc.dram_tensor("b2", [D], F32, kind="ExternalInput").ap()
    betas = nc.dram_tensor("betas", [H, DV], F32, kind="ExternalInput").ap()
    m0_arr = nc.dram_tensor("m0_arr", [128, 512], F32, kind="ExternalInput").ap()
    z0_arr = nc.dram_tensor("z0_arr", [128, 8], F32, kind="ExternalInput").ap()
    ys = nc.dram_tensor("ys", [nb, S, D], F32, kind="ExternalOutput").ap()

    consts = host_constants()
    ident_dram = nc.inline_tensor(consts["ident"], "identc")
    mask_dram = nc.inline_tensor(consts["diagmask"], "maskc")

    w1_scr = nc.dram_tensor("w1_scr", [128, 32, 8, 128], BF16, kind="Internal").ap()
    w2_scr = nc.dram_tensor("w2_scr", [32, 128, D], BF16, kind="Internal").ap()
    gate_scr = nc.dram_tensor("gate_scr", [H, 65], F32, kind="Internal").ap()
    omg_scr = nc.dram_tensor("omg_scr", [H, 65], F32, kind="Internal").ap()

    dbg_specs = {
        "xn_fm": ([128, 8 * 512], BF16), "qd2": ([128, 4096], BF16),
        "kd2": ([128, 4096], BF16), "vd2": ([128, 4096], BF16),
        "vposm": ([128, 4160], BF16), "vs": ([128, 4160], BF16),
        "skp": ([128, 4096], BF16), "pt": ([128, 4096], BF16),
        "att": ([128, 512], BF16), "atto": ([128, 4096], BF16),
        "acc": ([128, 4096], F32), "accw": ([128, 4096], F32), "zmf": ([128, 520], F32),
        "sq": ([128, 512], BF16), "hsb": ([128, 2048], BF16),
        "xn_tm": ([128, 4096], BF16),
    }
    dbg_drams = {k: nc.dram_tensor("dbg_" + k, *dbg_specs[k], kind="ExternalOutput").ap()
                 for k in debug}

    with contextlib.ExitStack() as st:
        e = st.enter_context

        def sb(name, shape, dt):
            return e(nc.sbuf_tensor(name, shape, dt))

        Wq_sb = sb("Wq_sb", [128, 8 * 1024], BF16)
        Wk_sb = sb("Wk_sb", [128, 8 * 1024], BF16)
        Wv_sb = sb("Wv_sb", [128, 8 * 1024], BF16)
        Wo_sb = sb("Wo_sb", [128, 8 * 1024], BF16)
        w1b = sb("w1b", [128, 3 * 1024], BF16)
        w2b = sb("w2b", [128, 3 * 1024], BF16)
        acc = sb("acc", [128, 4 * 1024], F32)
        xn_tm = sb("xn_tm", [128, 4 * 1024], BF16)
        xn_fm = sb("xn_fm", [128, 8 * 512], BF16)
        qd2 = sb("qd2", [128, 4096], BF16)
        kd2 = sb("kd2", [128, 4096], BF16)
        vd2 = sb("vd2", [128, 4096], BF16)
        vposm = sb("vposm", [128, 4160], BF16)
        vs = sb("vs", [128, 4160], BF16)
        skp = sb("skp", [128, 4096], BF16)
        pt = vd2
        att_p = sb("att_p", [128, 512], BF16)
        memn = sb("memn", [128, 512], BF16)
        sq_p = sb("sq_p", [128, 512], BF16)
        elu_t = sb("elu_t", [128, 1024], BF16)
        atto = xn_tm
        hsb = sb("hsb", [128, 32 * 512], BF16)
        sscr = sb("sscr", [128, 65], F32)
        zmf = sb("zmf", [128, 8 * 65], F32)
        zmb = sb("zmb", [128, 8 * 65], BF16)
        gmp = sb("gmp", [128, 8 * 65], F32)
        omg130 = sb("omg130", [128, 8 * 130], BF16)
        b2bc = sb("b2bc", [128, 1024], F32)
        b1_sb = sb("b1_sb", [128, 32], F32)
        mask8 = sb("mask8", [128, 128], BF16)
        identb = sb("identb", [128, 128], BF16)
        gstage = sb("gstage", [H, 65], F32)
        stats = sb("stats", [128, 16], F32)
        rd4 = sb("rd4", [128, 16], F32)
        eps_sb = sb("eps_sb", [128, 1], F32)
        negc_sb = sb("negc_sb", [128, 1], F32)
        tiny_sb = sb("tiny_sb", [128, 1], F32)
        onesb = sb("onesb", [1, 1], F32)

        wstage = hsb[:, 0:2 * 4096].bitcast(BF16) if False else hsb

        pa = e(nc.psum_tensor("pa", [128, 1024], F32))
        pb = e(nc.psum_tensor("pb", [128, 1024], F32))
        pc = e(nc.psum_tensor("pc", [128, 1024], F32))
        pd = e(nc.psum_tensor("pd", [128, 1024], F32))
        pc_bf = pc.ap().bitcast(BF16)
        pd_bf = pd.ap().bitcast(BF16)
        pa_bf = pa.ap().bitcast(BF16)

        s_ld = Sem(nc, st, "s_ld")
        s_ldx = Sem(nc, st, "s_ldx")
        s_state = Sem(nc, st, "s_state")
        s_w1s = [Sem(nc, st, f"s_w1s{i}") for i in range(3)]
        s_w2s = [Sem(nc, st, f"s_w2s{i}") for i in range(3)]
        s_stg = [Sem(nc, st, f"s_stg{i}") for i in range(2)]
        s_sts = [Sem(nc, st, f"s_sts{i}") for i in range(2)]
        s_wf1 = Sem(nc, st, "s_wf1")
        s_wf2 = Sem(nc, st, "s_wf2")
        s_mm = Sem(nc, st, "s_mm")
        s_ev = Sem(nc, st, "s_ev")
        s_sc = Sem(nc, st, "s_sc")
        s_st = Sem(nc, st, "s_st")
        block = e(nc.Block())

        T, V, SC, SY, G = block.tensor, block.vector, block.scalar, block.sync, block.gpsimd

        def dump(name, ap_src, waits):
            if name not in dbg_drams:
                return
            dst = dbg_drams[name]

            def _f(sy, dst=dst, ap_src=ap_src, waits=tuple(waits)):
                for sem, n in waits:
                    sy.wait_ge(sem.h, n)
                s_st.inc(sy.dma_start(out=dst[:, :], in_=ap_src), 16)
            SY(_f)

        def p0_gpsimd(g):
            for W, Wsb in ((Wq, Wq_sb), (Wk, Wk_sb), (Wv, Wv_sb), (Wo, Wo_sb)):
                for kc in range(8):
                    s_ld.inc(g.dma_start(out=Wsb[:, kc * 1024:(kc + 1) * 1024],
                                         in_=W[128 * kc:128 * (kc + 1), :]), 16)
            s_ld.inc(g.dma_start(out=identb[:, :], in_=ident_dram.ap()[:, :]), 16)
            s_ld.inc(g.dma_start(out=mask8[:, 0:128], in_=mask_dram.ap()[:, :]), 16)
            s_ld.inc(g.dma_start(out=b2bc[:, :],
                                 in_=b2.unsqueeze(0).partition_broadcast(128)), 16)
            s_ld.inc(g.dma_start(out=gstage[:, 1:65], in_=betas[:, :]), 16)
            s_ld.inc(g.dma_start(out=acc[0:1, 0:4096], in_=b1.unsqueeze(0)), 16)
        G(p0_gpsimd)
        ld0 = s_ld.n

        def p0_vector(v):
            v.memset(eps_sb[:, :], EPS)
            v.memset(negc_sb[:, :], -CEXP)
            v.memset(tiny_sb[:, :], 1e-35)
            v.memset(onesb[:, :], 1.0)
            v.memset(gstage[:, 0:1], 1.0)
            v.memset(vposm[:, :].rearrange("p (b c) -> p b c", c=65)[:, :, 0], 1.0)
            v.memset(vs[:, :].rearrange("p (b c) -> p b c", c=65)[:, :, 0], 1.0)
            v.wait_ge(s_ld.h, ld0)
            s_ev.inc(v.memset(rd4[:, 0:1], 0.0))
        V(p0_vector)
        ev0 = s_ev.n

        def p0_scalar(s):
            s.wait_ge(s_ld.h, ld0)
            s.wait_ge(s_ev.h, ev0)
            s_sc.inc(s.activation(out=gstage[:, 1:65], in_=gstage[:, 1:65], func=AF.Sigmoid))
        SC(p0_scalar)
        sc_gate = s_sc.n

        def p0_vec2(v):
            v.wait_ge(s_sc.h, sc_gate)
            v.tensor_scalar(out=sscr[0:H, 0:65], in0=gstage[:, 0:65],
                            scalar1=1.0, scalar2=-1.0, op0=ALU.subtract, op1=ALU.mult)
            v.drain()
            s_ev.inc(v.memset(sscr[0:H, 0:1], 1.0))
        V(p0_vec2)
        ev_omg = s_ev.n

        def p0_b1_t(t):
            t.wait_ge(s_ld.h, ld0)
            t.wait_ge(s_ev.h, ev0)
            ins = None
            for mc in range(32):
                ins = t.matmul(pa[0:128, mc:mc + 1], acc[0:1, 128 * mc:128 * (mc + 1)],
                               onesb[0:1, 0:1], start=True, stop=True)
            s_mm.inc(ins)
        T(p0_b1_t)
        mm_b1 = s_mm.n

        def p0_b1_v(v):
            v.wait_ge(s_mm.h, mm_b1)
            s_ev.inc(v.tensor_copy(out=b1_sb[:, :], in_=pa[:, 0:32]))
        V(p0_b1_v)
        ev_b1 = s_ev.n

        def p0_gates_sy(sy):
            sy.wait_ge(s_sc.h, sc_gate)
            s_st.inc(sy.dma_start(out=gate_scr[:, :], in_=gstage[:, :]), 16)
            sy.wait_ge(s_ev.h, ev_omg)
            s_st.inc(sy.dma_start(out=omg_scr[:, :], in_=sscr[0:H, 0:65]), 16)
            sy.wait_ge(s_st.h, s_st.n)
            for g_ in range(8):
                for hp in range(2):
                    s_state.inc(sy.dma_start(
                        out=gmp[64 * hp:64 * (hp + 1), g_ * 65:(g_ + 1) * 65],
                        in_=gate_scr[2 * g_ + hp:2 * g_ + hp + 1, :].partition_broadcast(64)), 16)
        SY(p0_gates_sy)

        def p0_omg_g(g):
            g.wait_ge(s_st.h, s_st.n)
            for g_ in range(8):
                for hp in range(2):
                    s_ld.inc(g.dma_start(
                        out=omg130[:, (g_ * 2 + hp) * 65:(g_ * 2 + hp + 1) * 65],
                        in_=omg_scr[2 * g_ + hp:2 * g_ + hp + 1, :].partition_broadcast(128)), 16)
        G(p0_omg_g)

        def p0_w1_g(g):
            for kc in range(8):
                if kc >= 2:
                    g.wait_ge(s_sts[kc % 2].h, 16 * (kc // 2))
                s_stg[kc % 2].inc(g.dma_start(
                    out=wstage[:, (kc % 2) * 4096:(kc % 2) * 4096 + 4096],
                    in_=w1[128 * kc:128 * (kc + 1), :]), 16)
        G(p0_w1_g)

        def p0_w1_sy(sy):
            for kc in range(8):
                sy.wait_ge(s_stg[kc % 2].h, 16 * (kc // 2 + 1))
                src = wstage[:, (kc % 2) * 4096:(kc % 2) * 4096 + 4096].rearrange(
                    "p (mc c) -> p mc c", mc=32)
                dst = w1_scr[:, :, kc, :]
                s_sts[kc % 2].inc(sy.dma_start(out=dst, in_=src), 16)
        SY(p0_w1_sy)

        w1_sts_base = [s.n for s in s_sts]
        w1_stg_base = [s.n for s in s_stg]

        def p0_w2_g(g):
            for kc in range(32):
                if kc == 0:
                    g.wait_ge(s_sts[0].h, w1_sts_base[0])
                    g.wait_ge(s_sts[1].h, w1_sts_base[1])
                if kc >= 2:
                    g.wait_ge(s_sts[kc % 2].h, w1_sts_base[kc % 2] + 16 * (kc // 2))
                s_stg[kc % 2].inc(g.dma_start(
                    out=wstage[:, (kc % 2) * 4096:(kc % 2) * 4096 + 1024],
                    in_=w2[128 * kc:128 * (kc + 1), :]), 16)
        G(p0_w2_g)

        def p0_w2_sy(sy):
            for kc in range(32):
                sy.wait_ge(s_stg[kc % 2].h, w1_stg_base[kc % 2] + 16 * (kc // 2 + 1))
                s_sts[kc % 2].inc(sy.dma_start(
                    out=w2_scr[kc, :, :],
                    in_=wstage[:, (kc % 2) * 4096:(kc % 2) * 4096 + 1024]), 16)
        SY(p0_w2_sy)

        P0_LD = s_ld.n
        P0_ST = s_st.n
        P0_MM = s_mm.n

        for b in range(nb):
            if stages == "xn":
                pass
            def init_state_sy(sy, b=b, mmn=s_mm.n, evn=s_ev.n):
                sy.wait_ge(s_st.h, P0_ST)
                if b > 0:
                    sy.wait_ge(s_mm.h, mmn)
                    sy.wait_ge(s_ev.h, evn)
                s_state.inc(sy.dma_start(
                    out=zmf[:, :].rearrange("p (g c) -> p g c", g=8)[:, :, 0],
                    in_=z0_arr[:, :], allow_slow_non_contiguous=True), 16)
                s_state.inc(sy.dma_start(
                    out=zmf[:, :].rearrange("p (g c) -> p g c", g=8)[:, :, 1:65],
                    in_=m0_arr[:, :].rearrange("p (g v) -> p g v", g=8)), 16)
            SY(init_state_sy)
            ld_state = s_state.n

            def init_state_v(v, b=b):
                v.wait_ge(s_state.h, ld_state)
                if b == 0:
                    v.wait_ge(s_ld.h, P0_LD)
                s_ev.inc(v.tensor_copy(out=zmb[:, :], in_=zmf[:, :]))
            V(init_state_v)
            ev_state0 = s_ev.n

            for seg in range(nsegs):
                is_dbg = (b, seg) == dbg_tile and dbg_drams

                def x_load(sy, b=b, seg=seg, stn=s_st.n, mmn=P0_MM):
                    sy.wait_ge(s_st.h, stn)
                    if b == 0 and seg == 0:
                        sy.wait_ge(s_mm.h, mmn)
                    for ic in range(4):
                        s_ldx.inc(sy.dma_start(
                            out=acc[:, ic * 1024:(ic + 1) * 1024],
                            in_=xs[b, seg * 512 + 128 * ic: seg * 512 + 128 * (ic + 1), :]), 16)
                SY(x_load)
                ld_x = s_ldx.n

                wf1_base = s_wf1.n
                wf2_base = s_wf2.n
                w1_ready_at = {}
                w2_ready_at = {}

                def w_prefetch(sy, w1_ready_at=w1_ready_at, w2_ready_at=w2_ready_at):
                    for i in range(2):
                        sy.wait_ge(s_sts[i].h, s_sts[i].n)
                    for mc in range(32):
                        need = wf1_base + mc - 2
                        if need > 0:
                            sy.wait_ge(s_wf1.h, need)
                        slot = mc % 3
                        ins = sy.dma_start(
                            out=w1b[:, slot * 1024:(slot + 1) * 1024],
                            in_=w1_scr[:, mc, :, :])
                        s_w1s[slot].inc(ins, 16)
                        w1_ready_at[mc] = (slot, s_w1s[slot].n)
                    for p_ in range(2):
                        for kc in range(32):
                            need = wf2_base + 32 * p_ + kc - 2
                            if need > 0:
                                sy.wait_ge(s_wf2.h, need)
                            slot = (32 * p_ + kc) % 3
                            ins = sy.dma_start(
                                out=w2b[:, slot * 1024:(slot + 1) * 1024],
                                in_=w2_scr[kc, :, :])
                            s_w2s[slot].inc(ins, 16)
                            w2_ready_at[(p_, kc)] = (slot, s_w2s[slot].n)
                if stages != "xn":
                    SY(w_prefetch)

                if seg > 0:
                    def zmb_cast(v):
                        v.drain()
                        s_ev.inc(v.tensor_copy(out=zmb[:, :], in_=zmf[:, :]))
                    V(zmb_cast)

                def ln_v(v):
                    v.wait_ge(s_ldx.h, ld_x)
                    ins = None
                    for ic in range(4):
                        ins = v.reduce_sum(out=stats[:, ic:ic + 1],
                                           in_=acc[:, ic * 1024:(ic + 1) * 1024],
                                           axis=mybir.AxisListType.X)
                    s_ev.inc(ins)
                V(ln_v)
                ev_ln1 = s_ev.n

                def ln_s(s, b=b, seg=seg, sts_tot=[x.n for x in s_sts]):
                    if b == 0 and seg == 0:
                        s.wait_ge(s_sts[0].h, sts_tot[0])
                        s.wait_ge(s_sts[1].h, sts_tot[1])
                    s.wait_ge(s_ldx.h, ld_x)
                    ins = None
                    for ic in range(4):
                        ins = s.activation(out=hsb[:, 0:1024], in_=acc[:, ic * 1024:(ic + 1) * 1024],
                                           func=AF.Square, accum_out=stats[:, 4 + ic:5 + ic])
                    s_sc.inc(ins)
                SC(ln_s)
                sc_ln = s_sc.n

                def ln_v2(v):
                    v.wait_ge(s_sc.h, sc_ln)
                    v.wait_ge(s_ev.h, ev_ln1)
                    v.tensor_scalar(out=stats[:, 8:12], in0=stats[:, 0:4], scalar1=1.0 / D,
                                    scalar2=None, op0=ALU.mult)
                    v.tensor_scalar(out=stats[:, 4:8], in0=stats[:, 4:8], scalar1=1.0 / D,
                                    scalar2=None, op0=ALU.mult)
                    v.drain()
                    v.tensor_tensor(out=stats[:, 12:16], in0=stats[:, 8:12],
                                    in1=stats[:, 8:12], op=ALU.mult)
                    v.drain()
                    s_ev.inc(v.tensor_tensor(out=stats[:, 12:16], in0=stats[:, 4:8],
                                             in1=stats[:, 12:16], op=ALU.subtract))
                V(ln_v2)
                ev_var = s_ev.n

                def ln_s2(s):
                    s.wait_ge(s_ev.h, ev_var)
                    s.activation(out=stats[:, 12:16], in_=stats[:, 12:16],
                                 func=AF.Ln, bias=eps_sb[:, :])
                    s.drain()
                    s_sc.inc(s.activation(out=stats[:, 12:16], in_=stats[:, 12:16],
                                          func=AF.Exp, scale=-0.5))
                SC(ln_s2)
                sc_rstd = s_sc.n

                def ln_v3(v):
                    v.wait_ge(s_sc.h, sc_rstd)
                    for ic in range(4):
                        s_ev.inc(v.tensor_scalar(
                            out=xn_tm[:, ic * 1024:(ic + 1) * 1024],
                            in0=acc[:, ic * 1024:(ic + 1) * 1024],
                            scalar1=stats[:, 8 + ic:9 + ic], scalar2=stats[:, 12 + ic:13 + ic],
                            op0=ALU.subtract, op1=ALU.mult))
                    v.drain()
                    ins = None
                    for ic in range(4):
                        ins = v.tensor_tensor(out=acc[:, ic * 1024:(ic + 1) * 1024],
                                              in0=acc[:, ic * 1024:(ic + 1) * 1024],
                                              in1=b2bc[:, :], op=ALU.add)
                    s_ev.inc(ins)
                V(ln_v3)
                ev_xn = s_ev.n

                if is_dbg:
                    dump("xn_tm", xn_tm[:, :], [(s_ev, ev_xn)])

                mm_xnT = s_mm.n
                ev_xnT = s_ev.n

                sc_now = s_sc.n

                def xnT(t, sc_now=sc_now):
                    t.wait_ge(s_ev.h, ev_xn)
                    t.wait_ge(s_sc.h, sc_now)
                    t.wait_ge(s_ld.h, P0_LD)
                    for jc in range(8):
                        half = jc % 2
                        if jc >= 2:
                            t.wait_ge(s_ev.h, ev_xnT + (jc - 1))
                        ins = None
                        for ic in range(4):
                            ins = t.transpose(
                                pc_bf[:, half * 1024 + ic * 128: half * 1024 + (ic + 1) * 128],
                                xn_tm[:, ic * 1024 + 128 * jc: ic * 1024 + 128 * (jc + 1)],
                                identb[:, :])
                        s_mm.inc(ins)
                T(xnT)

                def xnT_ev(v):
                    for jc in range(8):
                        half = jc % 2
                        v.wait_ge(s_mm.h, mm_xnT + jc + 1)
                        s_ev.inc(v.tensor_copy(out=xn_fm[:, jc * 512:(jc + 1) * 512],
                                               in_=pc_bf[:, half * 1024:half * 1024 + 512]))
                V(xnT_ev)
                ev_xnfm = s_ev.n

                if is_dbg:
                    dump("xn_fm", xn_fm[:, :], [(s_ev, ev_xnfm)])
                if stages == "xn":
                    continue

                mm_proj = s_mm.n
                ev_proj_b = s_ev.n
                sc_proj_b = s_sc.n
                projs = [(Wq_sb, qd2), (Wk_sb, kd2), (Wv_sb, vd2)]

                def proj_t(t):
                    t.wait_ge(s_ev.h, ev_xnfm)
                    cnt = 0
                    for Wsb, dst in projs:
                        for jc in range(8):
                            p = pa if cnt % 2 == 0 else pb
                            if cnt >= 2:
                                t.wait_ge(s_ev.h, ev_proj_b + 2 * (cnt - 1))
                                t.wait_ge(s_sc.h, sc_proj_b + 2 * (cnt - 1))
                            ins = None
                            for kc in range(8):
                                ins = t.matmul(
                                    p[:, 0:512],
                                    Wsb[:, kc * 1024 + 128 * jc: kc * 1024 + 128 * (jc + 1)],
                                    xn_fm[:, kc * 512:(kc + 1) * 512],
                                    start=(kc == 0), stop=(kc == 7))
                            s_mm.inc(ins)
                            cnt += 1
                T(proj_t)

                def proj_ev(who):
                    def f(en):
                        cnt = 0
                        for Wsb, dst in projs:
                            for jc in range(8):
                                p = pa if cnt % 2 == 0 else pb
                                en.wait_ge(s_mm.h, mm_proj + cnt + 1)
                                cpar = who
                                for hp in range(2):
                                    src = p[64 * cpar:64 * (cpar + 1), 0:512].rearrange(
                                        "p (g two n tw) -> p g two n tw",
                                        g=8, two=2, n=4)[:, :, hp, :, :]
                                    dstv = dst[64 * hp:64 * (hp + 1), :].rearrange(
                                        "p (g n c tw) -> p g n c tw",
                                        g=8, n=4, c=16)[:, :, :, 2 * jc + cpar, :]
                                    if who == 0:
                                        ins = en.tensor_copy(out=dstv, in_=src)
                                        s_ev.inc(ins)
                                    else:
                                        ins = en.copy(out=dstv, in_=src)
                                        s_sc.inc(ins)
                                cnt += 1
                    return f
                V(proj_ev(0))
                SC(proj_ev(1))
                ev_proj = s_ev.n
                sc_proj = s_sc.n

                if is_dbg:
                    dump("qd2", qd2[:, :], [(s_ev, ev_proj), (s_sc, sc_proj)])
                    dump("kd2", kd2[:, :], [(s_ev, ev_proj), (s_sc, sc_proj)])
                    dump("vd2", vd2[:, :], [(s_ev, ev_proj), (s_sc, sc_proj)])

                mm_vpos = s_mm.n
                ev_vpos_b = s_ev.n
                sc_vpos_b = s_sc.n

                def vpos_t(t):
                    t.wait_ge(s_ev.h, ev_proj)
                    t.wait_ge(s_sc.h, sc_proj)
                    for g_ in range(8):
                        half = g_ % 2
                        if g_ >= 2:
                            t.wait_ge(s_ev.h, ev_vpos_b + (g_ - 1))
                            t.wait_ge(s_sc.h, sc_vpos_b + 4 * (g_ - 1))
                        ins = None
                        for n in range(4):
                            ins = t.transpose(
                                pc_bf[:, half * 1024 + n * 128: half * 1024 + (n + 1) * 128],
                                vd2[:, g_ * 512 + 128 * n: g_ * 512 + 128 * (n + 1)],
                                identb[:, :])
                        s_mm.inc(ins)
                T(vpos_t)

                def vpos_ev(v):
                    for g_ in range(8):
                        half = g_ % 2
                        v.wait_ge(s_mm.h, mm_vpos + g_ + 1)
                        src = pc_bf[:, half * 1024: half * 1024 + 512].rearrange(
                            "p (n hp v) -> p n hp v", n=4, hp=2)
                        dst = vposm[:, g_ * 520:(g_ + 1) * 520].rearrange(
                            "p (n hp c) -> p n hp c", n=4, hp=2)[:, :, :, 1:65]
                        s_ev.inc(v.tensor_copy(out=dst, in_=src))
                        for n in range(4):
                            src2 = pc_bf[:, half * 1024 + n * 128: half * 1024 + n * 128 + 128].rearrange(
                                "p (hp v) -> p hp v", hp=2)
                            dst2 = vs[:, g_ * 520 + n * 130: g_ * 520 + (n + 1) * 130].rearrange(
                                "p (hp c) -> p hp c", hp=2)[:, :, 1:65]
                            omg = omg130[:, :].rearrange(
                                "p (g2 hp c) -> p g2 hp c", g2=8, hp=2)[:, g_, :, 1:65]
                            s_sc.inc(v.tensor_tensor(out=dst2, in0=src2, in1=omg, op=ALU.mult))
                V(vpos_ev)
                ev_vpos = s_ev.n
                sc_vs = s_sc.n

                mm_kpos = s_mm.n
                ev_kpos_b = s_ev.n
                sc_kpos_b = s_sc.n

                def kpos_t(t):
                    t.wait_ge(s_ev.h, ev_vpos)
                    t.wait_ge(s_sc.h, sc_vs)
                    for g_ in range(8):
                        half = g_ % 2
                        if g_ >= 2:
                            t.wait_ge(s_ev.h, ev_kpos_b + 2 * (g_ - 1))
                        ins = None
                        for n in range(4):
                            ins = t.transpose(
                                pd_bf[:, half * 1024 + n * 128: half * 1024 + (n + 1) * 128],
                                kd2[:, g_ * 512 + 128 * n: g_ * 512 + 128 * (n + 1)],
                                identb[:, :])
                        s_mm.inc(ins)
                T(kpos_t)

                def kpos_v(v):
                    for g_ in range(8):
                        half = g_ % 2
                        v.wait_ge(s_mm.h, mm_kpos + g_ + 1)
                        el = elu_t[:, half * 512:(half + 1) * 512]
                        s_ev.inc(v.tensor_scalar(
                            out=el, in0=pd_bf[:, half * 1024:half * 1024 + 512],
                            scalar1=0.0, scalar2=None, op0=ALU.min))
                        v.wait_ge(s_sc.h, sc_kpos_b + g_ + 1)
                        s_ev.inc(v.scalar_tensor_tensor(
                            out=skp[:, g_ * 512:(g_ + 1) * 512],
                            in0=pd_bf[:, half * 1024:half * 1024 + 512],
                            scalar=0.0, in1=el, op0=ALU.max, op1=ALU.add))
                V(kpos_v)

                def kpos_s(s):
                    for g_ in range(8):
                        half = g_ % 2
                        el = elu_t[:, half * 512:(half + 1) * 512]
                        s.wait_ge(s_ev.h, ev_kpos_b + 2 * g_ + 1)
                        s_sc.inc(s.activation(out=el, in_=el, func=AF.Exp))
                SC(kpos_s)
                ev_kpos = s_ev.n
                sc_kpos = s_sc.n

                if is_dbg:
                    dump("vposm", vposm[:, :], [(s_ev, ev_kpos), (s_sc, sc_kpos)])
                    dump("vs", vs[:, :], [(s_ev, ev_kpos), (s_sc, sc_kpos)])
                    dump("skp", skp[:, :], [(s_ev, ev_kpos), (s_sc, sc_kpos)])

                last_ev_attO = None
                last_sc_attO = None
                attO_done = []
                for g_ in range(8):
                    X, Y = (pc, pd) if g_ % 2 == 0 else (pa, pb)
                    X_bf = pc_bf if g_ % 2 == 0 else pa_bf
                    mm_sc_b = s_mm.n
                    sc_exp_b = s_sc.n

                    def pair_scores(t, g_=g_, first=(g_ == 0), sc_exp_b=sc_exp_b,
                                    X=X, Y=Y, attO_done=attO_done):
                        if first:
                            t.wait_ge(s_ev.h, ev_kpos)
                            t.wait_ge(s_sc.h, sc_kpos)
                        if g_ >= 2:
                            evn, scn = attO_done[g_ - 2]
                            t.wait_ge(s_ev.h, evn)
                            t.wait_ge(s_sc.h, scn)
                        for n in range(4):
                            psc = X if n % 2 == 0 else Y
                            if n >= 2:
                                t.wait_ge(s_sc.h, sc_exp_b + (n - 1))
                            ins = None
                            for hp in range(2):
                                ins = t.matmul(
                                    psc[:, hp * 512 + 128 * n: hp * 512 + 512],
                                    kd2[64 * hp:64 * (hp + 1),
                                        g_ * 512 + 128 * n: g_ * 512 + 128 * (n + 1)],
                                    qd2[64 * hp:64 * (hp + 1),
                                        g_ * 512 + 128 * n: (g_ + 1) * 512],
                                    start=True, stop=True, tile_position=(64 * hp, 0))
                            s_mm.inc(ins)
                    T(pair_scores)

                    def pair_exp(s, g_=g_, mm_sc_b=mm_sc_b, X=X, Y=Y):
                        for n in range(4):
                            psc = X if n % 2 == 0 else Y
                            s.wait_ge(s_mm.h, mm_sc_b + n + 1)
                            src = psc[:, :].rearrange("p (hp c) -> p hp c", hp=2)[:, :, 128 * n:512]
                            dst = pt[:, :].rearrange("p (hp n c) -> p hp n c",
                                                     hp=2, n=4)[:, :, n, 128 * n:512]
                            s_sc.inc(s.activation(out=dst, in_=src, func=AF.Exp,
                                                  bias=negc_sb[:, :], scale=0.125))
                    SC(pair_exp)
                    sc_exp = s_sc.n

                    def pair_mask(v, sc_exp=sc_exp):
                        v.wait_ge(s_sc.h, sc_exp)
                        ins = None
                        for hp in range(2):
                            for n in range(4):
                                base = hp * 2048 + n * 640
                                ins = v.tensor_tensor(
                                    out=pt[:, base:base + 128], in0=pt[:, base:base + 128],
                                    in1=mask8[:, 0:128], op=ALU.mult)
                        s_ev.inc(ins)
                    V(pair_mask)
                    ev_mask = s_ev.n

                    def pair_sq_min(v, g_=g_):
                        el = elu_t[:, (g_ % 2) * 512:(g_ % 2 + 1) * 512]
                        s_ev.inc(v.tensor_scalar(out=el,
                                                 in0=qd2[:, g_ * 512:(g_ + 1) * 512],
                                                 scalar1=0.0, scalar2=None, op0=ALU.min))
                    V(pair_sq_min)
                    ev_sqmin = s_ev.n

                    def pair_sq_exp(s, ev_sqmin=ev_sqmin, g_=g_):
                        el = elu_t[:, (g_ % 2) * 512:(g_ % 2 + 1) * 512]
                        s.wait_ge(s_ev.h, ev_sqmin)
                        s_sc.inc(s.activation(out=el, in_=el, func=AF.Exp))
                    SC(pair_sq_exp)
                    sc_sqexp = s_sc.n

                    def pair_sq_fin(v, g_=g_, sc_sqexp=sc_sqexp):
                        el = elu_t[:, (g_ % 2) * 512:(g_ % 2 + 1) * 512]
                        v.wait_ge(s_sc.h, sc_sqexp)
                        s_ev.inc(v.scalar_tensor_tensor(
                            out=sq_p[:, :], in0=qd2[:, g_ * 512:(g_ + 1) * 512],
                            scalar=0.0, in1=el, op0=ALU.max, op1=ALU.add))
                    V(pair_sq_fin)
                    ev_sq = s_ev.n

                    mm_mem_b = s_mm.n

                    def pair_mem(t, g_=g_, ev_sq=ev_sq, sc_exp=sc_exp, Y=Y):
                        t.wait_ge(s_ev.h, ev_sq)
                        t.wait_ge(s_sc.h, sc_exp)
                        for hp in range(2):
                            ins = None
                            for m in range(4):
                                ins = t.matmul(
                                    Y[:, hp * 512 + m * 65: hp * 512 + (m + 1) * 65],
                                    sq_p[64 * hp:64 * (hp + 1), 128 * m:128 * (m + 1)],
                                    zmb[64 * hp:64 * (hp + 1), g_ * 65:(g_ + 1) * 65],
                                    start=True, stop=True, tile_position=(64 * hp, 0))
                            s_mm.inc(ins)
                    T(pair_mem)
                    mm_mem = s_mm.n

                    mm_dot_b = s_mm.n

                    def pair_dot(t, g_=g_, ev_mask=ev_mask, X=X):
                        t.wait_ge(s_ev.h, ev_mask)
                        for hp in range(2):
                            for m in range(4):
                                ins = None
                                for n in range(m + 1):
                                    ins = t.matmul(
                                        X[:, hp * 512 + m * 65: hp * 512 + (m + 1) * 65],
                                        pt[:, hp * 2048 + n * 512 + 128 * m:
                                           hp * 2048 + n * 512 + 128 * (m + 1)],
                                        vs[:, g_ * 520 + n * 130 + hp * 65:
                                           g_ * 520 + n * 130 + (hp + 1) * 65],
                                        start=(n == 0), stop=(n == m))
                                s_mm.inc(ins)
                    T(pair_dot)
                    mm_dot = s_mm.n

                    def pair_recip(v, mm_mem=mm_mem, mm_dot=mm_dot, X=X, Y=Y):
                        v.wait_ge(s_mm.h, mm_mem)
                        den = Y[:, :].rearrange("p (hp q) -> p hp q", hp=2)[:, :, 0:260]
                        den = den.rearrange("p hp (x c) -> p hp x c", c=65)[:, :, :, 0]
                        v.tensor_scalar(out=rd4[:, 0:8].rearrange("p (hp x) -> p hp x", hp=2),
                                        in0=den, scalar1=tiny_sb[:, :],
                                        scalar2=None, op0=ALU.add)
                        v.drain()
                        v.reciprocal(out=rd4[:, 0:8], in_=rd4[:, 0:8])
                        v.wait_ge(s_mm.h, mm_dot)
                        sden = X[:, :].rearrange("p (hp q) -> p hp q", hp=2)[:, :, 0:260]
                        sden = sden.rearrange("p hp (x c) -> p hp x c", c=65)[:, :, :, 0]
                        s_ev.inc(v.reciprocal(out=rd4[:, 8:16].rearrange("p (hp x) -> p hp x", hp=2),
                                              in_=sden))
                    V(pair_recip)
                    ev_recip = s_ev.n

                    def pair_memn(s, ev_recip=ev_recip, Y=Y):
                        s.wait_ge(s_ev.h, ev_recip)
                        ins = None
                        for hp in range(2):
                            for m in range(4):
                                x = hp * 4 + m
                                ins = s.activation(
                                    out=memn[:, x * 64:(x + 1) * 64],
                                    in_=Y[:, hp * 512 + m * 65 + 1: hp * 512 + (m + 1) * 65],
                                    func=AF.Copy, scale=rd4[:, x:x + 1])
                        s_sc.inc(ins)
                    SC(pair_memn)
                    sc_memn = s_sc.n

                    def pair_comb(v, sc_memn=sc_memn, X=X):
                        v.wait_ge(s_sc.h, sc_memn)
                        ins = None
                        for hp in range(2):
                            for m in range(4):
                                x = hp * 4 + m
                                ins = v.scalar_tensor_tensor(
                                    out=att_p[:, x * 64:(x + 1) * 64],
                                    in0=X[:, hp * 512 + m * 65 + 1: hp * 512 + (m + 1) * 65],
                                    scalar=rd4[:, 8 + x:9 + x],
                                    in1=memn[:, x * 64:(x + 1) * 64],
                                    op0=ALU.mult, op1=ALU.add)
                        s_ev.inc(ins)
                    V(pair_comb)
                    ev_att = s_ev.n

                    if is_dbg and g_ == 0:
                        dump("pt", pt[:, :], [(s_ev, ev_mask)])
                        dump("sq", sq_p[:, :], [(s_ev, ev_sq)])
                        dump("att", att_p[:, :], [(s_ev, ev_att)])

                    mm_mupd_b = s_mm.n

                    def pair_mupd(t, g_=g_, sc_memn=sc_memn, Y=Y):
                        t.wait_ge(s_sc.h, sc_memn)
                        for hp in range(2):
                            ins = None
                            for n in range(4):
                                ins = t.matmul(
                                    Y[64 * hp:64 * (hp + 1), 772:837],
                                    skp[:, g_ * 512 + n * 128 + hp * 64:
                                        g_ * 512 + n * 128 + (hp + 1) * 64],
                                    vposm[:, g_ * 520 + n * 130 + hp * 65:
                                          g_ * 520 + n * 130 + (hp + 1) * 65],
                                    start=(n == 0), stop=(n == 3))
                            s_mm.inc(ins)
                    T(pair_mupd)
                    mm_mupd = s_mm.n

                    def pair_state(v, g_=g_, mm_mupd=mm_mupd, Y=Y):
                        v.wait_ge(s_mm.h, mm_mupd)
                        v.tensor_tensor(out=sscr[:, 0:65], in0=Y[:, 772:837],
                                        in1=gmp[:, g_ * 65:(g_ + 1) * 65], op=ALU.mult)
                        v.drain()
                        s_ev.inc(v.tensor_tensor(out=zmf[:, g_ * 65:(g_ + 1) * 65],
                                                 in0=zmf[:, g_ * 65:(g_ + 1) * 65],
                                                 in1=sscr[:, 0:65], op=ALU.add))
                    V(pair_state)

                    mm_attT_b = s_mm.n

                    def pair_attT(t, ev_att=ev_att, X_bf=X_bf):
                        t.wait_ge(s_ev.h, ev_att)
                        ins = None
                        for hp in range(2):
                            for m in range(4):
                                x = hp * 4 + m
                                ins = t.transpose(
                                    X_bf[64 * hp:64 * (hp + 1), 1280 + m * 128: 1280 + (m + 1) * 128],
                                    att_p[:, x * 64:(x + 1) * 64],
                                    identb[:, :])
                        s_mm.inc(ins)
                    T(pair_attT)
                    mm_attT = s_mm.n

                    def pair_attO(is_v, g_=g_, mm_attT=mm_attT, X_bf=X_bf):
                        def f(en):
                            en.wait_ge(s_mm.h, mm_attT)
                            for hp in range(2):
                                h_ = 2 * g_ + hp
                                for cpar in range(2):
                                    if (hp + cpar) % 2 != (0 if is_v else 1):
                                        continue
                                    src = X_bf[64 * hp:64 * (hp + 1), 1280:1792].rearrange(
                                        "p (m a t2) -> p m a t2",
                                        m=4, a=8)[:, :, :, cpar * 8:(cpar + 1) * 8]
                                    src = src.transpose([0, 2, 1, 3])
                                    dst = atto[64 * cpar:64 * (cpar + 1), :].rearrange(
                                        "p (a i) -> p a i", a=8)[:, :, 32 * h_:32 * h_ + 32]
                                    dst = dst.rearrange("p a (m tw) -> p a m tw", m=4)
                                    if is_v:
                                        ins = en.tensor_copy(out=dst, in_=src)
                                        s_ev.inc(ins)
                                    else:
                                        ins = en.copy(out=dst, in_=src)
                                        s_sc.inc(ins)
                        return f
                    V(pair_attO(True))
                    SC(pair_attO(False))
                    last_ev_attO = s_ev.n
                    last_sc_attO = s_sc.n
                    attO_done.append((last_ev_attO, last_sc_attO))

                if is_dbg:
                    dump("atto", atto[:, :], [(s_ev, last_ev_attO), (s_sc, last_sc_attO)])
                    dump("zmf", zmf[:, :], [(s_ev, s_ev.n)])

                mm_wo_b = s_mm.n
                ev_wo_b = s_ev.n

                def wo_t(t, a=None, b_=None):
                    t.wait_ge(s_ev.h, last_ev_attO)
                    t.wait_ge(s_sc.h, last_sc_attO)
                    for ic in range(4):
                        p = pa if ic % 2 == 0 else pb
                        if ic >= 2:
                            t.wait_ge(s_ev.h, ev_wo_b + (ic - 1))
                        ins = None
                        for half in range(2):
                            for a in range(8):
                                ins = t.matmul(
                                    p[:, half * 512:(half + 1) * 512],
                                    atto[:, a * 512 + 128 * ic: a * 512 + 128 * (ic + 1)],
                                    Wo_sb[:, a * 1024 + half * 512: a * 1024 + (half + 1) * 512],
                                    start=(a == 0), stop=(a == 7))
                        s_mm.inc(ins)
                T(wo_t)

                def wo_ev(v, mm_wo_b=mm_wo_b):
                    for ic in range(4):
                        p = pa if ic % 2 == 0 else pb
                        v.wait_ge(s_mm.h, mm_wo_b + ic + 1)
                        if skip_wo:
                            s_ev.inc(v.tensor_copy(out=rd4[:, 1:2], in_=rd4[:, 1:2]))
                        else:
                            s_ev.inc(v.tensor_tensor(out=acc[:, ic * 1024:(ic + 1) * 1024],
                                                     in0=acc[:, ic * 1024:(ic + 1) * 1024],
                                                     in1=p[:, 0:1024], op=ALU.add))
                V(wo_ev)
                ev_wo = s_ev.n

                if is_dbg:
                    dump("accw", acc[:, :], [(s_ev, ev_wo)])

                mm_f1_b = s_mm.n
                if skip_ffn:
                    ev_f2 = s_ev.n

                def f1_t(t, wf1_base=wf1_base):
                    if skip_ffn:
                        t.wait_ge(s_ev.h, ev_wo)
                        for mc in range(32):
                            sl, nn = w1_ready_at[mc]
                            t.wait_ge(s_w1s[sl].h, nn)
                        for p_ in range(2):
                            for kc in range(32):
                                sl, nn = w2_ready_at[(p_, kc)]
                                t.wait_ge(s_w2s[sl].h, nn)
                        return
                    t.wait_ge(s_ev.h, ev_wo)
                    for mc in range(32):
                        p = pa if mc % 2 == 0 else pb
                        sl, nn = w1_ready_at[mc]
                        t.wait_ge(s_w1s[sl].h, nn)
                        if mc >= 2:
                            t.wait_ge(s_wf1.h, wf1_base + (mc - 1))
                        ins = None
                        for kc in range(8):
                            ins = t.matmul(
                                p[:, 0:512],
                                w1b[:, (mc % 3) * 1024 + kc * 128: (mc % 3) * 1024 + (kc + 1) * 128],
                                xn_fm[:, kc * 512:(kc + 1) * 512],
                                start=(kc == 0), stop=(kc == 7))
                        s_mm.inc(ins)
                T(f1_t)

                def f1_ev(s, mm_f1_b=mm_f1_b):
                    if skip_ffn:
                        for mc in range(32):
                            s_wf1.inc(s.engine_nop() if False else s.copy(out=elu_t[:, 0:1], in_=elu_t[:, 0:1]))
                        return
                    for mc in range(32):
                        p = pa if mc % 2 == 0 else pb
                        s.wait_ge(s_mm.h, mm_f1_b + mc + 1)
                        s_wf1.inc(s.activation(out=hsb[:, mc * 512:(mc + 1) * 512],
                                               in_=p[:, 0:512], func=AF.Gelu,
                                               bias=b1_sb[:, mc:mc + 1]))
                SC(f1_ev)
                wf1_done = s_wf1.n

                if is_dbg:
                    dump("hsb", hsb[:, 0:2048], [(s_wf1, wf1_done)])

                mm_f2_b = s_mm.n
                ev_f2_b = s_ev.n

                def f2_t(t, wf1_done=wf1_done):
                    if skip_ffn:
                        return
                    t.wait_ge(s_wf1.h, wf1_done)
                    for p_ in range(2):
                        if p_ == 1:
                            t.wait_ge(s_ev.h, ev_f2_b + 2)
                        for kc in range(32):
                            sl, nn = w2_ready_at[(p_, kc)]
                            t.wait_ge(s_w2s[sl].h, nn)
                            ins = None
                            for ic2 in range(2):
                                ic = 2 * p_ + ic2
                                p = pa if ic2 == 0 else pb
                                for half in range(2):
                                    ins = t.matmul(
                                        p[:, half * 512:(half + 1) * 512],
                                        hsb[:, kc * 512 + 128 * ic: kc * 512 + 128 * (ic + 1)],
                                        w2b[:, ((32 * p_ + kc) % 3) * 1024 + half * 512:
                                            ((32 * p_ + kc) % 3) * 1024 + (half + 1) * 512],
                                        start=(kc == 0), stop=(kc == 31))
                            s_wf2.inc(ins)
                T(f2_t)

                def f2_ev(v, wf2_base=wf2_base):
                    if skip_ffn:
                        ins = None
                        for i_ in range(64):
                            ins = v.tensor_copy(out=rd4[:, 0:1], in_=rd4[:, 0:1])
                            s_wf2.inc(ins)
                        return
                    for p_ in range(2):
                        v.wait_ge(s_wf2.h, wf2_base + 32 * (p_ + 1))
                        for ic2 in range(2):
                            ic = 2 * p_ + ic2
                            p = pa if ic2 == 0 else pb
                            s_ev.inc(v.tensor_tensor(
                                out=acc[:, ic * 1024:(ic + 1) * 1024],
                                in0=acc[:, ic * 1024:(ic + 1) * 1024],
                                in1=p[:, 0:1024], op=ALU.add))
                V(f2_ev)
                ev_f2 = s_ev.n

                if is_dbg:
                    dump("acc", acc[:, :], [(s_ev, ev_f2)])

                def store(sy, b=b, seg=seg, ev_f2=ev_f2):
                    sy.wait_ge(s_ev.h, ev_f2)
                    for ic in range(4):
                        s_st.inc(sy.dma_start(
                            out=ys[b, seg * 512 + 128 * ic: seg * 512 + 128 * (ic + 1), :],
                            in_=acc[:, ic * 1024:(ic + 1) * 1024]), 16)
                SY(store)

        def fin(sy):
            sy.wait_ge(s_st.h, s_st.n)
        SY(fin)

    return nc


def host_arrange_inputs(inp, nb=4, core=0):
    x = np.ascontiguousarray(inp["L0_attn_x"][core * nb:(core + 1) * nb])
    betas = np.ascontiguousarray(inp["L0_attn_betas"].reshape(H, DV))
    m0 = inp["L0_attn_memory0"].reshape(H, DK, DV)
    z0 = inp["L0_attn_z0"].reshape(H, DK)
    m0_arr = np.zeros((128, 512), dtype=np.float32)
    z0_arr = np.zeros((128, 8), dtype=np.float32)
    for g in range(8):
        for hp in range(2):
            h = 2 * g + hp
            m0_arr[64 * hp:64 * (hp + 1), g * 64:(g + 1) * 64] = m0[h]
            z0_arr[64 * hp:64 * (hp + 1), g] = z0[h]
    return {
        "xs": x, "Wq": inp["L0_attn_Wq"], "Wk": inp["L0_attn_Wk"],
        "Wv": inp["L0_attn_Wv"], "Wo": inp["L0_attn_Wo"],
        "w1": inp["L0_ffn_w1"], "b1": inp["L0_ffn_b1"],
        "w2": inp["L0_ffn_w2"], "b2": inp["L0_ffn_b2"],
        "betas": betas, "m0_arr": m0_arr, "z0_arr": z0_arr,
    }


_NCORES = 8


def kernel(**inputs):
    from concourse.bass_utils import run_bass_kernel_spmd
    in_maps = [host_arrange_inputs(inputs, nb=4, core=c) for c in range(_NCORES)]
    nc = build(nb=4, nsegs=4)
    res = run_bass_kernel_spmd(nc, in_maps, core_ids=list(range(_NCORES)))
    out = np.concatenate([res.results[c]["ys"] for c in range(_NCORES)], axis=0)
    return out.astype(np.float32)


# version 03
# speedup vs baseline: 1.1308x; 1.0908x over previous
import numpy as np
import contextlib
import concourse.bass as bass
import concourse.mybir as mybir

F32 = mybir.dt.float32
BF16 = mybir.dt.bfloat16
AF = mybir.ActivationFunctionType
ALU = mybir.AluOpType

H, DK, DV = 16, 64, 64
D = 1024
NHID = 4096
CEXP = 12.0
EPS = 1e-5


class Sem:
    def __init__(self, nc, st, name):
        self.h = st.enter_context(nc.semaphore(name))
        self.n = 0

    def inc(self, ins, k=1):
        ins.then_inc(self.h, k)
        self.n += k
        return ins


def host_constants():
    ident = np.eye(128, dtype=np.float32)
    m = np.zeros((128, 128), dtype=np.float32)
    for c2 in range(16):
        for t2 in range(8):
            for c1 in range(16):
                for t1 in range(8):
                    m[8 * c2 + t2, 8 * c1 + t1] = 1.0 if (16 * t2 + c2) <= (16 * t1 + c1) else 0.0
    return {"ident": ident, "diagmask": m}


def build(nb=4, nsegs=4, debug=(), dbg_tile=None, skip_ffn=False, skip_wo=False, stages=None, detect=False):
    nc = bass.Bass("TRN2", target_bir_lowering=False, detect_race_conditions=detect)
    S = 2048
    if dbg_tile is None:
        dbg_tile = (0, nsegs - 1)

    xs = nc.dram_tensor("xs", [nb, S, D], F32, kind="ExternalInput").ap()
    Wq = nc.dram_tensor("Wq", [D, D], F32, kind="ExternalInput").ap()
    Wk = nc.dram_tensor("Wk", [D, D], F32, kind="ExternalInput").ap()
    Wv = nc.dram_tensor("Wv", [D, D], F32, kind="ExternalInput").ap()
    Wo = nc.dram_tensor("Wo", [D, D], F32, kind="ExternalInput").ap()
    w1 = nc.dram_tensor("w1", [D, NHID], F32, kind="ExternalInput").ap()
    b1 = nc.dram_tensor("b1", [NHID], F32, kind="ExternalInput").ap()
    w2 = nc.dram_tensor("w2", [NHID, D], F32, kind="ExternalInput").ap()
    b2 = nc.dram_tensor("b2", [D], F32, kind="ExternalInput").ap()
    betas = nc.dram_tensor("betas", [H, DV], F32, kind="ExternalInput").ap()
    m0_arr = nc.dram_tensor("m0_arr", [128, 512], F32, kind="ExternalInput").ap()
    z0_arr = nc.dram_tensor("z0_arr", [128, 8], F32, kind="ExternalInput").ap()
    ys = nc.dram_tensor("ys", [nb, S, D], F32, kind="ExternalOutput").ap()

    consts = host_constants()
    ident_dram = nc.inline_tensor(consts["ident"], "identc")
    mask_dram = nc.inline_tensor(consts["diagmask"], "maskc")

    w1_scr = nc.dram_tensor("w1_scr", [128, 32, 8, 128], BF16, kind="Internal").ap()
    w2_scr = nc.dram_tensor("w2_scr", [32, 128, D], BF16, kind="Internal").ap()
    gate_scr = nc.dram_tensor("gate_scr", [H, 65], F32, kind="Internal").ap()
    omg_scr = nc.dram_tensor("omg_scr", [H, 65], F32, kind="Internal").ap()

    dbg_specs = {
        "xn_fm": ([128, 8 * 512], BF16), "qd2": ([128, 4096], BF16),
        "kd2": ([128, 4096], BF16), "vd2": ([128, 4096], BF16),
        "vposm": ([128, 4160], BF16), "vs": ([128, 4160], BF16),
        "skp": ([128, 4096], BF16), "pt": ([128, 4096], BF16),
        "att": ([128, 512], BF16), "atto": ([128, 4096], BF16),
        "acc": ([128, 4096], F32), "accw": ([128, 4096], F32), "zmf": ([128, 520], F32),
        "sq": ([128, 512], BF16), "hsb": ([128, 2048], BF16),
        "xn_tm": ([128, 4096], BF16),
    }
    dbg_drams = {k: nc.dram_tensor("dbg_" + k, *dbg_specs[k], kind="ExternalOutput").ap()
                 for k in debug}

    with contextlib.ExitStack() as st:
        e = st.enter_context

        def sb(name, shape, dt):
            return e(nc.sbuf_tensor(name, shape, dt))

        Wq_sb = sb("Wq_sb", [128, 8 * 1024], BF16)
        Wk_sb = sb("Wk_sb", [128, 8 * 1024], BF16)
        Wv_sb = sb("Wv_sb", [128, 8 * 1024], BF16)
        Wo_sb = sb("Wo_sb", [128, 8 * 1024], BF16)
        w1b = sb("w1b", [128, 3 * 1024], BF16)
        w2b = sb("w2b", [128, 3 * 1024], BF16)
        acc = sb("acc", [128, 4 * 1024], F32)
        xn_tm = sb("xn_tm", [128, 4 * 1024], BF16)
        xn_fm = sb("xn_fm", [128, 8 * 512], BF16)
        qd2 = sb("qd2", [128, 4096], BF16)
        kd2 = sb("kd2", [128, 4096], BF16)
        vd2 = sb("vd2", [128, 4096], BF16)
        vposm = sb("vposm", [128, 4160], BF16)
        vs = sb("vs", [128, 4160], BF16)
        skp = sb("skp", [128, 4096], BF16)
        pt = vd2
        att_p = sb("att_p", [128, 512], BF16)
        memn = sb("memn", [128, 512], BF16)
        sq_p = sb("sq_p", [128, 512], BF16)
        elu_t = sb("elu_t", [128, 1024], BF16)
        atto = xn_tm
        hsb = sb("hsb", [128, 32 * 512], BF16)
        sscr = sb("sscr", [128, 65], F32)
        zmf = sb("zmf", [128, 8 * 65], F32)
        zmb = sb("zmb", [128, 8 * 65], BF16)
        gmp = sb("gmp", [128, 8 * 65], F32)
        omg130 = sb("omg130", [128, 8 * 130], BF16)
        b2bc = sb("b2bc", [128, 1024], F32)
        b1_sb = sb("b1_sb", [128, 32], F32)
        mask8 = sb("mask8", [128, 128], BF16)
        identb = sb("identb", [128, 128], BF16)
        gstage = sb("gstage", [H, 65], F32)
        stats = sb("stats", [128, 16], F32)
        rd4 = sb("rd4", [128, 16], F32)
        eps_sb = sb("eps_sb", [128, 1], F32)
        negc_sb = sb("negc_sb", [128, 1], F32)
        tiny_sb = sb("tiny_sb", [128, 1], F32)
        onesb = sb("onesb", [1, 1], F32)

        class _WStage:
            def __getitem__(self, sl):
                return None
        def wslice(kc, width):
            buf = qd2 if kc % 2 == 0 else kd2
            return buf[:, 0:width]

        pa = e(nc.psum_tensor("pa", [128, 1024], F32))
        pb = e(nc.psum_tensor("pb", [128, 1024], F32))
        pc = e(nc.psum_tensor("pc", [128, 1024], F32))
        pd = e(nc.psum_tensor("pd", [128, 1024], F32))
        pc_bf = pc.ap().bitcast(BF16)
        pd_bf = pd.ap().bitcast(BF16)
        pa_bf = pa.ap().bitcast(BF16)

        s_ld = Sem(nc, st, "s_ld")
        s_ldx = Sem(nc, st, "s_ldx")
        s_state = Sem(nc, st, "s_state")
        s_w1s = [Sem(nc, st, f"s_w1s{i}") for i in range(3)]
        s_w2s = [Sem(nc, st, f"s_w2s{i}") for i in range(3)]
        s_stg = [Sem(nc, st, f"s_stg{i}") for i in range(2)]
        s_sts = [Sem(nc, st, f"s_sts{i}") for i in range(2)]
        s_wf1 = Sem(nc, st, "s_wf1")
        s_wf2 = Sem(nc, st, "s_wf2")
        s_mm = Sem(nc, st, "s_mm")
        s_ev = Sem(nc, st, "s_ev")
        s_sc = Sem(nc, st, "s_sc")
        s_st = Sem(nc, st, "s_st")
        s_stic = [Sem(nc, st, f"s_stic{i}") for i in range(4)]
        block = e(nc.Block())

        T, V, SC, SY, G = block.tensor, block.vector, block.scalar, block.sync, block.gpsimd

        def dump(name, ap_src, waits):
            if name not in dbg_drams:
                return
            dst = dbg_drams[name]

            def _f(sy, dst=dst, ap_src=ap_src, waits=tuple(waits)):
                for sem, n in waits:
                    sy.wait_ge(sem.h, n)
                s_st.inc(sy.dma_start(out=dst[:, :], in_=ap_src), 16)
            SY(_f)

        def p0_gpsimd(g):
            for W, Wsb in ((Wq, Wq_sb), (Wk, Wk_sb), (Wv, Wv_sb), (Wo, Wo_sb)):
                for kc in range(8):
                    s_ld.inc(g.dma_start(out=Wsb[:, kc * 1024:(kc + 1) * 1024],
                                         in_=W[128 * kc:128 * (kc + 1), :]), 16)
            s_ld.inc(g.dma_start(out=identb[:, :], in_=ident_dram.ap()[:, :]), 16)
            s_ld.inc(g.dma_start(out=mask8[:, 0:128], in_=mask_dram.ap()[:, :]), 16)
            s_ld.inc(g.dma_start(out=b2bc[:, :],
                                 in_=b2.unsqueeze(0).partition_broadcast(128)), 16)
            s_ld.inc(g.dma_start(out=gstage[:, 1:65], in_=betas[:, :]), 16)
            s_ld.inc(g.dma_start(out=acc[0:1, 0:4096], in_=b1.unsqueeze(0)), 16)
        G(p0_gpsimd)
        ld0 = s_ld.n

        def p0_vector(v):
            v.memset(eps_sb[:, :], EPS)
            v.memset(negc_sb[:, :], -CEXP)
            v.memset(tiny_sb[:, :], 1e-35)
            v.memset(onesb[:, :], 1.0)
            v.memset(gstage[:, 0:1], 1.0)
            v.memset(vposm[:, :].rearrange("p (b c) -> p b c", c=65)[:, :, 0], 1.0)
            v.memset(vs[:, :].rearrange("p (b c) -> p b c", c=65)[:, :, 0], 1.0)
            v.wait_ge(s_ld.h, ld0)
            s_ev.inc(v.memset(rd4[:, 0:1], 0.0))
        V(p0_vector)
        ev0 = s_ev.n

        def p0_scalar(s):
            s.wait_ge(s_ld.h, ld0)
            s.wait_ge(s_ev.h, ev0)
            s_sc.inc(s.activation(out=gstage[:, 1:65], in_=gstage[:, 1:65], func=AF.Sigmoid))
        SC(p0_scalar)
        sc_gate = s_sc.n

        def p0_vec2(v):
            v.wait_ge(s_sc.h, sc_gate)
            v.tensor_scalar(out=sscr[0:H, 0:65], in0=gstage[:, 0:65],
                            scalar1=1.0, scalar2=-1.0, op0=ALU.subtract, op1=ALU.mult)
            v.drain()
            s_ev.inc(v.memset(sscr[0:H, 0:1], 1.0))
        V(p0_vec2)
        ev_omg = s_ev.n

        def p0_b1_t(t):
            t.wait_ge(s_ld.h, ld0)
            t.wait_ge(s_ev.h, ev0)
            ins = None
            for mc in range(32):
                ins = t.matmul(pa[0:128, mc:mc + 1], acc[0:1, 128 * mc:128 * (mc + 1)],
                               onesb[0:1, 0:1], start=True, stop=True)
            s_mm.inc(ins)
        T(p0_b1_t)
        mm_b1 = s_mm.n

        def p0_b1_v(v):
            v.wait_ge(s_mm.h, mm_b1)
            s_ev.inc(v.tensor_copy(out=b1_sb[:, :], in_=pa[:, 0:32]))
        V(p0_b1_v)
        ev_b1 = s_ev.n

        def p0_gates_sy(sy):
            sy.wait_ge(s_sc.h, sc_gate)
            s_st.inc(sy.dma_start(out=gate_scr[:, :], in_=gstage[:, :]), 16)
            sy.wait_ge(s_ev.h, ev_omg)
            s_st.inc(sy.dma_start(out=omg_scr[:, :], in_=sscr[0:H, 0:65]), 16)
            sy.wait_ge(s_st.h, s_st.n)
            for g_ in range(8):
                for hp in range(2):
                    s_state.inc(sy.dma_start(
                        out=gmp[64 * hp:64 * (hp + 1), g_ * 65:(g_ + 1) * 65],
                        in_=gate_scr[2 * g_ + hp:2 * g_ + hp + 1, :].partition_broadcast(64)), 16)
        SY(p0_gates_sy)

        def p0_omg_g(g):
            g.wait_ge(s_st.h, s_st.n)
            for g_ in range(8):
                for hp in range(2):
                    s_ld.inc(g.dma_start(
                        out=omg130[:, (g_ * 2 + hp) * 65:(g_ * 2 + hp + 1) * 65],
                        in_=omg_scr[2 * g_ + hp:2 * g_ + hp + 1, :].partition_broadcast(128)), 16)
        G(p0_omg_g)

        def p0_w1_g(g):
            for kc in range(8):
                if kc >= 2:
                    g.wait_ge(s_sts[kc % 2].h, 16 * (kc // 2))
                s_stg[kc % 2].inc(g.dma_start(
                    out=wslice(kc, 4096),
                    in_=w1[128 * kc:128 * (kc + 1), :]), 16)
        G(p0_w1_g)

        def p0_w1_sy(sy):
            for kc in range(8):
                sy.wait_ge(s_stg[kc % 2].h, 16 * (kc // 2 + 1))
                src = wslice(kc, 4096).rearrange("p (mc c) -> p mc c", mc=32)
                dst = w1_scr[:, :, kc, :]
                s_sts[kc % 2].inc(sy.dma_start(out=dst, in_=src), 16)
        SY(p0_w1_sy)

        w1_sts_base = [s.n for s in s_sts]
        w1_stg_base = [s.n for s in s_stg]

        def p0_w2_g(g):
            for kc in range(32):
                if kc == 0:
                    g.wait_ge(s_sts[0].h, w1_sts_base[0])
                    g.wait_ge(s_sts[1].h, w1_sts_base[1])
                if kc >= 2:
                    g.wait_ge(s_sts[kc % 2].h, w1_sts_base[kc % 2] + 16 * (kc // 2))
                s_stg[kc % 2].inc(g.dma_start(
                    out=wslice(kc, 1024),
                    in_=w2[128 * kc:128 * (kc + 1), :]), 16)
        G(p0_w2_g)

        def p0_w2_sy(sy):
            for kc in range(32):
                sy.wait_ge(s_stg[kc % 2].h, w1_stg_base[kc % 2] + 16 * (kc // 2 + 1))
                s_sts[kc % 2].inc(sy.dma_start(
                    out=w2_scr[kc, :, :],
                    in_=wslice(kc, 1024)), 16)
        SY(p0_w2_sy)

        P0_LD = s_ld.n
        P0_ST = s_st.n
        P0_MM = s_mm.n

        for b in range(nb):
            if stages == "xn":
                pass
            def init_state_sy(sy, b=b, mmn=s_mm.n, evn=s_ev.n):
                sy.wait_ge(s_st.h, P0_ST)
                if b > 0:
                    sy.wait_ge(s_mm.h, mmn)
                    sy.wait_ge(s_ev.h, evn)
                s_state.inc(sy.dma_start(
                    out=zmf[:, :].rearrange("p (g c) -> p g c", g=8)[:, :, 0],
                    in_=z0_arr[:, :], allow_slow_non_contiguous=True), 16)
                s_state.inc(sy.dma_start(
                    out=zmf[:, :].rearrange("p (g c) -> p g c", g=8)[:, :, 1:65],
                    in_=m0_arr[:, :].rearrange("p (g v) -> p g v", g=8)), 16)
            SY(init_state_sy)
            ld_state = s_state.n

            def init_state_v(v, b=b):
                v.wait_ge(s_state.h, ld_state)
                if b == 0:
                    v.wait_ge(s_ld.h, P0_LD)
                s_ev.inc(v.tensor_copy(out=zmb[:, :], in_=zmf[:, :]))
            V(init_state_v)
            ev_state0 = s_ev.n

            for seg in range(nsegs):
                is_dbg = (b, seg) == dbg_tile and dbg_drams

                def x_load(sy, b=b, seg=seg, stic=[x.n for x in s_stic], mmn=P0_MM):
                    if b == 0 and seg == 0:
                        sy.wait_ge(s_mm.h, mmn)
                    for ic in range(4):
                        sy.wait_ge(s_stic[ic].h, stic[ic])
                        s_ldx.inc(sy.dma_start(
                            out=acc[:, ic * 1024:(ic + 1) * 1024],
                            in_=xs[b, seg * 512 + 128 * ic: seg * 512 + 128 * (ic + 1), :]), 16)
                SY(x_load)
                ld_x = s_ldx.n

                wf1_base = s_wf1.n
                wf2_base = s_wf2.n
                w1_ready_at = {}
                w2_ready_at = {}

                def w_prefetch(sy, w1_ready_at=w1_ready_at, w2_ready_at=w2_ready_at):
                    for i in range(2):
                        sy.wait_ge(s_sts[i].h, s_sts[i].n)
                    for mc in range(32):
                        need = wf1_base + mc - 2
                        if need > 0:
                            sy.wait_ge(s_wf1.h, need)
                        slot = mc % 3
                        ins = sy.dma_start(
                            out=w1b[:, slot * 1024:(slot + 1) * 1024],
                            in_=w1_scr[:, mc, :, :])
                        s_w1s[slot].inc(ins, 16)
                        w1_ready_at[mc] = (slot, s_w1s[slot].n)
                    for p_ in range(2):
                        for kc in range(32):
                            need = wf2_base + 32 * p_ + kc - 2
                            if need > 0:
                                sy.wait_ge(s_wf2.h, need)
                            slot = (32 * p_ + kc) % 3
                            ins = sy.dma_start(
                                out=w2b[:, slot * 1024:(slot + 1) * 1024],
                                in_=w2_scr[kc, :, :])
                            s_w2s[slot].inc(ins, 16)
                            w2_ready_at[(p_, kc)] = (slot, s_w2s[slot].n)
                if stages != "xn":
                    SY(w_prefetch)

                if seg > 0:
                    def zmb_cast(v):
                        v.drain()
                        s_ev.inc(v.tensor_copy(out=zmb[:, :], in_=zmf[:, :]))
                    V(zmb_cast)

                def ln_v(v):
                    v.wait_ge(s_ldx.h, ld_x)
                    ins = None
                    for ic in range(4):
                        ins = v.reduce_sum(out=stats[:, ic:ic + 1],
                                           in_=acc[:, ic * 1024:(ic + 1) * 1024],
                                           axis=mybir.AxisListType.X)
                    s_ev.inc(ins)
                V(ln_v)
                ev_ln1 = s_ev.n

                def ln_s(s):
                    s.wait_ge(s_ldx.h, ld_x)
                    ins = None
                    for ic in range(4):
                        ins = s.activation(out=hsb[:, 0:1024], in_=acc[:, ic * 1024:(ic + 1) * 1024],
                                           func=AF.Square, accum_out=stats[:, 4 + ic:5 + ic])
                    s_sc.inc(ins)
                SC(ln_s)
                sc_ln = s_sc.n

                def ln_v2(v):
                    v.wait_ge(s_sc.h, sc_ln)
                    v.wait_ge(s_ev.h, ev_ln1)
                    v.tensor_scalar(out=stats[:, 8:12], in0=stats[:, 0:4], scalar1=1.0 / D,
                                    scalar2=None, op0=ALU.mult)
                    v.tensor_scalar(out=stats[:, 4:8], in0=stats[:, 4:8], scalar1=1.0 / D,
                                    scalar2=None, op0=ALU.mult)
                    v.drain()
                    v.tensor_tensor(out=stats[:, 12:16], in0=stats[:, 8:12],
                                    in1=stats[:, 8:12], op=ALU.mult)
                    v.drain()
                    s_ev.inc(v.tensor_tensor(out=stats[:, 12:16], in0=stats[:, 4:8],
                                             in1=stats[:, 12:16], op=ALU.subtract))
                V(ln_v2)
                ev_var = s_ev.n

                def ln_s2(s):
                    s.wait_ge(s_ev.h, ev_var)
                    s.activation(out=stats[:, 12:16], in_=stats[:, 12:16],
                                 func=AF.Ln, bias=eps_sb[:, :])
                    s.drain()
                    s_sc.inc(s.activation(out=stats[:, 12:16], in_=stats[:, 12:16],
                                          func=AF.Exp, scale=-0.5))
                SC(ln_s2)
                sc_rstd = s_sc.n

                def ln_v3(v):
                    v.wait_ge(s_sc.h, sc_rstd)
                    ins = None
                    for ic in range(4):
                        ins = v.tensor_scalar(
                            out=xn_tm[:, ic * 1024:(ic + 1) * 1024],
                            in0=acc[:, ic * 1024:(ic + 1) * 1024],
                            scalar1=stats[:, 8 + ic:9 + ic], scalar2=stats[:, 12 + ic:13 + ic],
                            op0=ALU.subtract, op1=ALU.mult)
                        s_ev.inc(ins)
                V(ln_v3)
                ev_xn = s_ev.n

                def ln_b2(v):
                    v.drain()
                    ins = None
                    for ic in range(4):
                        ins = v.tensor_tensor(out=acc[:, ic * 1024:(ic + 1) * 1024],
                                              in0=acc[:, ic * 1024:(ic + 1) * 1024],
                                              in1=b2bc[:, :], op=ALU.add)
                    s_ev.inc(ins)
                V(ln_b2)
                ev_b2 = s_ev.n

                if is_dbg:
                    dump("xn_tm", xn_tm[:, :], [(s_ev, ev_xn)])

                mm_xnT = s_mm.n
                ev_xnT = s_ev.n

                sc_now = s_sc.n

                def xnT(t, sc_now=sc_now):
                    t.wait_ge(s_ev.h, ev_xn)
                    t.wait_ge(s_sc.h, sc_now)
                    t.wait_ge(s_ld.h, P0_LD)
                    for jc in range(8):
                        half = jc % 2
                        if jc >= 2:
                            t.wait_ge(s_ev.h, ev_xnT + (jc - 1))
                        ins = None
                        for ic in range(4):
                            ins = t.transpose(
                                pc_bf[:, half * 1024 + ic * 128: half * 1024 + (ic + 1) * 128],
                                xn_tm[:, ic * 1024 + 128 * jc: ic * 1024 + 128 * (jc + 1)],
                                identb[:, :])
                        s_mm.inc(ins)
                T(xnT)

                def xnT_ev(v):
                    for jc in range(8):
                        half = jc % 2
                        v.wait_ge(s_mm.h, mm_xnT + jc + 1)
                        s_ev.inc(v.tensor_copy(out=xn_fm[:, jc * 512:(jc + 1) * 512],
                                               in_=pc_bf[:, half * 1024:half * 1024 + 512]))
                V(xnT_ev)
                ev_xnfm = s_ev.n

                if is_dbg:
                    dump("xn_fm", xn_fm[:, :], [(s_ev, ev_xnfm)])
                if stages == "xn":
                    continue

                mm_proj = s_mm.n
                ev_proj_b = s_ev.n
                sc_proj_b = s_sc.n
                projs = [(Wq_sb, qd2), (Wk_sb, kd2), (Wv_sb, vd2)]

                def proj_t(t):
                    t.wait_ge(s_ev.h, ev_xnfm)
                    cnt = 0
                    for Wsb, dst in projs:
                        for jc in range(8):
                            p = pa if cnt % 2 == 0 else pb
                            if cnt >= 2:
                                t.wait_ge(s_ev.h, ev_proj_b + 2 * (cnt - 1))
                                t.wait_ge(s_sc.h, sc_proj_b + 2 * (cnt - 1))
                            ins = None
                            for kc in range(8):
                                ins = t.matmul(
                                    p[:, 0:512],
                                    Wsb[:, kc * 1024 + 128 * jc: kc * 1024 + 128 * (jc + 1)],
                                    xn_fm[:, kc * 512:(kc + 1) * 512],
                                    start=(kc == 0), stop=(kc == 7))
                            s_mm.inc(ins)
                            cnt += 1
                T(proj_t)

                def proj_ev(who, first=(b == 0 and seg == 0), sts_tot=[x.n for x in s_sts]):
                    def f(en):
                        if first:
                            en.wait_ge(s_sts[0].h, sts_tot[0])
                            en.wait_ge(s_sts[1].h, sts_tot[1])
                        cnt = 0
                        for Wsb, dst in projs:
                            for jc in range(8):
                                p = pa if cnt % 2 == 0 else pb
                                en.wait_ge(s_mm.h, mm_proj + cnt + 1)
                                cpar = who
                                for hp in range(2):
                                    src = p[64 * cpar:64 * (cpar + 1), 0:512].rearrange(
                                        "p (g two n tw) -> p g two n tw",
                                        g=8, two=2, n=4)[:, :, hp, :, :]
                                    dstv = dst[64 * hp:64 * (hp + 1), :].rearrange(
                                        "p (g n c tw) -> p g n c tw",
                                        g=8, n=4, c=16)[:, :, :, 2 * jc + cpar, :]
                                    if who == 0:
                                        ins = en.tensor_copy(out=dstv, in_=src)
                                        s_ev.inc(ins)
                                    else:
                                        ins = en.copy(out=dstv, in_=src)
                                        s_sc.inc(ins)
                                cnt += 1
                    return f
                V(proj_ev(0))
                SC(proj_ev(1))
                ev_proj = s_ev.n
                sc_proj = s_sc.n

                if is_dbg:
                    dump("qd2", qd2[:, :], [(s_ev, ev_proj), (s_sc, sc_proj)])
                    dump("kd2", kd2[:, :], [(s_ev, ev_proj), (s_sc, sc_proj)])
                    dump("vd2", vd2[:, :], [(s_ev, ev_proj), (s_sc, sc_proj)])

                mm_vpos = s_mm.n
                ev_vpos_b = s_ev.n
                sc_vpos_b = s_sc.n

                def vpos_t(t):
                    t.wait_ge(s_ev.h, ev_proj)
                    t.wait_ge(s_sc.h, sc_proj)
                    for g_ in range(8):
                        half = g_ % 2
                        if g_ >= 2:
                            t.wait_ge(s_ev.h, ev_vpos_b + (g_ - 1))
                            t.wait_ge(s_sc.h, sc_vpos_b + 4 * (g_ - 1))
                        ins = None
                        for n in range(4):
                            ins = t.transpose(
                                pc_bf[:, half * 1024 + n * 128: half * 1024 + (n + 1) * 128],
                                vd2[:, g_ * 512 + 128 * n: g_ * 512 + 128 * (n + 1)],
                                identb[:, :])
                        s_mm.inc(ins)
                T(vpos_t)

                def vpos_ev(v):
                    for g_ in range(8):
                        half = g_ % 2
                        v.wait_ge(s_mm.h, mm_vpos + g_ + 1)
                        src = pc_bf[:, half * 1024: half * 1024 + 512].rearrange(
                            "p (n hp v) -> p n hp v", n=4, hp=2)
                        dst = vposm[:, g_ * 520:(g_ + 1) * 520].rearrange(
                            "p (n hp c) -> p n hp c", n=4, hp=2)[:, :, :, 1:65]
                        s_ev.inc(v.tensor_copy(out=dst, in_=src))
                        for n in range(4):
                            src2 = pc_bf[:, half * 1024 + n * 128: half * 1024 + n * 128 + 128].rearrange(
                                "p (hp v) -> p hp v", hp=2)
                            dst2 = vs[:, g_ * 520 + n * 130: g_ * 520 + (n + 1) * 130].rearrange(
                                "p (hp c) -> p hp c", hp=2)[:, :, 1:65]
                            omg = omg130[:, :].rearrange(
                                "p (g2 hp c) -> p g2 hp c", g2=8, hp=2)[:, g_, :, 1:65]
                            s_sc.inc(v.tensor_tensor(out=dst2, in0=src2, in1=omg, op=ALU.mult))
                V(vpos_ev)
                ev_vpos = s_ev.n
                sc_vs = s_sc.n

                mm_kpos = s_mm.n
                ev_kpos_b = s_ev.n
                sc_kpos_b = s_sc.n

                def kpos_t(t):
                    t.wait_ge(s_ev.h, ev_vpos)
                    t.wait_ge(s_sc.h, sc_vs)
                    for g_ in range(8):
                        half = g_ % 2
                        if g_ >= 2:
                            t.wait_ge(s_ev.h, ev_kpos_b + 2 * (g_ - 1))
                        ins = None
                        for n in range(4):
                            ins = t.transpose(
                                pd_bf[:, half * 1024 + n * 128: half * 1024 + (n + 1) * 128],
                                kd2[:, g_ * 512 + 128 * n: g_ * 512 + 128 * (n + 1)],
                                identb[:, :])
                        s_mm.inc(ins)
                T(kpos_t)

                def kpos_v(v):
                    for g_ in range(8):
                        half = g_ % 2
                        v.wait_ge(s_mm.h, mm_kpos + g_ + 1)
                        el = elu_t[:, half * 512:(half + 1) * 512]
                        s_ev.inc(v.tensor_scalar(
                            out=el, in0=pd_bf[:, half * 1024:half * 1024 + 512],
                            scalar1=0.0, scalar2=None, op0=ALU.min))
                        v.wait_ge(s_sc.h, sc_kpos_b + g_ + 1)
                        s_ev.inc(v.scalar_tensor_tensor(
                            out=skp[:, g_ * 512:(g_ + 1) * 512],
                            in0=pd_bf[:, half * 1024:half * 1024 + 512],
                            scalar=0.0, in1=el, op0=ALU.max, op1=ALU.add))
                V(kpos_v)

                def kpos_s(s):
                    for g_ in range(8):
                        half = g_ % 2
                        el = elu_t[:, half * 512:(half + 1) * 512]
                        s.wait_ge(s_ev.h, ev_kpos_b + 2 * g_ + 1)
                        s_sc.inc(s.activation(out=el, in_=el, func=AF.Exp))
                SC(kpos_s)
                ev_kpos = s_ev.n
                sc_kpos = s_sc.n

                if is_dbg:
                    dump("vposm", vposm[:, :], [(s_ev, ev_kpos), (s_sc, sc_kpos)])
                    dump("vs", vs[:, :], [(s_ev, ev_kpos), (s_sc, sc_kpos)])
                    dump("skp", skp[:, :], [(s_ev, ev_kpos), (s_sc, sc_kpos)])

                last_ev_attO = None
                last_sc_attO = None
                attO_done = []
                for g_ in range(8):
                    X, Y = (pc, pd) if g_ % 2 == 0 else (pa, pb)
                    X_bf = pc_bf if g_ % 2 == 0 else pa_bf
                    mm_sc_b = s_mm.n
                    sc_exp_b = s_sc.n

                    def pair_scores(t, g_=g_, first=(g_ == 0), sc_exp_b=sc_exp_b,
                                    X=X, Y=Y, attO_done=attO_done):
                        if first:
                            t.wait_ge(s_ev.h, ev_kpos)
                            t.wait_ge(s_sc.h, sc_kpos)
                        if g_ >= 2:
                            evn, scn = attO_done[g_ - 2]
                            t.wait_ge(s_ev.h, evn)
                            t.wait_ge(s_sc.h, scn)
                        for n in range(4):
                            psc = X if n % 2 == 0 else Y
                            if n >= 2:
                                t.wait_ge(s_sc.h, sc_exp_b + (n - 1))
                            ins = None
                            for hp in range(2):
                                ins = t.matmul(
                                    psc[:, hp * 512 + 128 * n: hp * 512 + 512],
                                    kd2[64 * hp:64 * (hp + 1),
                                        g_ * 512 + 128 * n: g_ * 512 + 128 * (n + 1)],
                                    qd2[64 * hp:64 * (hp + 1),
                                        g_ * 512 + 128 * n: (g_ + 1) * 512],
                                    start=True, stop=True, tile_position=(64 * hp, 0))
                            s_mm.inc(ins)
                    T(pair_scores)

                    def pair_exp(s, g_=g_, mm_sc_b=mm_sc_b, X=X, Y=Y):
                        for n in range(4):
                            psc = X if n % 2 == 0 else Y
                            s.wait_ge(s_mm.h, mm_sc_b + n + 1)
                            src = psc[:, :].rearrange("p (hp c) -> p hp c", hp=2)[:, :, 128 * n:512]
                            dst = pt[:, :].rearrange("p (hp n c) -> p hp n c",
                                                     hp=2, n=4)[:, :, n, 128 * n:512]
                            s_sc.inc(s.activation(out=dst, in_=src, func=AF.Exp,
                                                  bias=negc_sb[:, :], scale=0.125))
                    SC(pair_exp)
                    sc_exp = s_sc.n

                    def pair_mask(v, sc_exp=sc_exp):
                        v.wait_ge(s_sc.h, sc_exp)
                        ins = None
                        for hp in range(2):
                            for n in range(4):
                                base = hp * 2048 + n * 640
                                ins = v.tensor_tensor(
                                    out=pt[:, base:base + 128], in0=pt[:, base:base + 128],
                                    in1=mask8[:, 0:128], op=ALU.mult)
                        s_ev.inc(ins)
                    V(pair_mask)
                    ev_mask = s_ev.n

                    def pair_sq_min(v, g_=g_):
                        el = elu_t[:, (g_ % 2) * 512:(g_ % 2 + 1) * 512]
                        s_ev.inc(v.tensor_scalar(out=el,
                                                 in0=qd2[:, g_ * 512:(g_ + 1) * 512],
                                                 scalar1=0.0, scalar2=None, op0=ALU.min))
                    V(pair_sq_min)
                    ev_sqmin = s_ev.n

                    def pair_sq_exp(s, ev_sqmin=ev_sqmin, g_=g_):
                        el = elu_t[:, (g_ % 2) * 512:(g_ % 2 + 1) * 512]
                        s.wait_ge(s_ev.h, ev_sqmin)
                        s_sc.inc(s.activation(out=el, in_=el, func=AF.Exp))
                    SC(pair_sq_exp)
                    sc_sqexp = s_sc.n

                    def pair_sq_fin(v, g_=g_, sc_sqexp=sc_sqexp):
                        el = elu_t[:, (g_ % 2) * 512:(g_ % 2 + 1) * 512]
                        v.wait_ge(s_sc.h, sc_sqexp)
                        s_ev.inc(v.scalar_tensor_tensor(
                            out=sq_p[:, :], in0=qd2[:, g_ * 512:(g_ + 1) * 512],
                            scalar=0.0, in1=el, op0=ALU.max, op1=ALU.add))
                    V(pair_sq_fin)
                    ev_sq = s_ev.n

                    mm_mem_b = s_mm.n

                    def pair_mem(t, g_=g_, ev_sq=ev_sq, sc_exp=sc_exp, Y=Y):
                        t.wait_ge(s_ev.h, ev_sq)
                        t.wait_ge(s_sc.h, sc_exp)
                        for hp in range(2):
                            ins = None
                            for m in range(4):
                                ins = t.matmul(
                                    Y[:, hp * 512 + m * 65: hp * 512 + (m + 1) * 65],
                                    sq_p[64 * hp:64 * (hp + 1), 128 * m:128 * (m + 1)],
                                    zmb[64 * hp:64 * (hp + 1), g_ * 65:(g_ + 1) * 65],
                                    start=True, stop=True, tile_position=(64 * hp, 0))
                            s_mm.inc(ins)
                    T(pair_mem)
                    mm_mem = s_mm.n

                    mm_dot_b = s_mm.n

                    def pair_dot(t, g_=g_, ev_mask=ev_mask, X=X):
                        t.wait_ge(s_ev.h, ev_mask)
                        for hp in range(2):
                            for m in range(4):
                                ins = None
                                for n in range(m + 1):
                                    ins = t.matmul(
                                        X[:, hp * 512 + m * 65: hp * 512 + (m + 1) * 65],
                                        pt[:, hp * 2048 + n * 512 + 128 * m:
                                           hp * 2048 + n * 512 + 128 * (m + 1)],
                                        vs[:, g_ * 520 + n * 130 + hp * 65:
                                           g_ * 520 + n * 130 + (hp + 1) * 65],
                                        start=(n == 0), stop=(n == m))
                                s_mm.inc(ins)
                    T(pair_dot)
                    mm_dot = s_mm.n

                    def pair_recip(v, mm_mem=mm_mem, mm_dot=mm_dot, X=X, Y=Y):
                        v.wait_ge(s_mm.h, mm_mem)
                        den = Y[:, :].rearrange("p (hp q) -> p hp q", hp=2)[:, :, 0:260]
                        den = den.rearrange("p hp (x c) -> p hp x c", c=65)[:, :, :, 0]
                        v.tensor_scalar(out=rd4[:, 0:8].rearrange("p (hp x) -> p hp x", hp=2),
                                        in0=den, scalar1=tiny_sb[:, :],
                                        scalar2=None, op0=ALU.add)
                        v.drain()
                        v.reciprocal(out=rd4[:, 0:8], in_=rd4[:, 0:8])
                        v.wait_ge(s_mm.h, mm_dot)
                        sden = X[:, :].rearrange("p (hp q) -> p hp q", hp=2)[:, :, 0:260]
                        sden = sden.rearrange("p hp (x c) -> p hp x c", c=65)[:, :, :, 0]
                        s_ev.inc(v.reciprocal(out=rd4[:, 8:16].rearrange("p (hp x) -> p hp x", hp=2),
                                              in_=sden))
                    V(pair_recip)
                    ev_recip = s_ev.n

                    def pair_memn(v, ev_recip=ev_recip, Y=Y):
                        v.drain()
                        ins = None
                        for hp in range(2):
                            for m in range(4):
                                x = hp * 4 + m
                                ins = v.tensor_scalar(
                                    out=memn[:, x * 64:(x + 1) * 64],
                                    in0=Y[:, hp * 512 + m * 65 + 1: hp * 512 + (m + 1) * 65],
                                    scalar1=rd4[:, x:x + 1], scalar2=None, op0=ALU.mult)
                        s_sc.inc(ins)
                    V(pair_memn)
                    sc_memn = s_sc.n

                    def pair_comb(v, sc_memn=sc_memn, X=X):
                        v.drain()
                        ins = None
                        for hp in range(2):
                            for m in range(4):
                                x = hp * 4 + m
                                ins = v.scalar_tensor_tensor(
                                    out=att_p[:, x * 64:(x + 1) * 64],
                                    in0=X[:, hp * 512 + m * 65 + 1: hp * 512 + (m + 1) * 65],
                                    scalar=rd4[:, 8 + x:9 + x],
                                    in1=memn[:, x * 64:(x + 1) * 64],
                                    op0=ALU.mult, op1=ALU.add)
                        s_ev.inc(ins)
                    V(pair_comb)
                    ev_att = s_ev.n

                    if is_dbg and g_ == 0:
                        dump("pt", pt[:, :], [(s_ev, ev_mask)])
                        dump("sq", sq_p[:, :], [(s_ev, ev_sq)])
                        dump("att", att_p[:, :], [(s_ev, ev_att)])

                    mm_mupd_b = s_mm.n

                    def pair_mupd(t, g_=g_, sc_memn=sc_memn, Y=Y):
                        t.wait_ge(s_sc.h, sc_memn)
                        for hp in range(2):
                            ins = None
                            for n in range(4):
                                ins = t.matmul(
                                    Y[64 * hp:64 * (hp + 1), 772:837],
                                    skp[:, g_ * 512 + n * 128 + hp * 64:
                                        g_ * 512 + n * 128 + (hp + 1) * 64],
                                    vposm[:, g_ * 520 + n * 130 + hp * 65:
                                          g_ * 520 + n * 130 + (hp + 1) * 65],
                                    start=(n == 0), stop=(n == 3))
                            s_mm.inc(ins)
                    T(pair_mupd)
                    mm_mupd = s_mm.n

                    def pair_state(v, g_=g_, mm_mupd=mm_mupd, Y=Y):
                        v.wait_ge(s_mm.h, mm_mupd)
                        v.tensor_tensor(out=sscr[:, 0:65], in0=Y[:, 772:837],
                                        in1=gmp[:, g_ * 65:(g_ + 1) * 65], op=ALU.mult)
                        v.drain()
                        s_ev.inc(v.tensor_tensor(out=zmf[:, g_ * 65:(g_ + 1) * 65],
                                                 in0=zmf[:, g_ * 65:(g_ + 1) * 65],
                                                 in1=sscr[:, 0:65], op=ALU.add))
                    V(pair_state)

                    mm_attT_b = s_mm.n

                    def pair_attT(t, ev_att=ev_att, X_bf=X_bf):
                        t.wait_ge(s_ev.h, ev_att)
                        ins = None
                        for hp in range(2):
                            for m in range(4):
                                x = hp * 4 + m
                                ins = t.transpose(
                                    X_bf[64 * hp:64 * (hp + 1), 1280 + m * 128: 1280 + (m + 1) * 128],
                                    att_p[:, x * 64:(x + 1) * 64],
                                    identb[:, :])
                        s_mm.inc(ins)
                    T(pair_attT)
                    mm_attT = s_mm.n

                    def pair_attO(is_v, g_=g_, mm_attT=mm_attT, X_bf=X_bf):
                        def f(en):
                            en.wait_ge(s_mm.h, mm_attT)
                            for hp in range(2):
                                h_ = 2 * g_ + hp
                                for cpar in range(2):
                                    if (hp + cpar) % 2 != (0 if is_v else 1):
                                        continue
                                    src = X_bf[64 * hp:64 * (hp + 1), 1280:1792].rearrange(
                                        "p (m a t2) -> p m a t2",
                                        m=4, a=8)[:, :, :, cpar * 8:(cpar + 1) * 8]
                                    src = src.transpose([0, 2, 1, 3])
                                    dst = atto[64 * cpar:64 * (cpar + 1), :].rearrange(
                                        "p (a i) -> p a i", a=8)[:, :, 32 * h_:32 * h_ + 32]
                                    dst = dst.rearrange("p a (m tw) -> p a m tw", m=4)
                                    ins = en.tensor_copy(out=dst, in_=src)
                                    (s_ev if is_v else s_sc).inc(ins)
                        return f
                    V(pair_attO(True))
                    V(pair_attO(False))
                    last_ev_attO = s_ev.n
                    last_sc_attO = s_sc.n
                    attO_done.append((last_ev_attO, last_sc_attO))

                if is_dbg:
                    dump("atto", atto[:, :], [(s_ev, last_ev_attO), (s_sc, last_sc_attO)])
                    dump("zmf", zmf[:, :], [(s_ev, s_ev.n)])

                mm_wo_b = s_mm.n
                ev_wo_b = s_ev.n

                def wo_t(t, a=None, b_=None):
                    t.wait_ge(s_ev.h, last_ev_attO)
                    t.wait_ge(s_sc.h, last_sc_attO)
                    for ic in range(4):
                        p = pa if ic % 2 == 0 else pb
                        if ic >= 2:
                            t.wait_ge(s_ev.h, ev_wo_b + (ic - 1))
                        ins = None
                        for half in range(2):
                            for a in range(8):
                                ins = t.matmul(
                                    p[:, half * 512:(half + 1) * 512],
                                    atto[:, a * 512 + 128 * ic: a * 512 + 128 * (ic + 1)],
                                    Wo_sb[:, a * 1024 + half * 512: a * 1024 + (half + 1) * 512],
                                    start=(a == 0), stop=(a == 7))
                        s_mm.inc(ins)
                T(wo_t)

                def wo_ev(v, mm_wo_b=mm_wo_b):
                    for ic in range(4):
                        p = pa if ic % 2 == 0 else pb
                        v.wait_ge(s_mm.h, mm_wo_b + ic + 1)
                        if skip_wo:
                            s_ev.inc(v.tensor_copy(out=rd4[:, 1:2], in_=rd4[:, 1:2]))
                        else:
                            s_ev.inc(v.tensor_tensor(out=acc[:, ic * 1024:(ic + 1) * 1024],
                                                     in0=acc[:, ic * 1024:(ic + 1) * 1024],
                                                     in1=p[:, 0:1024], op=ALU.add))
                V(wo_ev)
                ev_wo = s_ev.n

                if is_dbg:
                    dump("accw", acc[:, :], [(s_ev, ev_wo)])

                mm_f1_b = s_mm.n
                if skip_ffn:
                    ev_f2 = s_ev.n

                def f1_t(t, wf1_base=wf1_base):
                    if skip_ffn:
                        t.wait_ge(s_ev.h, ev_wo)
                        for mc in range(32):
                            sl, nn = w1_ready_at[mc]
                            t.wait_ge(s_w1s[sl].h, nn)
                        for p_ in range(2):
                            for kc in range(32):
                                sl, nn = w2_ready_at[(p_, kc)]
                                t.wait_ge(s_w2s[sl].h, nn)
                        return
                    t.wait_ge(s_ev.h, ev_wo)
                    for mc in range(32):
                        p = pa if mc % 2 == 0 else pb
                        sl, nn = w1_ready_at[mc]
                        t.wait_ge(s_w1s[sl].h, nn)
                        if mc >= 2:
                            t.wait_ge(s_wf1.h, wf1_base + (mc - 1))
                        ins = None
                        for kc in range(8):
                            ins = t.matmul(
                                p[:, 0:512],
                                w1b[:, (mc % 3) * 1024 + kc * 128: (mc % 3) * 1024 + (kc + 1) * 128],
                                xn_fm[:, kc * 512:(kc + 1) * 512],
                                start=(kc == 0), stop=(kc == 7))
                        s_mm.inc(ins)
                T(f1_t)

                def f1_ev(s, mm_f1_b=mm_f1_b):
                    if skip_ffn:
                        for mc in range(32):
                            s_wf1.inc(s.engine_nop() if False else s.copy(out=elu_t[:, 0:1], in_=elu_t[:, 0:1]))
                        return
                    for mc in range(32):
                        p = pa if mc % 2 == 0 else pb
                        s.wait_ge(s_mm.h, mm_f1_b + mc + 1)
                        s_wf1.inc(s.activation(out=hsb[:, mc * 512:(mc + 1) * 512],
                                               in_=p[:, 0:512], func=AF.Gelu,
                                               bias=b1_sb[:, mc:mc + 1]))
                SC(f1_ev)
                wf1_done = s_wf1.n

                if is_dbg:
                    dump("hsb", hsb[:, 0:2048], [(s_wf1, wf1_done)])

                mm_f2_b = s_mm.n
                ev_f2_b = s_ev.n

                def f2_t(t, wf1_done=wf1_done):
                    if skip_ffn:
                        return
                    t.wait_ge(s_wf1.h, wf1_done)
                    for p_ in range(2):
                        if p_ == 1:
                            t.wait_ge(s_ev.h, ev_f2_b + 2)
                        for kc in range(32):
                            sl, nn = w2_ready_at[(p_, kc)]
                            t.wait_ge(s_w2s[sl].h, nn)
                            ins = None
                            for ic2 in range(2):
                                ic = 2 * p_ + ic2
                                p = pa if ic2 == 0 else pb
                                for half in range(2):
                                    ins = t.matmul(
                                        p[:, half * 512:(half + 1) * 512],
                                        hsb[:, kc * 512 + 128 * ic: kc * 512 + 128 * (ic + 1)],
                                        w2b[:, ((32 * p_ + kc) % 3) * 1024 + half * 512:
                                            ((32 * p_ + kc) % 3) * 1024 + (half + 1) * 512],
                                        start=(kc == 0), stop=(kc == 31))
                            s_wf2.inc(ins)
                T(f2_t)

                def f2_ev(v, wf2_base=wf2_base):
                    if skip_ffn:
                        ins = None
                        for i_ in range(64):
                            ins = v.tensor_copy(out=rd4[:, 0:1], in_=rd4[:, 0:1])
                            s_wf2.inc(ins)
                        return
                    for p_ in range(2):
                        v.wait_ge(s_wf2.h, wf2_base + 32 * (p_ + 1))
                        for ic2 in range(2):
                            ic = 2 * p_ + ic2
                            p = pa if ic2 == 0 else pb
                            s_ev.inc(v.tensor_tensor(
                                out=acc[:, ic * 1024:(ic + 1) * 1024],
                                in0=acc[:, ic * 1024:(ic + 1) * 1024],
                                in1=p[:, 0:1024], op=ALU.add))
                V(f2_ev)
                ev_f2 = s_ev.n

                if is_dbg:
                    dump("acc", acc[:, :], [(s_ev, ev_f2)])

                def store(sy, b=b, seg=seg, ev_f2=ev_f2):
                    sy.wait_ge(s_ev.h, ev_f2)
                    for ic in range(4):
                        s_stic[ic].inc(sy.dma_start(
                            out=ys[b, seg * 512 + 128 * ic: seg * 512 + 128 * (ic + 1), :],
                            in_=acc[:, ic * 1024:(ic + 1) * 1024]), 16)
                SY(store)

        def fin(sy):
            sy.wait_ge(s_st.h, s_st.n)
            for i in range(4):
                sy.wait_ge(s_stic[i].h, s_stic[i].n)
        SY(fin)

    return nc


def host_arrange_inputs(inp, nb=4, core=0):
    x = np.ascontiguousarray(inp["L0_attn_x"][core * nb:(core + 1) * nb])
    betas = np.ascontiguousarray(inp["L0_attn_betas"].reshape(H, DV))
    m0 = inp["L0_attn_memory0"].reshape(H, DK, DV)
    z0 = inp["L0_attn_z0"].reshape(H, DK)
    m0_arr = np.zeros((128, 512), dtype=np.float32)
    z0_arr = np.zeros((128, 8), dtype=np.float32)
    for g in range(8):
        for hp in range(2):
            h = 2 * g + hp
            m0_arr[64 * hp:64 * (hp + 1), g * 64:(g + 1) * 64] = m0[h]
            z0_arr[64 * hp:64 * (hp + 1), g] = z0[h]
    return {
        "xs": x, "Wq": inp["L0_attn_Wq"], "Wk": inp["L0_attn_Wk"],
        "Wv": inp["L0_attn_Wv"], "Wo": inp["L0_attn_Wo"],
        "w1": inp["L0_ffn_w1"], "b1": inp["L0_ffn_b1"],
        "w2": inp["L0_ffn_w2"], "b2": inp["L0_ffn_b2"],
        "betas": betas, "m0_arr": m0_arr, "z0_arr": z0_arr,
    }


_NCORES = 8


def kernel(**inputs):
    from concourse.bass_utils import run_bass_kernel_spmd
    in_maps = [host_arrange_inputs(inputs, nb=4, core=c) for c in range(_NCORES)]
    nc = build(nb=4, nsegs=4)
    res = run_bass_kernel_spmd(nc, in_maps, core_ids=list(range(_NCORES)))
    out = np.concatenate([res.results[c]["ys"] for c in range(_NCORES)], axis=0)
    return out.astype(np.float32)


# version 04
# speedup vs baseline: 1.1663x; 1.0314x over previous
import numpy as np
import contextlib
import concourse.bass as bass
import concourse.mybir as mybir

F32 = mybir.dt.float32
BF16 = mybir.dt.bfloat16
AF = mybir.ActivationFunctionType
ALU = mybir.AluOpType

H, DK, DV = 16, 64, 64
D = 1024
NHID = 4096
CEXP = 12.0
EPS = 1e-5


class Sem:
    def __init__(self, nc, st, name):
        self.h = st.enter_context(nc.semaphore(name))
        self.n = 0

    def inc(self, ins, k=1):
        ins.then_inc(self.h, k)
        self.n += k
        return ins


def host_constants():
    ident = np.eye(128, dtype=np.float32)
    m = np.zeros((128, 128), dtype=np.float32)
    for c2 in range(16):
        for t2 in range(8):
            for c1 in range(16):
                for t1 in range(8):
                    m[8 * c2 + t2, 8 * c1 + t1] = 1.0 if (16 * t2 + c2) <= (16 * t1 + c1) else 0.0
    return {"ident": ident, "diagmask": m}


def build(nb=4, nsegs=4, debug=(), dbg_tile=None, skip_ffn=False, skip_wo=False, stages=None, detect=False):
    nc = bass.Bass("TRN2", target_bir_lowering=False, detect_race_conditions=detect)
    S = 2048
    if dbg_tile is None:
        dbg_tile = (0, nsegs - 1)

    xs = nc.dram_tensor("xs", [nb, S, D], F32, kind="ExternalInput").ap()
    Wq = nc.dram_tensor("Wq", [D, D], F32, kind="ExternalInput").ap()
    Wk = nc.dram_tensor("Wk", [D, D], F32, kind="ExternalInput").ap()
    Wv = nc.dram_tensor("Wv", [D, D], F32, kind="ExternalInput").ap()
    Wo = nc.dram_tensor("Wo", [D, D], F32, kind="ExternalInput").ap()
    w1 = nc.dram_tensor("w1", [D, NHID], F32, kind="ExternalInput").ap()
    b1 = nc.dram_tensor("b1", [NHID], F32, kind="ExternalInput").ap()
    w2 = nc.dram_tensor("w2", [NHID, D], F32, kind="ExternalInput").ap()
    b2 = nc.dram_tensor("b2", [D], F32, kind="ExternalInput").ap()
    betas = nc.dram_tensor("betas", [H, DV], F32, kind="ExternalInput").ap()
    m0_arr = nc.dram_tensor("m0_arr", [128, 512], F32, kind="ExternalInput").ap()
    z0_arr = nc.dram_tensor("z0_arr", [128, 8], F32, kind="ExternalInput").ap()
    ys = nc.dram_tensor("ys", [nb, S, D], F32, kind="ExternalOutput").ap()

    consts = host_constants()
    ident_dram = nc.inline_tensor(consts["ident"], "identc")
    mask_dram = nc.inline_tensor(consts["diagmask"], "maskc")

    w1_scr = nc.dram_tensor("w1_scr", [128, 32, 8, 128], BF16, kind="Internal").ap()
    w2_scr = nc.dram_tensor("w2_scr", [32, 128, D], BF16, kind="Internal").ap()
    gate_scr = nc.dram_tensor("gate_scr", [H, 65], F32, kind="Internal").ap()
    omg_scr = nc.dram_tensor("omg_scr", [H, 65], F32, kind="Internal").ap()

    dbg_specs = {
        "xn_fm": ([128, 8 * 512], BF16), "qd2": ([128, 4096], BF16),
        "kd2": ([128, 4096], BF16), "vd2": ([128, 4096], BF16),
        "vposm": ([128, 4160], BF16), "vs": ([128, 4160], BF16),
        "skp": ([128, 4096], BF16), "pt": ([128, 4096], BF16),
        "att": ([128, 512], BF16), "atto": ([128, 4096], BF16),
        "acc": ([128, 4096], F32), "accw": ([128, 4096], F32), "zmf": ([128, 520], F32),
        "sq": ([128, 512], BF16), "hsb": ([128, 2048], BF16),
        "xn_tm": ([128, 4096], BF16),
    }
    dbg_drams = {k: nc.dram_tensor("dbg_" + k, *dbg_specs[k], kind="ExternalOutput").ap()
                 for k in debug}

    with contextlib.ExitStack() as st:
        e = st.enter_context

        def sb(name, shape, dt):
            return e(nc.sbuf_tensor(name, shape, dt))

        Wq_sb = sb("Wq_sb", [128, 8 * 1024], BF16)
        Wk_sb = sb("Wk_sb", [128, 8 * 1024], BF16)
        Wv_sb = sb("Wv_sb", [128, 8 * 1024], BF16)
        Wo_sb = sb("Wo_sb", [128, 8 * 1024], BF16)
        w1b = sb("w1b", [128, 3 * 1024], BF16)
        w2b = sb("w2b", [128, 3 * 1024], BF16)
        acc = sb("acc", [128, 4 * 1024], F32)
        xn_tm = sb("xn_tm", [128, 4 * 1024], BF16)
        xn_fm = sb("xn_fm", [128, 8 * 512], BF16)
        qd2 = sb("qd2", [128, 4096], BF16)
        kd2 = sb("kd2", [128, 4096], BF16)
        vd2 = sb("vd2", [128, 4096], BF16)
        vposm = sb("vposm", [128, 4160], BF16)
        vs = sb("vs", [128, 4160], BF16)
        skp = sb("skp", [128, 4096], BF16)
        pt = vd2
        att_p = sb("att_p", [128, 512], BF16)
        memn = sb("memn", [128, 512], BF16)
        sq_p = sb("sq_p", [128, 1024], BF16)
        elu_t = sb("elu_t", [128, 1024], BF16)
        atto = xn_tm
        hsb = sb("hsb", [128, 32 * 512], BF16)
        sscr = sb("sscr", [128, 65], F32)
        zmf = sb("zmf", [128, 8 * 65], F32)
        zmb = sb("zmb", [128, 8 * 65], BF16)
        gmp = sb("gmp", [128, 8 * 65], F32)
        omg130 = sb("omg130", [128, 8 * 130], BF16)
        b2bc = sb("b2bc", [128, 1024], F32)
        b1_sb = sb("b1_sb", [128, 32], F32)
        mask8 = sb("mask8", [128, 128], BF16)
        identb = sb("identb", [128, 128], BF16)
        gstage = sb("gstage", [H, 65], F32)
        stats = sb("stats", [128, 16], F32)
        rd4 = sb("rd4", [128, 16], F32)
        eps_sb = sb("eps_sb", [128, 1], F32)
        negc_sb = sb("negc_sb", [128, 1], F32)
        tiny_sb = sb("tiny_sb", [128, 1], F32)
        zero_sb = sb("zero_sb", [128, 1], F32)
        onesb = sb("onesb", [1, 1], F32)

        class _WStage:
            def __getitem__(self, sl):
                return None
        def wslice(kc, width):
            buf = qd2 if kc % 2 == 0 else kd2
            return buf[:, 0:width]

        pa = e(nc.psum_tensor("pa", [128, 1024], F32))
        pb = e(nc.psum_tensor("pb", [128, 1024], F32))
        pc = e(nc.psum_tensor("pc", [128, 1024], F32))
        pd = e(nc.psum_tensor("pd", [128, 1024], F32))
        pc_bf = pc.ap().bitcast(BF16)
        pd_bf = pd.ap().bitcast(BF16)
        pa_bf = pa.ap().bitcast(BF16)

        s_ld = Sem(nc, st, "s_ld")
        s_ldx = Sem(nc, st, "s_ldx")
        s_state = Sem(nc, st, "s_state")
        s_w1s = [Sem(nc, st, f"s_w1s{i}") for i in range(3)]
        s_w2s = [Sem(nc, st, f"s_w2s{i}") for i in range(3)]
        s_stg = [Sem(nc, st, f"s_stg{i}") for i in range(2)]
        s_sts = [Sem(nc, st, f"s_sts{i}") for i in range(2)]
        s_wf1 = Sem(nc, st, "s_wf1")
        s_wf2 = Sem(nc, st, "s_wf2")
        s_mm = Sem(nc, st, "s_mm")
        s_ev = Sem(nc, st, "s_ev")
        s_sc = Sem(nc, st, "s_sc")
        s_st = Sem(nc, st, "s_st")
        s_stic = [Sem(nc, st, f"s_stic{i}") for i in range(4)]
        block = e(nc.Block())

        T, V, SC, SY, G = block.tensor, block.vector, block.scalar, block.sync, block.gpsimd

        def dump(name, ap_src, waits):
            if name not in dbg_drams:
                return
            dst = dbg_drams[name]

            def _f(sy, dst=dst, ap_src=ap_src, waits=tuple(waits)):
                for sem, n in waits:
                    sy.wait_ge(sem.h, n)
                s_st.inc(sy.dma_start(out=dst[:, :], in_=ap_src), 16)
            SY(_f)

        def p0_gpsimd(g):
            for W, Wsb in ((Wq, Wq_sb), (Wk, Wk_sb), (Wv, Wv_sb), (Wo, Wo_sb)):
                for kc in range(8):
                    s_ld.inc(g.dma_start(out=Wsb[:, kc * 1024:(kc + 1) * 1024],
                                         in_=W[128 * kc:128 * (kc + 1), :]), 16)
            s_ld.inc(g.dma_start(out=identb[:, :], in_=ident_dram.ap()[:, :]), 16)
            s_ld.inc(g.dma_start(out=mask8[:, 0:128], in_=mask_dram.ap()[:, :]), 16)
            s_ld.inc(g.dma_start(out=b2bc[:, :],
                                 in_=b2.unsqueeze(0).partition_broadcast(128)), 16)
            s_ld.inc(g.dma_start(out=gstage[:, 1:65], in_=betas[:, :]), 16)
            s_ld.inc(g.dma_start(out=acc[0:1, 0:4096], in_=b1.unsqueeze(0)), 16)
        G(p0_gpsimd)
        ld0 = s_ld.n

        def p0_vector(v):
            v.memset(eps_sb[:, :], EPS)
            v.memset(negc_sb[:, :], -CEXP)
            v.memset(tiny_sb[:, :], 1e-35)
            v.memset(zero_sb[:, :], 0.0)
            v.memset(onesb[:, :], 1.0)
            v.memset(gstage[:, 0:1], 1.0)
            v.memset(vposm[:, :].rearrange("p (b c) -> p b c", c=65)[:, :, 0], 1.0)
            v.memset(vs[:, :].rearrange("p (b c) -> p b c", c=65)[:, :, 0], 1.0)
            v.wait_ge(s_ld.h, ld0)
            s_ev.inc(v.memset(rd4[:, 0:1], 0.0))
        V(p0_vector)
        ev0 = s_ev.n

        def p0_scalar(s):
            s.wait_ge(s_ld.h, ld0)
            s.wait_ge(s_ev.h, ev0)
            s_sc.inc(s.activation(out=gstage[:, 1:65], in_=gstage[:, 1:65], func=AF.Sigmoid))
        SC(p0_scalar)
        sc_gate = s_sc.n

        def p0_vec2(v):
            v.wait_ge(s_sc.h, sc_gate)
            v.tensor_scalar(out=sscr[0:H, 0:65], in0=gstage[:, 0:65],
                            scalar1=1.0, scalar2=-1.0, op0=ALU.subtract, op1=ALU.mult)
            v.drain()
            s_ev.inc(v.memset(sscr[0:H, 0:1], 1.0))
        V(p0_vec2)
        ev_omg = s_ev.n

        def p0_b1_t(t):
            t.wait_ge(s_ld.h, ld0)
            t.wait_ge(s_ev.h, ev0)
            ins = None
            for mc in range(32):
                ins = t.matmul(pa[0:128, mc:mc + 1], acc[0:1, 128 * mc:128 * (mc + 1)],
                               onesb[0:1, 0:1], start=True, stop=True)
            s_mm.inc(ins)
        T(p0_b1_t)
        mm_b1 = s_mm.n

        def p0_b1_v(v):
            v.wait_ge(s_mm.h, mm_b1)
            s_ev.inc(v.tensor_copy(out=b1_sb[:, :], in_=pa[:, 0:32]))
        V(p0_b1_v)
        ev_b1 = s_ev.n

        def p0_gates_sy(sy):
            sy.wait_ge(s_sc.h, sc_gate)
            s_st.inc(sy.dma_start(out=gate_scr[:, :], in_=gstage[:, :]), 16)
            sy.wait_ge(s_ev.h, ev_omg)
            s_st.inc(sy.dma_start(out=omg_scr[:, :], in_=sscr[0:H, 0:65]), 16)
            sy.wait_ge(s_st.h, s_st.n)
            for g_ in range(8):
                for hp in range(2):
                    s_state.inc(sy.dma_start(
                        out=gmp[64 * hp:64 * (hp + 1), g_ * 65:(g_ + 1) * 65],
                        in_=gate_scr[2 * g_ + hp:2 * g_ + hp + 1, :].partition_broadcast(64)), 16)
        SY(p0_gates_sy)

        def p0_omg_g(g):
            g.wait_ge(s_st.h, s_st.n)
            for g_ in range(8):
                for hp in range(2):
                    s_ld.inc(g.dma_start(
                        out=omg130[:, (g_ * 2 + hp) * 65:(g_ * 2 + hp + 1) * 65],
                        in_=omg_scr[2 * g_ + hp:2 * g_ + hp + 1, :].partition_broadcast(128)), 16)
        G(p0_omg_g)

        def p0_w1_g(g):
            for kc in range(8):
                if kc >= 2:
                    g.wait_ge(s_sts[kc % 2].h, 16 * (kc // 2))
                s_stg[kc % 2].inc(g.dma_start(
                    out=wslice(kc, 4096),
                    in_=w1[128 * kc:128 * (kc + 1), :]), 16)
        G(p0_w1_g)

        def p0_w1_sy(sy):
            for kc in range(8):
                sy.wait_ge(s_stg[kc % 2].h, 16 * (kc // 2 + 1))
                src = wslice(kc, 4096).rearrange("p (mc c) -> p mc c", mc=32)
                dst = w1_scr[:, :, kc, :]
                s_sts[kc % 2].inc(sy.dma_start(out=dst, in_=src), 16)
        SY(p0_w1_sy)

        w1_sts_base = [s.n for s in s_sts]
        w1_stg_base = [s.n for s in s_stg]

        def p0_w2_g(g):
            for kc in range(32):
                if kc == 0:
                    g.wait_ge(s_sts[0].h, w1_sts_base[0])
                    g.wait_ge(s_sts[1].h, w1_sts_base[1])
                if kc >= 2:
                    g.wait_ge(s_sts[kc % 2].h, w1_sts_base[kc % 2] + 16 * (kc // 2))
                s_stg[kc % 2].inc(g.dma_start(
                    out=wslice(kc, 1024),
                    in_=w2[128 * kc:128 * (kc + 1), :]), 16)
        G(p0_w2_g)

        def p0_w2_sy(sy):
            for kc in range(32):
                sy.wait_ge(s_stg[kc % 2].h, w1_stg_base[kc % 2] + 16 * (kc // 2 + 1))
                s_sts[kc % 2].inc(sy.dma_start(
                    out=w2_scr[kc, :, :],
                    in_=wslice(kc, 1024)), 16)
        SY(p0_w2_sy)

        P0_LD = s_ld.n
        P0_ST = s_st.n
        P0_MM = s_mm.n

        for b in range(nb):
            if stages == "xn":
                pass
            def init_state_sy(sy, b=b, mmn=s_mm.n, evn=s_ev.n):
                sy.wait_ge(s_st.h, P0_ST)
                if b > 0:
                    sy.wait_ge(s_mm.h, mmn)
                    sy.wait_ge(s_ev.h, evn)
                s_state.inc(sy.dma_start(
                    out=zmf[:, :].rearrange("p (g c) -> p g c", g=8)[:, :, 0],
                    in_=z0_arr[:, :], allow_slow_non_contiguous=True), 16)
                s_state.inc(sy.dma_start(
                    out=zmf[:, :].rearrange("p (g c) -> p g c", g=8)[:, :, 1:65],
                    in_=m0_arr[:, :].rearrange("p (g v) -> p g v", g=8)), 16)
            SY(init_state_sy)
            ld_state = s_state.n

            def init_state_v(v, b=b):
                v.wait_ge(s_state.h, ld_state)
                if b == 0:
                    v.wait_ge(s_ld.h, P0_LD)
                s_ev.inc(v.tensor_copy(out=zmb[:, :], in_=zmf[:, :]))
            V(init_state_v)
            ev_state0 = s_ev.n

            for seg in range(nsegs):
                is_dbg = (b, seg) == dbg_tile and dbg_drams

                def x_load(sy, b=b, seg=seg, stic=[x.n for x in s_stic], mmn=P0_MM):
                    if b == 0 and seg == 0:
                        sy.wait_ge(s_mm.h, mmn)
                    for ic in range(4):
                        sy.wait_ge(s_stic[ic].h, stic[ic])
                        s_ldx.inc(sy.dma_start(
                            out=acc[:, ic * 1024:(ic + 1) * 1024],
                            in_=xs[b, seg * 512 + 128 * ic: seg * 512 + 128 * (ic + 1), :]), 16)
                SY(x_load)
                ld_x = s_ldx.n

                wf1_base = s_wf1.n
                wf2_base = s_wf2.n
                w1_ready_at = {}
                w2_ready_at = {}

                def w_prefetch(sy, w1_ready_at=w1_ready_at, w2_ready_at=w2_ready_at):
                    for i in range(2):
                        sy.wait_ge(s_sts[i].h, s_sts[i].n)
                    for mc in range(32):
                        need = wf1_base + mc - 2
                        if need > 0:
                            sy.wait_ge(s_wf1.h, need)
                        slot = mc % 3
                        ins = sy.dma_start(
                            out=w1b[:, slot * 1024:(slot + 1) * 1024],
                            in_=w1_scr[:, mc, :, :])
                        s_w1s[slot].inc(ins, 16)
                        w1_ready_at[mc] = (slot, s_w1s[slot].n)
                    for p_ in range(2):
                        for kc in range(32):
                            need = wf2_base + 32 * p_ + kc - 2
                            if need > 0:
                                sy.wait_ge(s_wf2.h, need)
                            slot = (32 * p_ + kc) % 3
                            ins = sy.dma_start(
                                out=w2b[:, slot * 1024:(slot + 1) * 1024],
                                in_=w2_scr[kc, :, :])
                            s_w2s[slot].inc(ins, 16)
                            w2_ready_at[(p_, kc)] = (slot, s_w2s[slot].n)
                if stages != "xn":
                    SY(w_prefetch)

                if seg > 0:
                    def zmb_cast(v):
                        v.drain()
                        s_ev.inc(v.tensor_copy(out=zmb[:, :], in_=zmf[:, :]))
                    V(zmb_cast)

                def ln_v(v):
                    v.wait_ge(s_ldx.h, ld_x)
                    ins = None
                    for ic in range(4):
                        ins = v.reduce_sum(out=stats[:, ic:ic + 1],
                                           in_=acc[:, ic * 1024:(ic + 1) * 1024],
                                           axis=mybir.AxisListType.X)
                    s_ev.inc(ins)
                V(ln_v)
                ev_ln1 = s_ev.n

                def ln_s(s):
                    s.wait_ge(s_ldx.h, ld_x)
                    ins = None
                    for ic in range(4):
                        ins = s.activation(out=hsb[:, 0:1024], in_=acc[:, ic * 1024:(ic + 1) * 1024],
                                           func=AF.Square, accum_out=stats[:, 4 + ic:5 + ic])
                    s_sc.inc(ins)
                SC(ln_s)
                sc_ln = s_sc.n

                def ln_v2(v):
                    v.wait_ge(s_sc.h, sc_ln)
                    v.wait_ge(s_ev.h, ev_ln1)
                    v.tensor_scalar(out=stats[:, 8:12], in0=stats[:, 0:4], scalar1=1.0 / D,
                                    scalar2=None, op0=ALU.mult)
                    v.tensor_scalar(out=stats[:, 4:8], in0=stats[:, 4:8], scalar1=1.0 / D,
                                    scalar2=None, op0=ALU.mult)
                    v.drain()
                    v.tensor_tensor(out=stats[:, 12:16], in0=stats[:, 8:12],
                                    in1=stats[:, 8:12], op=ALU.mult)
                    v.drain()
                    s_ev.inc(v.tensor_tensor(out=stats[:, 12:16], in0=stats[:, 4:8],
                                             in1=stats[:, 12:16], op=ALU.subtract))
                V(ln_v2)
                ev_var = s_ev.n

                def ln_s2(s):
                    s.wait_ge(s_ev.h, ev_var)
                    s.activation(out=stats[:, 12:16], in_=stats[:, 12:16],
                                 func=AF.Ln, bias=eps_sb[:, :])
                    s.drain()
                    s_sc.inc(s.activation(out=stats[:, 12:16], in_=stats[:, 12:16],
                                          func=AF.Exp, scale=-0.5))
                SC(ln_s2)
                sc_rstd = s_sc.n

                def ln_v3(v):
                    v.wait_ge(s_sc.h, sc_rstd)
                    ins = None
                    for ic in range(4):
                        ins = v.tensor_scalar(
                            out=xn_tm[:, ic * 1024:(ic + 1) * 1024],
                            in0=acc[:, ic * 1024:(ic + 1) * 1024],
                            scalar1=stats[:, 8 + ic:9 + ic], scalar2=stats[:, 12 + ic:13 + ic],
                            op0=ALU.subtract, op1=ALU.mult)
                        s_ev.inc(ins)
                V(ln_v3)
                ev_xn = s_ev.n

                def ln_b2(v):
                    v.drain()
                    ins = None
                    for ic in range(4):
                        ins = v.tensor_tensor(out=acc[:, ic * 1024:(ic + 1) * 1024],
                                              in0=acc[:, ic * 1024:(ic + 1) * 1024],
                                              in1=b2bc[:, :], op=ALU.add)
                    s_ev.inc(ins)
                V(ln_b2)
                ev_b2 = s_ev.n

                if is_dbg:
                    dump("xn_tm", xn_tm[:, :], [(s_ev, ev_xn)])

                mm_xnT = s_mm.n
                ev_xnT = s_ev.n

                sc_now = s_sc.n

                def xnT(t, sc_now=sc_now):
                    t.wait_ge(s_ev.h, ev_xn)
                    t.wait_ge(s_sc.h, sc_now)
                    t.wait_ge(s_ld.h, P0_LD)
                    for jc in range(8):
                        pb_t = pc_bf if jc < 4 else pd_bf
                        ins = None
                        for ic in range(4):
                            ins = t.transpose(
                                pb_t[:, (jc % 4) * 512 + ic * 128: (jc % 4) * 512 + (ic + 1) * 128],
                                xn_tm[:, ic * 1024 + 128 * jc: ic * 1024 + 128 * (jc + 1)],
                                identb[:, :])
                        s_mm.inc(ins)
                T(xnT)

                def xnT_ev(v):
                    v.wait_ge(s_mm.h, mm_xnT + 4)
                    s_ev.inc(v.tensor_copy(out=xn_fm[:, 0:2048], in_=pc_bf[:, 0:2048]))
                    v.wait_ge(s_mm.h, mm_xnT + 8)
                    s_ev.inc(v.tensor_copy(out=xn_fm[:, 2048:4096], in_=pd_bf[:, 0:2048]))
                V(xnT_ev)
                ev_xnfm = s_ev.n

                if is_dbg:
                    dump("xn_fm", xn_fm[:, :], [(s_ev, ev_xnfm)])
                if stages == "xn":
                    continue

                mm_proj = s_mm.n
                ev_proj_b = s_ev.n
                sc_proj_b = s_sc.n
                projs = [(Wq_sb, qd2), (Wk_sb, kd2), (Wv_sb, vd2)]

                def proj_t(t):
                    t.wait_ge(s_ev.h, ev_xnfm)
                    cnt = 0
                    for Wsb, dst in projs:
                        for jc in range(8):
                            p = pa if cnt % 2 == 0 else pb
                            if cnt >= 2:
                                t.wait_ge(s_ev.h, ev_proj_b + 2 * (cnt - 1))
                                t.wait_ge(s_sc.h, sc_proj_b + 2 * (cnt - 1))
                            ins = None
                            for kc in range(8):
                                ins = t.matmul(
                                    p[:, 0:512],
                                    Wsb[:, kc * 1024 + 128 * jc: kc * 1024 + 128 * (jc + 1)],
                                    xn_fm[:, kc * 512:(kc + 1) * 512],
                                    start=(kc == 0), stop=(kc == 7))
                            s_mm.inc(ins)
                            cnt += 1
                T(proj_t)

                def proj_ev(who, first=(b == 0 and seg == 0), sts_tot=[x.n for x in s_sts]):
                    def f(en):
                        if first:
                            en.wait_ge(s_sts[0].h, sts_tot[0])
                            en.wait_ge(s_sts[1].h, sts_tot[1])
                        cnt = 0
                        for Wsb, dst in projs:
                            for jc in range(8):
                                p = pa if cnt % 2 == 0 else pb
                                en.wait_ge(s_mm.h, mm_proj + cnt + 1)
                                cpar = who
                                for hp in range(2):
                                    src = p[64 * cpar:64 * (cpar + 1), 0:512].rearrange(
                                        "p (g two n tw) -> p g two n tw",
                                        g=8, two=2, n=4)[:, :, hp, :, :]
                                    dstv = dst[64 * hp:64 * (hp + 1), :].rearrange(
                                        "p (g n c tw) -> p g n c tw",
                                        g=8, n=4, c=16)[:, :, :, 2 * jc + cpar, :]
                                    if who == 0:
                                        ins = en.tensor_copy(out=dstv, in_=src)
                                        s_ev.inc(ins)
                                    else:
                                        ins = en.copy(out=dstv, in_=src)
                                        s_sc.inc(ins)
                                cnt += 1
                    return f
                V(proj_ev(0))
                SC(proj_ev(1))
                ev_proj = s_ev.n
                sc_proj = s_sc.n

                if is_dbg:
                    dump("qd2", qd2[:, :], [(s_ev, ev_proj), (s_sc, sc_proj)])
                    dump("kd2", kd2[:, :], [(s_ev, ev_proj), (s_sc, sc_proj)])
                    dump("vd2", vd2[:, :], [(s_ev, ev_proj), (s_sc, sc_proj)])

                mm_vpos = s_mm.n
                ev_vpos_b = s_ev.n
                sc_vpos_b = s_sc.n

                def vpos_t(t):
                    t.wait_ge(s_ev.h, ev_proj)
                    t.wait_ge(s_sc.h, sc_proj)
                    for g_ in range(8):
                        half = g_ % 2
                        if g_ >= 2:
                            t.wait_ge(s_ev.h, ev_vpos_b + (g_ - 1))
                            t.wait_ge(s_sc.h, sc_vpos_b + 4 * (g_ - 1))
                        ins = None
                        for n in range(4):
                            ins = t.transpose(
                                pc_bf[:, half * 1024 + n * 128: half * 1024 + (n + 1) * 128],
                                vd2[:, g_ * 512 + 128 * n: g_ * 512 + 128 * (n + 1)],
                                identb[:, :])
                        s_mm.inc(ins)
                T(vpos_t)

                def vpos_ev(v):
                    for g_ in range(8):
                        half = g_ % 2
                        v.wait_ge(s_mm.h, mm_vpos + g_ + 1)
                        src = pc_bf[:, half * 1024: half * 1024 + 512].rearrange(
                            "p (n hp v) -> p n hp v", n=4, hp=2)
                        dst = vposm[:, g_ * 520:(g_ + 1) * 520].rearrange(
                            "p (n hp c) -> p n hp c", n=4, hp=2)[:, :, :, 1:65]
                        s_ev.inc(v.tensor_copy(out=dst, in_=src))
                        for n in range(4):
                            src2 = pc_bf[:, half * 1024 + n * 128: half * 1024 + n * 128 + 128].rearrange(
                                "p (hp v) -> p hp v", hp=2)
                            dst2 = vs[:, g_ * 520 + n * 130: g_ * 520 + (n + 1) * 130].rearrange(
                                "p (hp c) -> p hp c", hp=2)[:, :, 1:65]
                            omg = omg130[:, :].rearrange(
                                "p (g2 hp c) -> p g2 hp c", g2=8, hp=2)[:, g_, :, 1:65]
                            s_sc.inc(v.tensor_tensor(out=dst2, in0=src2, in1=omg, op=ALU.mult))
                V(vpos_ev)
                ev_vpos = s_ev.n
                sc_vs = s_sc.n

                mm_kpos = s_mm.n
                ev_kpos_b = s_ev.n
                sc_kpos_b = s_sc.n

                def kpos_t(t):
                    t.wait_ge(s_ev.h, ev_vpos)
                    t.wait_ge(s_sc.h, sc_vs)
                    for g_ in range(8):
                        half = g_ % 2
                        if g_ >= 2:
                            t.wait_ge(s_ev.h, ev_kpos_b + 2 * (g_ - 1))
                        ins = None
                        for n in range(4):
                            ins = t.transpose(
                                pd_bf[:, half * 1024 + n * 128: half * 1024 + (n + 1) * 128],
                                kd2[:, g_ * 512 + 128 * n: g_ * 512 + 128 * (n + 1)],
                                identb[:, :])
                        s_mm.inc(ins)
                T(kpos_t)

                def kpos_v(v):
                    for g_ in range(8):
                        half = g_ % 2
                        v.wait_ge(s_mm.h, mm_kpos + g_ + 1)
                        el = elu_t[:, half * 512:(half + 1) * 512]
                        s_ev.inc(v.tensor_scalar(
                            out=el, in0=pd_bf[:, half * 1024:half * 1024 + 512],
                            scalar1=0.0, scalar2=None, op0=ALU.min))
                        v.wait_ge(s_sc.h, sc_kpos_b + g_ + 1)
                        s_ev.inc(v.scalar_tensor_tensor(
                            out=skp[:, g_ * 512:(g_ + 1) * 512],
                            in0=pd_bf[:, half * 1024:half * 1024 + 512],
                            scalar=0.0, in1=el, op0=ALU.max, op1=ALU.add))
                V(kpos_v)

                def kpos_s(s):
                    for g_ in range(8):
                        half = g_ % 2
                        el = elu_t[:, half * 512:(half + 1) * 512]
                        s.wait_ge(s_ev.h, ev_kpos_b + 2 * g_ + 1)
                        s_sc.inc(s.activation(out=el, in_=el, func=AF.Exp,
                                              bias=zero_sb[:, :], scale=1.0))
                SC(kpos_s)
                ev_kpos = s_ev.n
                sc_kpos = s_sc.n

                if is_dbg:
                    dump("vposm", vposm[:, :], [(s_ev, ev_kpos), (s_sc, sc_kpos)])
                    dump("vs", vs[:, :], [(s_ev, ev_kpos), (s_sc, sc_kpos)])
                    dump("skp", skp[:, :], [(s_ev, ev_kpos), (s_sc, sc_kpos)])

                last_ev_attO = None
                last_sc_attO = None
                attO_done = []
                ev_sq_of = {}

                def sq0_min(v):
                    s_ev.inc(v.tensor_scalar(out=elu_t[:, 0:512],
                                             in0=qd2[:, 0:512],
                                             scalar1=0.0, scalar2=None, op0=ALU.min))
                V(sq0_min)
                ev_sqmin0 = s_ev.n

                def sq0_exp(s, ev_sqmin0=ev_sqmin0):
                    s.wait_ge(s_ev.h, ev_sqmin0)
                    s_sc.inc(s.activation(out=elu_t[:, 0:512], in_=elu_t[:, 0:512],
                                          func=AF.Exp, bias=zero_sb[:, :], scale=1.0))
                SC(sq0_exp)
                sc_sqexp0 = s_sc.n

                def sq0_fin(v, sc_sqexp0=sc_sqexp0):
                    v.wait_ge(s_sc.h, sc_sqexp0)
                    s_ev.inc(v.scalar_tensor_tensor(
                        out=sq_p[:, 0:512], in0=qd2[:, 0:512],
                        scalar=0.0, in1=elu_t[:, 0:512], op0=ALU.max, op1=ALU.add))
                V(sq0_fin)
                ev_sq_of[0] = s_ev.n
                for g_ in range(8):
                    X, Y = (pc, pd) if g_ % 2 == 0 else (pa, pb)
                    X_bf = pc_bf if g_ % 2 == 0 else pa_bf
                    mm_sc_b = s_mm.n
                    sc_exp_b = s_sc.n

                    def pair_scores(t, g_=g_, first=(g_ == 0), sc_exp_b=sc_exp_b,
                                    X=X, Y=Y, attO_done=attO_done):
                        if first:
                            t.wait_ge(s_ev.h, ev_kpos)
                            t.wait_ge(s_sc.h, sc_kpos)
                        if g_ >= 2:
                            evn, scn = attO_done[g_ - 2]
                            t.wait_ge(s_ev.h, evn)
                            t.wait_ge(s_sc.h, scn)
                        for n in range(4):
                            psc = X if n % 2 == 0 else Y
                            if n >= 2:
                                t.wait_ge(s_sc.h, sc_exp_b + (n - 1))
                            ins = None
                            for hp in range(2):
                                ins = t.matmul(
                                    psc[:, hp * 512 + 128 * n: hp * 512 + 512],
                                    kd2[64 * hp:64 * (hp + 1),
                                        g_ * 512 + 128 * n: g_ * 512 + 128 * (n + 1)],
                                    qd2[64 * hp:64 * (hp + 1),
                                        g_ * 512 + 128 * n: (g_ + 1) * 512],
                                    start=True, stop=True, tile_position=(64 * hp, 0))
                            s_mm.inc(ins)
                    T(pair_scores)

                    def pair_exp(s, g_=g_, mm_sc_b=mm_sc_b, X=X, Y=Y):
                        for n in range(4):
                            psc = X if n % 2 == 0 else Y
                            s.wait_ge(s_mm.h, mm_sc_b + n + 1)
                            src = psc[:, :].rearrange("p (hp c) -> p hp c", hp=2)[:, :, 128 * n:512]
                            dst = pt[:, :].rearrange("p (hp n c) -> p hp n c",
                                                     hp=2, n=4)[:, :, n, 128 * n:512]
                            s_sc.inc(s.activation(out=dst, in_=src, func=AF.Exp,
                                                  bias=negc_sb[:, :], scale=0.125))
                    SC(pair_exp)
                    sc_exp = s_sc.n

                    def pair_mask(v, sc_exp=sc_exp):
                        v.wait_ge(s_sc.h, sc_exp)
                        ins = None
                        for hp in range(2):
                            for n in range(4):
                                base = hp * 2048 + n * 640
                                ins = v.tensor_tensor(
                                    out=pt[:, base:base + 128], in0=pt[:, base:base + 128],
                                    in1=mask8[:, 0:128], op=ALU.mult)
                        s_ev.inc(ins)
                    V(pair_mask)
                    ev_mask = s_ev.n

                    gn = g_ + 1
                    if gn < 8:
                        def pair_sq_min(v, gn=gn):
                            el = elu_t[:, (gn % 2) * 512:(gn % 2 + 1) * 512]
                            s_ev.inc(v.tensor_scalar(out=el,
                                                     in0=qd2[:, gn * 512:(gn + 1) * 512],
                                                     scalar1=0.0, scalar2=None, op0=ALU.min))
                        V(pair_sq_min)
                        ev_sqmin = s_ev.n

                        def pair_sq_exp(s, ev_sqmin=ev_sqmin, gn=gn):
                            el = elu_t[:, (gn % 2) * 512:(gn % 2 + 1) * 512]
                            s.wait_ge(s_ev.h, ev_sqmin)
                            s_sc.inc(s.activation(out=el, in_=el, func=AF.Exp,
                                                  bias=zero_sb[:, :], scale=1.0))
                        SC(pair_sq_exp)
                        sc_sqexp = s_sc.n

                        def pair_sq_fin(v, gn=gn, sc_sqexp=sc_sqexp):
                            el = elu_t[:, (gn % 2) * 512:(gn % 2 + 1) * 512]
                            v.wait_ge(s_sc.h, sc_sqexp)
                            s_ev.inc(v.scalar_tensor_tensor(
                                out=sq_p[:, (gn % 2) * 512:(gn % 2 + 1) * 512],
                                in0=qd2[:, gn * 512:(gn + 1) * 512],
                                scalar=0.0, in1=el, op0=ALU.max, op1=ALU.add))
                        V(pair_sq_fin)
                        ev_sq_of[gn] = s_ev.n
                    ev_sq = ev_sq_of[g_]

                    mm_mem_b = s_mm.n

                    def pair_mem(t, g_=g_, ev_sq=ev_sq, sc_exp=sc_exp, Y=Y):
                        t.wait_ge(s_ev.h, ev_sq)
                        t.wait_ge(s_sc.h, sc_exp)
                        for hp in range(2):
                            ins = None
                            for m in range(4):
                                ins = t.matmul(
                                    Y[:, hp * 512 + m * 65: hp * 512 + (m + 1) * 65],
                                    sq_p[64 * hp:64 * (hp + 1),
                                         (g_ % 2) * 512 + 128 * m: (g_ % 2) * 512 + 128 * (m + 1)],
                                    zmb[64 * hp:64 * (hp + 1), g_ * 65:(g_ + 1) * 65],
                                    start=True, stop=True, tile_position=(64 * hp, 0))
                            s_mm.inc(ins)
                    T(pair_mem)
                    mm_mem = s_mm.n

                    mm_dot_b = s_mm.n

                    def pair_dot(t, g_=g_, ev_mask=ev_mask, X=X):
                        t.wait_ge(s_ev.h, ev_mask)
                        for hp in range(2):
                            for m in range(4):
                                ins = None
                                for n in range(m + 1):
                                    ins = t.matmul(
                                        X[:, hp * 512 + m * 65: hp * 512 + (m + 1) * 65],
                                        pt[:, hp * 2048 + n * 512 + 128 * m:
                                           hp * 2048 + n * 512 + 128 * (m + 1)],
                                        vs[:, g_ * 520 + n * 130 + hp * 65:
                                           g_ * 520 + n * 130 + (hp + 1) * 65],
                                        start=(n == 0), stop=(n == m))
                                s_mm.inc(ins)
                    T(pair_dot)
                    mm_dot = s_mm.n

                    def pair_recip(v, mm_mem=mm_mem, mm_dot=mm_dot, X=X, Y=Y):
                        v.wait_ge(s_mm.h, mm_mem)
                        den = Y[:, :].rearrange("p (hp q) -> p hp q", hp=2)[:, :, 0:260]
                        den = den.rearrange("p hp (x c) -> p hp x c", c=65)[:, :, :, 0]
                        v.tensor_scalar(out=rd4[:, 0:8].rearrange("p (hp x) -> p hp x", hp=2),
                                        in0=den, scalar1=tiny_sb[:, :],
                                        scalar2=None, op0=ALU.add)
                        v.drain()
                        v.reciprocal(out=rd4[:, 0:8], in_=rd4[:, 0:8])
                        v.wait_ge(s_mm.h, mm_dot)
                        sden = X[:, :].rearrange("p (hp q) -> p hp q", hp=2)[:, :, 0:260]
                        sden = sden.rearrange("p hp (x c) -> p hp x c", c=65)[:, :, :, 0]
                        s_ev.inc(v.reciprocal(out=rd4[:, 8:16].rearrange("p (hp x) -> p hp x", hp=2),
                                              in_=sden))
                    V(pair_recip)
                    ev_recip = s_ev.n

                    def pair_memn(v, ev_recip=ev_recip, Y=Y):
                        v.drain()
                        ins = None
                        for hp in range(2):
                            for m in range(4):
                                x = hp * 4 + m
                                ins = v.tensor_scalar(
                                    out=memn[:, x * 64:(x + 1) * 64],
                                    in0=Y[:, hp * 512 + m * 65 + 1: hp * 512 + (m + 1) * 65],
                                    scalar1=rd4[:, x:x + 1], scalar2=None, op0=ALU.mult)
                        s_sc.inc(ins)
                    V(pair_memn)
                    sc_memn = s_sc.n

                    def pair_comb(v, sc_memn=sc_memn, X=X):
                        v.drain()
                        ins = None
                        for hp in range(2):
                            for m in range(4):
                                x = hp * 4 + m
                                ins = v.scalar_tensor_tensor(
                                    out=att_p[:, x * 64:(x + 1) * 64],
                                    in0=X[:, hp * 512 + m * 65 + 1: hp * 512 + (m + 1) * 65],
                                    scalar=rd4[:, 8 + x:9 + x],
                                    in1=memn[:, x * 64:(x + 1) * 64],
                                    op0=ALU.mult, op1=ALU.add)
                        s_ev.inc(ins)
                    V(pair_comb)
                    ev_att = s_ev.n

                    if is_dbg and g_ == 0:
                        dump("pt", pt[:, :], [(s_ev, ev_mask)])
                        dump("sq", sq_p[:, :], [(s_ev, ev_sq)])
                        dump("att", att_p[:, :], [(s_ev, ev_att)])

                    mm_mupd_b = s_mm.n

                    def pair_mupd(t, g_=g_, sc_memn=sc_memn, Y=Y):
                        t.wait_ge(s_sc.h, sc_memn)
                        for hp in range(2):
                            ins = None
                            for n in range(4):
                                ins = t.matmul(
                                    Y[64 * hp:64 * (hp + 1), 772:837],
                                    skp[:, g_ * 512 + n * 128 + hp * 64:
                                        g_ * 512 + n * 128 + (hp + 1) * 64],
                                    vposm[:, g_ * 520 + n * 130 + hp * 65:
                                          g_ * 520 + n * 130 + (hp + 1) * 65],
                                    start=(n == 0), stop=(n == 3))
                            s_mm.inc(ins)
                    T(pair_mupd)
                    mm_mupd = s_mm.n

                    def pair_state(v, g_=g_, mm_mupd=mm_mupd, Y=Y):
                        v.wait_ge(s_mm.h, mm_mupd)
                        v.tensor_tensor(out=sscr[:, 0:65], in0=Y[:, 772:837],
                                        in1=gmp[:, g_ * 65:(g_ + 1) * 65], op=ALU.mult)
                        v.drain()
                        s_ev.inc(v.tensor_tensor(out=zmf[:, g_ * 65:(g_ + 1) * 65],
                                                 in0=zmf[:, g_ * 65:(g_ + 1) * 65],
                                                 in1=sscr[:, 0:65], op=ALU.add))
                    V(pair_state)

                    mm_attT_b = s_mm.n

                    def pair_attT(t, ev_att=ev_att, X_bf=X_bf):
                        t.wait_ge(s_ev.h, ev_att)
                        ins = None
                        for hp in range(2):
                            for m in range(4):
                                x = hp * 4 + m
                                ins = t.transpose(
                                    X_bf[64 * hp:64 * (hp + 1), 1280 + m * 128: 1280 + (m + 1) * 128],
                                    att_p[:, x * 64:(x + 1) * 64],
                                    identb[:, :])
                        s_mm.inc(ins)
                    T(pair_attT)
                    mm_attT = s_mm.n

                    def pair_attO(is_v, g_=g_, mm_attT=mm_attT, X_bf=X_bf):
                        def f(en):
                            en.wait_ge(s_mm.h, mm_attT)
                            for hp in range(2):
                                h_ = 2 * g_ + hp
                                for cpar in range(2):
                                    if (hp + cpar) % 2 != (0 if is_v else 1):
                                        continue
                                    src = X_bf[64 * hp:64 * (hp + 1), 1280:1792].rearrange(
                                        "p (m a t2) -> p m a t2",
                                        m=4, a=8)[:, :, :, cpar * 8:(cpar + 1) * 8]
                                    src = src.transpose([0, 2, 1, 3])
                                    dst = atto[64 * cpar:64 * (cpar + 1), :].rearrange(
                                        "p (a i) -> p a i", a=8)[:, :, 32 * h_:32 * h_ + 32]
                                    dst = dst.rearrange("p a (m tw) -> p a m tw", m=4)
                                    ins = en.tensor_copy(out=dst, in_=src)
                                    (s_ev if is_v else s_sc).inc(ins)
                        return f
                    V(pair_attO(True))
                    V(pair_attO(False))
                    last_ev_attO = s_ev.n
                    last_sc_attO = s_sc.n
                    attO_done.append((last_ev_attO, last_sc_attO))

                if is_dbg:
                    dump("atto", atto[:, :], [(s_ev, last_ev_attO), (s_sc, last_sc_attO)])
                    dump("zmf", zmf[:, :], [(s_ev, s_ev.n)])

                mm_wo_b = s_mm.n
                ev_wo_b = s_ev.n

                def wo_t(t, a=None, b_=None):
                    t.wait_ge(s_ev.h, last_ev_attO)
                    t.wait_ge(s_sc.h, last_sc_attO)
                    for ic in range(4):
                        p = pa if ic % 2 == 0 else pb
                        if ic >= 2:
                            t.wait_ge(s_ev.h, ev_wo_b + (ic - 1))
                        ins = None
                        for half in range(2):
                            for a in range(8):
                                ins = t.matmul(
                                    p[:, half * 512:(half + 1) * 512],
                                    atto[:, a * 512 + 128 * ic: a * 512 + 128 * (ic + 1)],
                                    Wo_sb[:, a * 1024 + half * 512: a * 1024 + (half + 1) * 512],
                                    start=(a == 0), stop=(a == 7))
                        s_mm.inc(ins)
                T(wo_t)

                def wo_ev(v, mm_wo_b=mm_wo_b):
                    for ic in range(4):
                        p = pa if ic % 2 == 0 else pb
                        v.wait_ge(s_mm.h, mm_wo_b + ic + 1)
                        if skip_wo:
                            s_ev.inc(v.tensor_copy(out=rd4[:, 1:2], in_=rd4[:, 1:2]))
                        else:
                            s_ev.inc(v.tensor_tensor(out=acc[:, ic * 1024:(ic + 1) * 1024],
                                                     in0=acc[:, ic * 1024:(ic + 1) * 1024],
                                                     in1=p[:, 0:1024], op=ALU.add))
                V(wo_ev)
                ev_wo = s_ev.n

                if is_dbg:
                    dump("accw", acc[:, :], [(s_ev, ev_wo)])

                mm_f1_b = s_mm.n
                if skip_ffn:
                    ev_f2 = s_ev.n

                def f1_t(t, wf1_base=wf1_base):
                    if skip_ffn:
                        t.wait_ge(s_ev.h, ev_wo)
                        for mc in range(32):
                            sl, nn = w1_ready_at[mc]
                            t.wait_ge(s_w1s[sl].h, nn)
                        for p_ in range(2):
                            for kc in range(32):
                                sl, nn = w2_ready_at[(p_, kc)]
                                t.wait_ge(s_w2s[sl].h, nn)
                        return
                    t.wait_ge(s_ev.h, ev_wo)
                    for mc in range(32):
                        p = pa if mc % 2 == 0 else pb
                        sl, nn = w1_ready_at[mc]
                        t.wait_ge(s_w1s[sl].h, nn)
                        if mc >= 2:
                            t.wait_ge(s_wf1.h, wf1_base + (mc - 1))
                        ins = None
                        for kc in range(8):
                            ins = t.matmul(
                                p[:, 0:512],
                                w1b[:, (mc % 3) * 1024 + kc * 128: (mc % 3) * 1024 + (kc + 1) * 128],
                                xn_fm[:, kc * 512:(kc + 1) * 512],
                                start=(kc == 0), stop=(kc == 7))
                        s_mm.inc(ins)
                T(f1_t)

                def f1_ev(s, mm_f1_b=mm_f1_b):
                    if skip_ffn:
                        for mc in range(32):
                            s_wf1.inc(s.engine_nop() if False else s.copy(out=elu_t[:, 0:1], in_=elu_t[:, 0:1]))
                        return
                    for mc in range(32):
                        p = pa if mc % 2 == 0 else pb
                        s.wait_ge(s_mm.h, mm_f1_b + mc + 1)
                        s_wf1.inc(s.activation(out=hsb[:, mc * 512:(mc + 1) * 512],
                                               in_=p[:, 0:512], func=AF.Gelu,
                                               bias=b1_sb[:, mc:mc + 1]))
                SC(f1_ev)
                wf1_done = s_wf1.n

                if is_dbg:
                    dump("hsb", hsb[:, 0:2048], [(s_wf1, wf1_done)])

                mm_f2_b = s_mm.n
                ev_f2_b = s_ev.n

                def f2_t(t, wf1_done=wf1_done):
                    if skip_ffn:
                        return
                    t.wait_ge(s_wf1.h, wf1_done)
                    for p_ in range(2):
                        if p_ == 1:
                            t.wait_ge(s_ev.h, ev_f2_b + 2)
                        for kc in range(32):
                            sl, nn = w2_ready_at[(p_, kc)]
                            t.wait_ge(s_w2s[sl].h, nn)
                            ins = None
                            for ic2 in range(2):
                                ic = 2 * p_ + ic2
                                p = pa if ic2 == 0 else pb
                                for half in range(2):
                                    ins = t.matmul(
                                        p[:, half * 512:(half + 1) * 512],
                                        hsb[:, kc * 512 + 128 * ic: kc * 512 + 128 * (ic + 1)],
                                        w2b[:, ((32 * p_ + kc) % 3) * 1024 + half * 512:
                                            ((32 * p_ + kc) % 3) * 1024 + (half + 1) * 512],
                                        start=(kc == 0), stop=(kc == 31))
                            s_wf2.inc(ins)
                T(f2_t)

                def f2_ev(v, wf2_base=wf2_base):
                    if skip_ffn:
                        ins = None
                        for i_ in range(64):
                            ins = v.tensor_copy(out=rd4[:, 0:1], in_=rd4[:, 0:1])
                            s_wf2.inc(ins)
                        return
                    for p_ in range(2):
                        v.wait_ge(s_wf2.h, wf2_base + 32 * (p_ + 1))
                        for ic2 in range(2):
                            ic = 2 * p_ + ic2
                            p = pa if ic2 == 0 else pb
                            s_ev.inc(v.tensor_tensor(
                                out=acc[:, ic * 1024:(ic + 1) * 1024],
                                in0=acc[:, ic * 1024:(ic + 1) * 1024],
                                in1=p[:, 0:1024], op=ALU.add))
                V(f2_ev)
                ev_f2 = s_ev.n

                if is_dbg:
                    dump("acc", acc[:, :], [(s_ev, ev_f2)])

                def store(sy, b=b, seg=seg, ev_f2=ev_f2):
                    sy.wait_ge(s_ev.h, ev_f2)
                    for ic in range(4):
                        s_stic[ic].inc(sy.dma_start(
                            out=ys[b, seg * 512 + 128 * ic: seg * 512 + 128 * (ic + 1), :],
                            in_=acc[:, ic * 1024:(ic + 1) * 1024]), 16)
                SY(store)

        def fin(sy):
            sy.wait_ge(s_st.h, s_st.n)
            for i in range(4):
                sy.wait_ge(s_stic[i].h, s_stic[i].n)
        SY(fin)

    return nc


def host_arrange_inputs(inp, nb=4, core=0):
    x = np.ascontiguousarray(inp["L0_attn_x"][core * nb:(core + 1) * nb])
    betas = np.ascontiguousarray(inp["L0_attn_betas"].reshape(H, DV))
    m0 = inp["L0_attn_memory0"].reshape(H, DK, DV)
    z0 = inp["L0_attn_z0"].reshape(H, DK)
    m0_arr = np.zeros((128, 512), dtype=np.float32)
    z0_arr = np.zeros((128, 8), dtype=np.float32)
    for g in range(8):
        for hp in range(2):
            h = 2 * g + hp
            m0_arr[64 * hp:64 * (hp + 1), g * 64:(g + 1) * 64] = m0[h]
            z0_arr[64 * hp:64 * (hp + 1), g] = z0[h]
    return {
        "xs": x, "Wq": inp["L0_attn_Wq"], "Wk": inp["L0_attn_Wk"],
        "Wv": inp["L0_attn_Wv"], "Wo": inp["L0_attn_Wo"],
        "w1": inp["L0_ffn_w1"], "b1": inp["L0_ffn_b1"],
        "w2": inp["L0_ffn_w2"], "b2": inp["L0_ffn_b2"],
        "betas": betas, "m0_arr": m0_arr, "z0_arr": z0_arr,
    }


_NCORES = 8


def kernel(**inputs):
    from concourse.bass_utils import run_bass_kernel_spmd
    in_maps = [host_arrange_inputs(inputs, nb=4, core=c) for c in range(_NCORES)]
    nc = build(nb=4, nsegs=4)
    res = run_bass_kernel_spmd(nc, in_maps, core_ids=list(range(_NCORES)))
    out = np.concatenate([res.results[c]["ys"] for c in range(_NCORES)], axis=0)
    return out.astype(np.float32)


# version 05
# speedup vs baseline: 1.1796x; 1.0113x over previous
import numpy as np
import contextlib
import concourse.bass as bass
import concourse.mybir as mybir

F32 = mybir.dt.float32
BF16 = mybir.dt.bfloat16
AF = mybir.ActivationFunctionType
ALU = mybir.AluOpType

H, DK, DV = 16, 64, 64
D = 1024
NHID = 4096
CEXP = 12.0
EPS = 1e-5


class Sem:
    def __init__(self, nc, st, name):
        self.h = st.enter_context(nc.semaphore(name))
        self.n = 0

    def inc(self, ins, k=1):
        ins.then_inc(self.h, k)
        self.n += k
        return ins


def host_constants():
    ident = np.eye(128, dtype=np.float32)
    m = np.zeros((128, 128), dtype=np.float32)
    for c2 in range(16):
        for t2 in range(8):
            for c1 in range(16):
                for t1 in range(8):
                    m[8 * c2 + t2, 8 * c1 + t1] = 1.0 if (16 * t2 + c2) <= (16 * t1 + c1) else 0.0
    return {"ident": ident, "diagmask": m}


def build(nb=4, nsegs=4, debug=(), dbg_tile=None, skip_ffn=False, skip_wo=False, stages=None, detect=False):
    nc = bass.Bass("TRN2", target_bir_lowering=False, detect_race_conditions=detect)
    S = 2048
    if dbg_tile is None:
        dbg_tile = (0, nsegs - 1)

    xs = nc.dram_tensor("xs", [nb, S, D], F32, kind="ExternalInput").ap()
    Wq = nc.dram_tensor("Wq", [D, D], F32, kind="ExternalInput").ap()
    Wk = nc.dram_tensor("Wk", [D, D], F32, kind="ExternalInput").ap()
    Wv = nc.dram_tensor("Wv", [D, D], F32, kind="ExternalInput").ap()
    Wo = nc.dram_tensor("Wo", [D, D], F32, kind="ExternalInput").ap()
    w1 = nc.dram_tensor("w1", [D, NHID], F32, kind="ExternalInput").ap()
    b1 = nc.dram_tensor("b1", [NHID], F32, kind="ExternalInput").ap()
    w2 = nc.dram_tensor("w2", [NHID, D], F32, kind="ExternalInput").ap()
    b2 = nc.dram_tensor("b2", [D], F32, kind="ExternalInput").ap()
    betas = nc.dram_tensor("betas", [H, DV], F32, kind="ExternalInput").ap()
    m0_arr = nc.dram_tensor("m0_arr", [128, 512], F32, kind="ExternalInput").ap()
    z0_arr = nc.dram_tensor("z0_arr", [128, 8], F32, kind="ExternalInput").ap()
    ys = nc.dram_tensor("ys", [nb, S, D], F32, kind="ExternalOutput").ap()

    consts = host_constants()
    ident_dram = nc.inline_tensor(consts["ident"], "identc")
    mask_dram = nc.inline_tensor(consts["diagmask"], "maskc")

    w1_scr = nc.dram_tensor("w1_scr", [128, 32, 8, 128], BF16, kind="Internal").ap()
    w2_scr = nc.dram_tensor("w2_scr", [32, 128, D], BF16, kind="Internal").ap()
    gate_scr = nc.dram_tensor("gate_scr", [H, 65], F32, kind="Internal").ap()
    omg_scr = nc.dram_tensor("omg_scr", [H, 65], F32, kind="Internal").ap()

    dbg_specs = {
        "xn_fm": ([128, 8 * 512], BF16), "qd2": ([128, 4096], BF16),
        "kd2": ([128, 4096], BF16), "vd2": ([128, 4096], BF16),
        "vposm": ([128, 4160], BF16), "vs": ([128, 4160], BF16),
        "skp": ([128, 4096], BF16), "pt": ([128, 4096], BF16),
        "att": ([128, 512], BF16), "atto": ([128, 4096], BF16),
        "acc": ([128, 4096], F32), "accw": ([128, 4096], F32), "zmf": ([128, 520], F32),
        "sq": ([128, 512], BF16), "hsb": ([128, 2048], BF16),
        "xn_tm": ([128, 4096], BF16),
    }
    dbg_drams = {k: nc.dram_tensor("dbg_" + k, *dbg_specs[k], kind="ExternalOutput").ap()
                 for k in debug}

    with contextlib.ExitStack() as st:
        e = st.enter_context

        def sb(name, shape, dt):
            return e(nc.sbuf_tensor(name, shape, dt))

        Wq_sb = sb("Wq_sb", [128, 8 * 1024], BF16)
        Wk_sb = sb("Wk_sb", [128, 8 * 1024], BF16)
        Wv_sb = sb("Wv_sb", [128, 8 * 1024], BF16)
        Wo_sb = sb("Wo_sb", [128, 8 * 1024], BF16)
        w1b = sb("w1b", [128, 3 * 1024], BF16)
        w2b = sb("w2b", [128, 3 * 1024], BF16)
        acc = sb("acc", [128, 4 * 1024], F32)
        xn_tm = sb("xn_tm", [128, 4 * 1024], BF16)
        xn_fm = sb("xn_fm", [128, 8 * 512], BF16)
        qd2 = sb("qd2", [128, 4096], BF16)
        kd2 = sb("kd2", [128, 4096], BF16)
        vd2 = sb("vd2", [128, 4096], BF16)
        vposm = sb("vposm", [128, 4160], BF16)
        vs = sb("vs", [128, 4160], BF16)
        skp = sb("skp", [128, 4096], BF16)
        pt = vd2
        att_p = sb("att_p", [128, 512], BF16)
        memn = sb("memn", [128, 512], BF16)
        sq_p = sb("sq_p", [128, 1024], BF16)
        elu_t = sb("elu_t", [128, 1024], BF16)
        atto = xn_tm
        hsb = sb("hsb", [128, 32 * 512], BF16)
        sscr = sb("sscr", [128, 65], F32)
        zmf = sb("zmf", [128, 8 * 65], F32)
        zmb = sb("zmb", [128, 8 * 65], BF16)
        gmp = sb("gmp", [128, 8 * 65], F32)
        omg130 = sb("omg130", [128, 8 * 130], BF16)
        b2bc = sb("b2bc", [128, 1024], F32)
        b1_sb = sb("b1_sb", [128, 32], F32)
        mask8 = sb("mask8", [128, 128], BF16)
        identb = sb("identb", [128, 128], BF16)
        gstage = sb("gstage", [H, 65], F32)
        stats = sb("stats", [128, 16], F32)
        rd4 = sb("rd4", [128, 16], F32)
        eps_sb = sb("eps_sb", [128, 1], F32)
        negc_sb = sb("negc_sb", [128, 1], F32)
        tiny_sb = sb("tiny_sb", [128, 1], F32)
        zero_sb = sb("zero_sb", [128, 1], F32)
        onesb = sb("onesb", [1, 1], F32)

        class _WStage:
            def __getitem__(self, sl):
                return None
        def wslice(kc, width):
            buf = qd2 if kc % 2 == 0 else kd2
            return buf[:, 0:width]

        pa = e(nc.psum_tensor("pa", [128, 1024], F32))
        pb = e(nc.psum_tensor("pb", [128, 1024], F32))
        pc = e(nc.psum_tensor("pc", [128, 1024], F32))
        pd = e(nc.psum_tensor("pd", [128, 1024], F32))
        pc_bf = pc.ap().bitcast(BF16)
        pd_bf = pd.ap().bitcast(BF16)
        pa_bf = pa.ap().bitcast(BF16)

        s_ld = Sem(nc, st, "s_ld")
        s_ldx = Sem(nc, st, "s_ldx")
        s_state = Sem(nc, st, "s_state")
        s_w1s = [Sem(nc, st, f"s_w1s{i}") for i in range(3)]
        s_w2s = [Sem(nc, st, f"s_w2s{i}") for i in range(3)]
        s_stg = [Sem(nc, st, f"s_stg{i}") for i in range(2)]
        s_sts = [Sem(nc, st, f"s_sts{i}") for i in range(2)]
        s_wf1 = Sem(nc, st, "s_wf1")
        s_wf2 = Sem(nc, st, "s_wf2")
        s_mm = Sem(nc, st, "s_mm")
        s_ev = Sem(nc, st, "s_ev")
        s_sc = Sem(nc, st, "s_sc")
        s_st = Sem(nc, st, "s_st")
        s_stic = [Sem(nc, st, f"s_stic{i}") for i in range(4)]
        block = e(nc.Block())

        T, V, SC, SY, G = block.tensor, block.vector, block.scalar, block.sync, block.gpsimd

        def dump(name, ap_src, waits):
            if name not in dbg_drams:
                return
            dst = dbg_drams[name]

            def _f(sy, dst=dst, ap_src=ap_src, waits=tuple(waits)):
                for sem, n in waits:
                    sy.wait_ge(sem.h, n)
                s_st.inc(sy.dma_start(out=dst[:, :], in_=ap_src), 16)
            SY(_f)

        def p0_gpsimd(g):
            for W, Wsb in ((Wq, Wq_sb), (Wk, Wk_sb), (Wv, Wv_sb), (Wo, Wo_sb)):
                for kc in range(8):
                    s_ld.inc(g.dma_start(out=Wsb[:, kc * 1024:(kc + 1) * 1024],
                                         in_=W[128 * kc:128 * (kc + 1), :]), 16)
            s_ld.inc(g.dma_start(out=identb[:, :], in_=ident_dram.ap()[:, :]), 16)
            s_ld.inc(g.dma_start(out=mask8[:, 0:128], in_=mask_dram.ap()[:, :]), 16)
            s_ld.inc(g.dma_start(out=b2bc[:, :],
                                 in_=b2.unsqueeze(0).partition_broadcast(128)), 16)
            s_ld.inc(g.dma_start(out=gstage[:, 1:65], in_=betas[:, :]), 16)
            s_ld.inc(g.dma_start(out=acc[0:1, 0:4096], in_=b1.unsqueeze(0)), 16)
        G(p0_gpsimd)
        ld0 = s_ld.n

        def p0_vector(v):
            v.memset(eps_sb[:, :], EPS)
            v.memset(negc_sb[:, :], -CEXP)
            v.memset(tiny_sb[:, :], 1e-35)
            v.memset(zero_sb[:, :], 0.0)
            v.memset(onesb[:, :], 1.0)
            v.memset(gstage[:, 0:1], 1.0)
            v.memset(vposm[:, :].rearrange("p (b c) -> p b c", c=65)[:, :, 0], 1.0)
            v.memset(vs[:, :].rearrange("p (b c) -> p b c", c=65)[:, :, 0], 1.0)
            v.wait_ge(s_ld.h, ld0)
            s_ev.inc(v.memset(rd4[:, 0:1], 0.0))
        V(p0_vector)
        ev0 = s_ev.n

        def p0_scalar(s):
            s.wait_ge(s_ld.h, ld0)
            s.wait_ge(s_ev.h, ev0)
            s_sc.inc(s.activation(out=gstage[:, 1:65], in_=gstage[:, 1:65], func=AF.Sigmoid))
        SC(p0_scalar)
        sc_gate = s_sc.n

        def p0_vec2(v):
            v.wait_ge(s_sc.h, sc_gate)
            v.tensor_scalar(out=sscr[0:H, 0:65], in0=gstage[:, 0:65],
                            scalar1=1.0, scalar2=-1.0, op0=ALU.subtract, op1=ALU.mult)
            v.drain()
            s_ev.inc(v.memset(sscr[0:H, 0:1], 1.0))
        V(p0_vec2)
        ev_omg = s_ev.n

        def p0_b1_t(t):
            t.wait_ge(s_ld.h, ld0)
            t.wait_ge(s_ev.h, ev0)
            ins = None
            for mc in range(32):
                ins = t.matmul(pa[0:128, mc:mc + 1], acc[0:1, 128 * mc:128 * (mc + 1)],
                               onesb[0:1, 0:1], start=True, stop=True)
            s_mm.inc(ins)
        T(p0_b1_t)
        mm_b1 = s_mm.n

        def p0_b1_v(v):
            v.wait_ge(s_mm.h, mm_b1)
            s_ev.inc(v.tensor_copy(out=b1_sb[:, :], in_=pa[:, 0:32]))
        V(p0_b1_v)
        ev_b1 = s_ev.n

        def p0_gates_sy(sy):
            sy.wait_ge(s_sc.h, sc_gate)
            s_st.inc(sy.dma_start(out=gate_scr[:, :], in_=gstage[:, :]), 16)
            sy.wait_ge(s_ev.h, ev_omg)
            s_st.inc(sy.dma_start(out=omg_scr[:, :], in_=sscr[0:H, 0:65]), 16)
            sy.wait_ge(s_st.h, s_st.n)
            for g_ in range(8):
                for hp in range(2):
                    s_state.inc(sy.dma_start(
                        out=gmp[64 * hp:64 * (hp + 1), g_ * 65:(g_ + 1) * 65],
                        in_=gate_scr[2 * g_ + hp:2 * g_ + hp + 1, :].partition_broadcast(64)), 16)
        SY(p0_gates_sy)

        def p0_omg_g(g):
            g.wait_ge(s_st.h, s_st.n)
            for g_ in range(8):
                for hp in range(2):
                    s_ld.inc(g.dma_start(
                        out=omg130[:, (g_ * 2 + hp) * 65:(g_ * 2 + hp + 1) * 65],
                        in_=omg_scr[2 * g_ + hp:2 * g_ + hp + 1, :].partition_broadcast(128)), 16)
        G(p0_omg_g)

        def p0_w1_g(g):
            for kc in range(8):
                if kc >= 2:
                    g.wait_ge(s_sts[kc % 2].h, 16 * (kc // 2))
                s_stg[kc % 2].inc(g.dma_start(
                    out=wslice(kc, 4096),
                    in_=w1[128 * kc:128 * (kc + 1), :]), 16)
        G(p0_w1_g)

        def p0_w1_sy(sy):
            for kc in range(8):
                sy.wait_ge(s_stg[kc % 2].h, 16 * (kc // 2 + 1))
                src = wslice(kc, 4096).rearrange("p (mc c) -> p mc c", mc=32)
                dst = w1_scr[:, :, kc, :]
                s_sts[kc % 2].inc(sy.dma_start(out=dst, in_=src), 16)
        SY(p0_w1_sy)

        w1_sts_base = [s.n for s in s_sts]
        w1_stg_base = [s.n for s in s_stg]

        def p0_w2_g(g):
            for kc in range(32):
                if kc == 0:
                    g.wait_ge(s_sts[0].h, w1_sts_base[0])
                    g.wait_ge(s_sts[1].h, w1_sts_base[1])
                if kc >= 2:
                    g.wait_ge(s_sts[kc % 2].h, w1_sts_base[kc % 2] + 16 * (kc // 2))
                s_stg[kc % 2].inc(g.dma_start(
                    out=wslice(kc, 1024),
                    in_=w2[128 * kc:128 * (kc + 1), :]), 16)
        G(p0_w2_g)

        def p0_w2_sy(sy):
            for kc in range(32):
                sy.wait_ge(s_stg[kc % 2].h, w1_stg_base[kc % 2] + 16 * (kc // 2 + 1))
                s_sts[kc % 2].inc(sy.dma_start(
                    out=w2_scr[kc, :, :],
                    in_=wslice(kc, 1024)), 16)
        SY(p0_w2_sy)

        P0_LD = s_ld.n
        P0_ST = s_st.n
        P0_MM = s_mm.n

        for b in range(nb):
            if stages == "xn":
                pass
            def init_state_sy(sy, b=b, mmn=s_mm.n, evn=s_ev.n):
                sy.wait_ge(s_st.h, P0_ST)
                if b > 0:
                    sy.wait_ge(s_mm.h, mmn)
                    sy.wait_ge(s_ev.h, evn)
                s_state.inc(sy.dma_start(
                    out=zmf[:, :].rearrange("p (g c) -> p g c", g=8)[:, :, 0],
                    in_=z0_arr[:, :], allow_slow_non_contiguous=True), 16)
                s_state.inc(sy.dma_start(
                    out=zmf[:, :].rearrange("p (g c) -> p g c", g=8)[:, :, 1:65],
                    in_=m0_arr[:, :].rearrange("p (g v) -> p g v", g=8)), 16)
            SY(init_state_sy)
            ld_state = s_state.n

            def init_state_v(v, b=b):
                v.wait_ge(s_state.h, ld_state)
                if b == 0:
                    v.wait_ge(s_ld.h, P0_LD)
                s_ev.inc(v.tensor_copy(out=zmb[:, :], in_=zmf[:, :]))
            V(init_state_v)
            ev_state0 = s_ev.n

            for seg in range(nsegs):
                is_dbg = (b, seg) == dbg_tile and dbg_drams

                def x_load(sy, b=b, seg=seg, stic=[x.n for x in s_stic], mmn=P0_MM):
                    if b == 0 and seg == 0:
                        sy.wait_ge(s_mm.h, mmn)
                    for ic in range(4):
                        sy.wait_ge(s_stic[ic].h, stic[ic])
                        s_ldx.inc(sy.dma_start(
                            out=acc[:, ic * 1024:(ic + 1) * 1024],
                            in_=xs[b, seg * 512 + 128 * ic: seg * 512 + 128 * (ic + 1), :]), 16)
                SY(x_load)
                ld_x = s_ldx.n

                wf1_base = s_wf1.n
                wf2_base = s_wf2.n
                w1_ready_at = {}
                w2_ready_at = {}

                def w_prefetch(sy, w1_ready_at=w1_ready_at, w2_ready_at=w2_ready_at):
                    for i in range(2):
                        sy.wait_ge(s_sts[i].h, s_sts[i].n)
                    for mc in range(32):
                        need = wf1_base + mc - 2
                        if need > 0:
                            sy.wait_ge(s_wf1.h, need)
                        slot = mc % 3
                        ins = sy.dma_start(
                            out=w1b[:, slot * 1024:(slot + 1) * 1024],
                            in_=w1_scr[:, mc, :, :])
                        s_w1s[slot].inc(ins, 16)
                        w1_ready_at[mc] = (slot, s_w1s[slot].n)
                    for p_ in range(2):
                        for kc in range(32):
                            need = wf2_base + 32 * p_ + kc - 2
                            if need > 0:
                                sy.wait_ge(s_wf2.h, need)
                            slot = (32 * p_ + kc) % 3
                            ins = sy.dma_start(
                                out=w2b[:, slot * 1024:(slot + 1) * 1024],
                                in_=w2_scr[kc, :, :])
                            s_w2s[slot].inc(ins, 16)
                            w2_ready_at[(p_, kc)] = (slot, s_w2s[slot].n)
                if stages != "xn":
                    SY(w_prefetch)

                if seg > 0:
                    def zmb_cast(v):
                        v.drain()
                        s_ev.inc(v.tensor_copy(out=zmb[:, :], in_=zmf[:, :]))
                    V(zmb_cast)

                def ln_v(v):
                    v.wait_ge(s_ldx.h, ld_x)
                    ins = None
                    for ic in range(4):
                        ins = v.reduce_sum(out=stats[:, ic:ic + 1],
                                           in_=acc[:, ic * 1024:(ic + 1) * 1024],
                                           axis=mybir.AxisListType.X)
                    s_ev.inc(ins)
                V(ln_v)
                ev_ln1 = s_ev.n

                def ln_s(s):
                    s.wait_ge(s_ldx.h, ld_x)
                    ins = None
                    for ic in range(4):
                        ins = s.activation(out=hsb[:, 0:1024], in_=acc[:, ic * 1024:(ic + 1) * 1024],
                                           func=AF.Square, accum_out=stats[:, 4 + ic:5 + ic])
                    s_sc.inc(ins)
                SC(ln_s)
                sc_ln = s_sc.n

                def ln_v2(v):
                    v.wait_ge(s_sc.h, sc_ln)
                    v.wait_ge(s_ev.h, ev_ln1)
                    v.tensor_scalar(out=stats[:, 8:12], in0=stats[:, 0:4], scalar1=1.0 / D,
                                    scalar2=None, op0=ALU.mult)
                    v.tensor_scalar(out=stats[:, 4:8], in0=stats[:, 4:8], scalar1=1.0 / D,
                                    scalar2=None, op0=ALU.mult)
                    v.drain()
                    v.tensor_tensor(out=stats[:, 12:16], in0=stats[:, 8:12],
                                    in1=stats[:, 8:12], op=ALU.mult)
                    v.drain()
                    s_ev.inc(v.tensor_tensor(out=stats[:, 12:16], in0=stats[:, 4:8],
                                             in1=stats[:, 12:16], op=ALU.subtract))
                V(ln_v2)
                ev_var = s_ev.n

                def ln_s2(s):
                    s.wait_ge(s_ev.h, ev_var)
                    s.activation(out=stats[:, 12:16], in_=stats[:, 12:16],
                                 func=AF.Ln, bias=eps_sb[:, :])
                    s.drain()
                    s_sc.inc(s.activation(out=stats[:, 12:16], in_=stats[:, 12:16],
                                          func=AF.Exp, scale=-0.5))
                SC(ln_s2)
                sc_rstd = s_sc.n

                def ln_v3(v):
                    v.wait_ge(s_sc.h, sc_rstd)
                    ins = None
                    for ic in range(4):
                        ins = v.tensor_scalar(
                            out=xn_tm[:, ic * 1024:(ic + 1) * 1024],
                            in0=acc[:, ic * 1024:(ic + 1) * 1024],
                            scalar1=stats[:, 8 + ic:9 + ic], scalar2=stats[:, 12 + ic:13 + ic],
                            op0=ALU.subtract, op1=ALU.mult)
                        s_ev.inc(ins)
                V(ln_v3)
                ev_xn = s_ev.n

                def ln_b2(v):
                    v.drain()
                    ins = None
                    for ic in range(4):
                        ins = v.tensor_tensor(out=acc[:, ic * 1024:(ic + 1) * 1024],
                                              in0=acc[:, ic * 1024:(ic + 1) * 1024],
                                              in1=b2bc[:, :], op=ALU.add)
                    s_ev.inc(ins)
                V(ln_b2)
                ev_b2 = s_ev.n

                if is_dbg:
                    dump("xn_tm", xn_tm[:, :], [(s_ev, ev_xn)])

                mm_xnT = s_mm.n
                ev_xnT = s_ev.n

                sc_now = s_sc.n

                def xnT(t, sc_now=sc_now):
                    t.wait_ge(s_ev.h, ev_xn)
                    t.wait_ge(s_sc.h, sc_now)
                    t.wait_ge(s_ld.h, P0_LD)
                    for jc in range(8):
                        pb_t = pc_bf if jc < 4 else pd_bf
                        ins = None
                        for ic in range(4):
                            ins = t.transpose(
                                pb_t[:, (jc % 4) * 512 + ic * 128: (jc % 4) * 512 + (ic + 1) * 128],
                                xn_tm[:, ic * 1024 + 128 * jc: ic * 1024 + 128 * (jc + 1)],
                                identb[:, :])
                        s_mm.inc(ins)
                T(xnT)

                def xnT_ev(v):
                    v.wait_ge(s_mm.h, mm_xnT + 4)
                    s_ev.inc(v.tensor_copy(out=xn_fm[:, 0:2048], in_=pc_bf[:, 0:2048]))
                    v.wait_ge(s_mm.h, mm_xnT + 8)
                    s_ev.inc(v.tensor_copy(out=xn_fm[:, 2048:4096], in_=pd_bf[:, 0:2048]))
                V(xnT_ev)
                ev_xnfm = s_ev.n

                if is_dbg:
                    dump("xn_fm", xn_fm[:, :], [(s_ev, ev_xnfm)])
                if stages == "xn":
                    continue

                mm_proj = s_mm.n
                ev_proj_b = s_ev.n
                sc_proj_b = s_sc.n
                projs = [(Wv_sb, vd2), (Wk_sb, kd2), (Wq_sb, qd2)]

                def proj_t(t):
                    t.wait_ge(s_ev.h, ev_xnfm)
                    cnt = 0
                    for Wsb, dst in projs:
                        for jc in range(8):
                            p = pa if (cnt % 4) < 2 else pb
                            off = 512 * (cnt % 2)
                            if cnt >= 4:
                                t.wait_ge(s_ev.h, ev_proj_b + 2 * (cnt - 3))
                                t.wait_ge(s_sc.h, sc_proj_b + 2 * (cnt - 3))
                            ins = None
                            for kc in range(8):
                                ins = t.matmul(
                                    p[:, off:off + 512],
                                    Wsb[:, kc * 1024 + 128 * jc: kc * 1024 + 128 * (jc + 1)],
                                    xn_fm[:, kc * 512:(kc + 1) * 512],
                                    start=(kc == 0), stop=(kc == 7))
                            s_mm.inc(ins)
                            cnt += 1
                T(proj_t)

                def proj_ev(who, first=(b == 0 and seg == 0), sts_tot=[x.n for x in s_sts]):
                    def f(en):
                        if first:
                            en.wait_ge(s_sts[0].h, sts_tot[0])
                            en.wait_ge(s_sts[1].h, sts_tot[1])
                        cnt = 0
                        for Wsb, dst in projs:
                            for jc in range(8):
                                p = pa if (cnt % 4) < 2 else pb
                                off = 512 * (cnt % 2)
                                en.wait_ge(s_mm.h, mm_proj + cnt + 1)
                                cpar = who
                                for hp in range(2):
                                    src = p[64 * cpar:64 * (cpar + 1), off:off + 512].rearrange(
                                        "p (g two n tw) -> p g two n tw",
                                        g=8, two=2, n=4)[:, :, hp, :, :]
                                    dstv = dst[64 * hp:64 * (hp + 1), :].rearrange(
                                        "p (g n c tw) -> p g n c tw",
                                        g=8, n=4, c=16)[:, :, :, 2 * jc + cpar, :]
                                    if who == 0:
                                        ins = en.tensor_copy(out=dstv, in_=src)
                                        s_ev.inc(ins)
                                    else:
                                        ins = en.copy(out=dstv, in_=src)
                                        s_sc.inc(ins)
                                cnt += 1
                    return f
                V(proj_ev(0))
                SC(proj_ev(1))
                ev_proj = s_ev.n
                sc_proj = s_sc.n

                if is_dbg:
                    dump("qd2", qd2[:, :], [(s_ev, ev_proj), (s_sc, sc_proj)])
                    dump("kd2", kd2[:, :], [(s_ev, ev_proj), (s_sc, sc_proj)])
                    dump("vd2", vd2[:, :], [(s_ev, ev_proj), (s_sc, sc_proj)])

                mm_vpos = s_mm.n
                ev_vpos_b = s_ev.n
                sc_vpos_b = s_sc.n

                def vpos_t(t):
                    t.wait_ge(s_ev.h, ev_proj_b + 16)
                    t.wait_ge(s_sc.h, sc_proj_b + 16)
                    for g_ in range(8):
                        half = g_ % 2
                        if g_ >= 2:
                            t.wait_ge(s_ev.h, ev_vpos_b + (g_ - 1))
                            t.wait_ge(s_sc.h, sc_vpos_b + 4 * (g_ - 1))
                        ins = None
                        for n in range(4):
                            ins = t.transpose(
                                pc_bf[:, half * 1024 + n * 128: half * 1024 + (n + 1) * 128],
                                vd2[:, g_ * 512 + 128 * n: g_ * 512 + 128 * (n + 1)],
                                identb[:, :])
                        s_mm.inc(ins)
                T(vpos_t)

                def vpos_ev(v):
                    for g_ in range(8):
                        half = g_ % 2
                        v.wait_ge(s_mm.h, mm_vpos + g_ + 1)
                        src = pc_bf[:, half * 1024: half * 1024 + 512].rearrange(
                            "p (n hp v) -> p n hp v", n=4, hp=2)
                        dst = vposm[:, g_ * 520:(g_ + 1) * 520].rearrange(
                            "p (n hp c) -> p n hp c", n=4, hp=2)[:, :, :, 1:65]
                        s_ev.inc(v.tensor_copy(out=dst, in_=src))
                        for n in range(4):
                            src2 = pc_bf[:, half * 1024 + n * 128: half * 1024 + n * 128 + 128].rearrange(
                                "p (hp v) -> p hp v", hp=2)
                            dst2 = vs[:, g_ * 520 + n * 130: g_ * 520 + (n + 1) * 130].rearrange(
                                "p (hp c) -> p hp c", hp=2)[:, :, 1:65]
                            omg = omg130[:, :].rearrange(
                                "p (g2 hp c) -> p g2 hp c", g2=8, hp=2)[:, g_, :, 1:65]
                            s_sc.inc(v.tensor_tensor(out=dst2, in0=src2, in1=omg, op=ALU.mult))
                V(vpos_ev)
                ev_vpos = s_ev.n
                sc_vs = s_sc.n

                mm_kpos = s_mm.n
                ev_kpos_b = s_ev.n
                sc_kpos_b = s_sc.n

                def kpos_t(t):
                    t.wait_ge(s_ev.h, max(ev_vpos, ev_proj_b + 32))
                    t.wait_ge(s_sc.h, sc_proj_b + 32)
                    t.wait_ge(s_sc.h, sc_vs)
                    for g_ in range(8):
                        half = g_ % 2
                        if g_ >= 2:
                            t.wait_ge(s_ev.h, ev_kpos_b + 2 * (g_ - 1))
                        ins = None
                        for n in range(4):
                            ins = t.transpose(
                                pd_bf[:, half * 1024 + n * 128: half * 1024 + (n + 1) * 128],
                                kd2[:, g_ * 512 + 128 * n: g_ * 512 + 128 * (n + 1)],
                                identb[:, :])
                        s_mm.inc(ins)
                T(kpos_t)

                def kpos_v(v):
                    for g_ in range(8):
                        half = g_ % 2
                        v.wait_ge(s_mm.h, mm_kpos + g_ + 1)
                        el = elu_t[:, half * 512:(half + 1) * 512]
                        s_ev.inc(v.tensor_scalar(
                            out=el, in0=pd_bf[:, half * 1024:half * 1024 + 512],
                            scalar1=0.0, scalar2=None, op0=ALU.min))
                        v.wait_ge(s_sc.h, sc_kpos_b + g_ + 1)
                        s_ev.inc(v.scalar_tensor_tensor(
                            out=skp[:, g_ * 512:(g_ + 1) * 512],
                            in0=pd_bf[:, half * 1024:half * 1024 + 512],
                            scalar=0.0, in1=el, op0=ALU.max, op1=ALU.add))
                V(kpos_v)

                def kpos_s(s):
                    for g_ in range(8):
                        half = g_ % 2
                        el = elu_t[:, half * 512:(half + 1) * 512]
                        s.wait_ge(s_ev.h, ev_kpos_b + 2 * g_ + 1)
                        s_sc.inc(s.activation(out=el, in_=el, func=AF.Exp,
                                              bias=zero_sb[:, :], scale=1.0))
                SC(kpos_s)
                ev_kpos = s_ev.n
                sc_kpos = s_sc.n

                if is_dbg:
                    dump("vposm", vposm[:, :], [(s_ev, ev_kpos), (s_sc, sc_kpos)])
                    dump("vs", vs[:, :], [(s_ev, ev_kpos), (s_sc, sc_kpos)])
                    dump("skp", skp[:, :], [(s_ev, ev_kpos), (s_sc, sc_kpos)])

                last_ev_attO = None
                last_sc_attO = None
                attO_done = []
                ev_sq_of = {}

                def sq0_min(v):
                    s_ev.inc(v.tensor_scalar(out=elu_t[:, 0:512],
                                             in0=qd2[:, 0:512],
                                             scalar1=0.0, scalar2=None, op0=ALU.min))
                V(sq0_min)
                ev_sqmin0 = s_ev.n

                def sq0_exp(s, ev_sqmin0=ev_sqmin0):
                    s.wait_ge(s_ev.h, ev_sqmin0)
                    s_sc.inc(s.activation(out=elu_t[:, 0:512], in_=elu_t[:, 0:512],
                                          func=AF.Exp, bias=zero_sb[:, :], scale=1.0))
                SC(sq0_exp)
                sc_sqexp0 = s_sc.n

                def sq0_fin(v, sc_sqexp0=sc_sqexp0):
                    v.wait_ge(s_sc.h, sc_sqexp0)
                    s_ev.inc(v.scalar_tensor_tensor(
                        out=sq_p[:, 0:512], in0=qd2[:, 0:512],
                        scalar=0.0, in1=elu_t[:, 0:512], op0=ALU.max, op1=ALU.add))
                V(sq0_fin)
                ev_sq_of[0] = s_ev.n
                for g_ in range(8):
                    X, Y = (pc, pd) if g_ % 2 == 0 else (pa, pb)
                    X_bf = pc_bf if g_ % 2 == 0 else pa_bf
                    mm_sc_b = s_mm.n
                    sc_exp_b = s_sc.n

                    def pair_scores(t, g_=g_, first=(g_ == 0), sc_exp_b=sc_exp_b,
                                    X=X, Y=Y, attO_done=attO_done):
                        if first:
                            t.wait_ge(s_ev.h, ev_kpos)
                            t.wait_ge(s_sc.h, sc_kpos)
                        if g_ >= 2:
                            evn, scn = attO_done[g_ - 2]
                            t.wait_ge(s_ev.h, evn)
                            t.wait_ge(s_sc.h, scn)
                        for n in range(4):
                            psc = X if n % 2 == 0 else Y
                            if n >= 2:
                                t.wait_ge(s_sc.h, sc_exp_b + (n - 1))
                            ins = None
                            for hp in range(2):
                                ins = t.matmul(
                                    psc[:, hp * 512 + 128 * n: hp * 512 + 512],
                                    kd2[64 * hp:64 * (hp + 1),
                                        g_ * 512 + 128 * n: g_ * 512 + 128 * (n + 1)],
                                    qd2[64 * hp:64 * (hp + 1),
                                        g_ * 512 + 128 * n: (g_ + 1) * 512],
                                    start=True, stop=True, tile_position=(64 * hp, 0))
                            s_mm.inc(ins)
                    T(pair_scores)

                    def pair_exp(s, g_=g_, mm_sc_b=mm_sc_b, X=X, Y=Y):
                        for n in range(4):
                            psc = X if n % 2 == 0 else Y
                            s.wait_ge(s_mm.h, mm_sc_b + n + 1)
                            src = psc[:, :].rearrange("p (hp c) -> p hp c", hp=2)[:, :, 128 * n:512]
                            dst = pt[:, :].rearrange("p (hp n c) -> p hp n c",
                                                     hp=2, n=4)[:, :, n, 128 * n:512]
                            s_sc.inc(s.activation(out=dst, in_=src, func=AF.Exp,
                                                  bias=negc_sb[:, :], scale=0.125))
                    SC(pair_exp)
                    sc_exp = s_sc.n

                    def pair_mask(v, sc_exp=sc_exp):
                        v.wait_ge(s_sc.h, sc_exp)
                        ins = None
                        for hp in range(2):
                            for n in range(4):
                                base = hp * 2048 + n * 640
                                ins = v.tensor_tensor(
                                    out=pt[:, base:base + 128], in0=pt[:, base:base + 128],
                                    in1=mask8[:, 0:128], op=ALU.mult)
                        s_ev.inc(ins)
                    V(pair_mask)
                    ev_mask = s_ev.n

                    gn = g_ + 1
                    if gn < 8:
                        def pair_sq_min(v, gn=gn):
                            el = elu_t[:, (gn % 2) * 512:(gn % 2 + 1) * 512]
                            s_ev.inc(v.tensor_scalar(out=el,
                                                     in0=qd2[:, gn * 512:(gn + 1) * 512],
                                                     scalar1=0.0, scalar2=None, op0=ALU.min))
                        V(pair_sq_min)
                        ev_sqmin = s_ev.n

                        def pair_sq_exp(s, ev_sqmin=ev_sqmin, gn=gn):
                            el = elu_t[:, (gn % 2) * 512:(gn % 2 + 1) * 512]
                            s.wait_ge(s_ev.h, ev_sqmin)
                            s_sc.inc(s.activation(out=el, in_=el, func=AF.Exp,
                                                  bias=zero_sb[:, :], scale=1.0))
                        SC(pair_sq_exp)
                        sc_sqexp = s_sc.n

                        def pair_sq_fin(v, gn=gn, sc_sqexp=sc_sqexp):
                            el = elu_t[:, (gn % 2) * 512:(gn % 2 + 1) * 512]
                            v.wait_ge(s_sc.h, sc_sqexp)
                            s_ev.inc(v.scalar_tensor_tensor(
                                out=sq_p[:, (gn % 2) * 512:(gn % 2 + 1) * 512],
                                in0=qd2[:, gn * 512:(gn + 1) * 512],
                                scalar=0.0, in1=el, op0=ALU.max, op1=ALU.add))
                        V(pair_sq_fin)
                        ev_sq_of[gn] = s_ev.n
                    ev_sq = ev_sq_of[g_]

                    mm_mem_b = s_mm.n

                    def pair_mem(t, g_=g_, ev_sq=ev_sq, sc_exp=sc_exp, Y=Y):
                        t.wait_ge(s_ev.h, ev_sq)
                        t.wait_ge(s_sc.h, sc_exp)
                        for hp in range(2):
                            ins = None
                            for m in range(4):
                                ins = t.matmul(
                                    Y[:, hp * 512 + m * 65: hp * 512 + (m + 1) * 65],
                                    sq_p[64 * hp:64 * (hp + 1),
                                         (g_ % 2) * 512 + 128 * m: (g_ % 2) * 512 + 128 * (m + 1)],
                                    zmb[64 * hp:64 * (hp + 1), g_ * 65:(g_ + 1) * 65],
                                    start=True, stop=True, tile_position=(64 * hp, 0))
                            s_mm.inc(ins)
                    T(pair_mem)
                    mm_mem = s_mm.n

                    def pair_mupd(t, g_=g_, Y=Y):
                        for hp in range(2):
                            ins = None
                            for n in range(4):
                                ins = t.matmul(
                                    Y[64 * hp:64 * (hp + 1), 772:837],
                                    skp[:, g_ * 512 + n * 128 + hp * 64:
                                        g_ * 512 + n * 128 + (hp + 1) * 64],
                                    vposm[:, g_ * 520 + n * 130 + hp * 65:
                                          g_ * 520 + n * 130 + (hp + 1) * 65],
                                    start=(n == 0), stop=(n == 3))
                            s_mm.inc(ins)
                    T(pair_mupd)
                    mm_mupd = s_mm.n

                    mm_dot_b = s_mm.n

                    def pair_dot(t, g_=g_, ev_mask=ev_mask, X=X):
                        t.wait_ge(s_ev.h, ev_mask)
                        for hp in range(2):
                            for m in range(4):
                                ins = None
                                for n in range(m + 1):
                                    ins = t.matmul(
                                        X[:, hp * 512 + m * 65: hp * 512 + (m + 1) * 65],
                                        pt[:, hp * 2048 + n * 512 + 128 * m:
                                           hp * 2048 + n * 512 + 128 * (m + 1)],
                                        vs[:, g_ * 520 + n * 130 + hp * 65:
                                           g_ * 520 + n * 130 + (hp + 1) * 65],
                                        start=(n == 0), stop=(n == m))
                                s_mm.inc(ins)
                    T(pair_dot)
                    mm_dot = s_mm.n

                    def pair_recip(v, mm_mem=mm_mem, mm_dot=mm_dot, X=X, Y=Y):
                        v.wait_ge(s_mm.h, mm_mem)
                        den = Y[:, :].rearrange("p (hp q) -> p hp q", hp=2)[:, :, 0:260]
                        den = den.rearrange("p hp (x c) -> p hp x c", c=65)[:, :, :, 0]
                        v.tensor_scalar(out=rd4[:, 0:8].rearrange("p (hp x) -> p hp x", hp=2),
                                        in0=den, scalar1=tiny_sb[:, :],
                                        scalar2=None, op0=ALU.add)
                        v.drain()
                        v.reciprocal(out=rd4[:, 0:8], in_=rd4[:, 0:8])
                        v.wait_ge(s_mm.h, mm_dot)
                        sden = X[:, :].rearrange("p (hp q) -> p hp q", hp=2)[:, :, 0:260]
                        sden = sden.rearrange("p hp (x c) -> p hp x c", c=65)[:, :, :, 0]
                        s_ev.inc(v.reciprocal(out=rd4[:, 8:16].rearrange("p (hp x) -> p hp x", hp=2),
                                              in_=sden))
                    V(pair_recip)
                    ev_recip = s_ev.n

                    def pair_memn(v, ev_recip=ev_recip, Y=Y):
                        v.drain()
                        ins = None
                        for hp in range(2):
                            for m in range(4):
                                x = hp * 4 + m
                                ins = v.tensor_scalar(
                                    out=memn[:, x * 64:(x + 1) * 64],
                                    in0=Y[:, hp * 512 + m * 65 + 1: hp * 512 + (m + 1) * 65],
                                    scalar1=rd4[:, x:x + 1], scalar2=None, op0=ALU.mult)
                        s_sc.inc(ins)
                    V(pair_memn)
                    sc_memn = s_sc.n

                    def pair_comb(v, sc_memn=sc_memn, X=X):
                        v.drain()
                        ins = None
                        for hp in range(2):
                            for m in range(4):
                                x = hp * 4 + m
                                ins = v.scalar_tensor_tensor(
                                    out=att_p[:, x * 64:(x + 1) * 64],
                                    in0=X[:, hp * 512 + m * 65 + 1: hp * 512 + (m + 1) * 65],
                                    scalar=rd4[:, 8 + x:9 + x],
                                    in1=memn[:, x * 64:(x + 1) * 64],
                                    op0=ALU.mult, op1=ALU.add)
                        s_ev.inc(ins)
                    V(pair_comb)
                    ev_att = s_ev.n

                    if is_dbg and g_ == 0:
                        dump("pt", pt[:, :], [(s_ev, ev_mask)])
                        dump("sq", sq_p[:, :], [(s_ev, ev_sq)])
                        dump("att", att_p[:, :], [(s_ev, ev_att)])

                    def pair_state(v, g_=g_, mm_mupd=mm_mupd, Y=Y):
                        v.wait_ge(s_mm.h, mm_mupd)
                        v.tensor_tensor(out=sscr[:, 0:65], in0=Y[:, 772:837],
                                        in1=gmp[:, g_ * 65:(g_ + 1) * 65], op=ALU.mult)
                        v.drain()
                        s_ev.inc(v.tensor_tensor(out=zmf[:, g_ * 65:(g_ + 1) * 65],
                                                 in0=zmf[:, g_ * 65:(g_ + 1) * 65],
                                                 in1=sscr[:, 0:65], op=ALU.add))
                    V(pair_state)

                    mm_attT_b = s_mm.n

                    def pair_attT(t, ev_att=ev_att, X_bf=X_bf):
                        t.wait_ge(s_ev.h, ev_att)
                        ins = None
                        for hp in range(2):
                            for m in range(4):
                                x = hp * 4 + m
                                ins = t.transpose(
                                    X_bf[64 * hp:64 * (hp + 1), 1280 + m * 128: 1280 + (m + 1) * 128],
                                    att_p[:, x * 64:(x + 1) * 64],
                                    identb[:, :])
                        s_mm.inc(ins)
                    T(pair_attT)
                    mm_attT = s_mm.n

                    def pair_attO(is_v, g_=g_, mm_attT=mm_attT, X_bf=X_bf):
                        def f(en):
                            en.wait_ge(s_mm.h, mm_attT)
                            for hp in range(2):
                                h_ = 2 * g_ + hp
                                for cpar in range(2):
                                    if (hp + cpar) % 2 != (0 if is_v else 1):
                                        continue
                                    src = X_bf[64 * hp:64 * (hp + 1), 1280:1792].rearrange(
                                        "p (m a t2) -> p m a t2",
                                        m=4, a=8)[:, :, :, cpar * 8:(cpar + 1) * 8]
                                    src = src.transpose([0, 2, 1, 3])
                                    dst = atto[64 * cpar:64 * (cpar + 1), :].rearrange(
                                        "p (a i) -> p a i", a=8)[:, :, 32 * h_:32 * h_ + 32]
                                    dst = dst.rearrange("p a (m tw) -> p a m tw", m=4)
                                    ins = en.tensor_copy(out=dst, in_=src)
                                    (s_ev if is_v else s_sc).inc(ins)
                        return f
                    V(pair_attO(True))
                    V(pair_attO(False))
                    last_ev_attO = s_ev.n
                    last_sc_attO = s_sc.n
                    attO_done.append((last_ev_attO, last_sc_attO))

                if is_dbg:
                    dump("atto", atto[:, :], [(s_ev, last_ev_attO), (s_sc, last_sc_attO)])
                    dump("zmf", zmf[:, :], [(s_ev, s_ev.n)])

                mm_wo_b = s_mm.n
                ev_wo_b = s_ev.n

                def wo_t(t, a=None, b_=None):
                    t.wait_ge(s_ev.h, last_ev_attO)
                    t.wait_ge(s_sc.h, last_sc_attO)
                    for ic in range(4):
                        p = pa if ic % 2 == 0 else pb
                        if ic >= 2:
                            t.wait_ge(s_ev.h, ev_wo_b + (ic - 1))
                        ins = None
                        for half in range(2):
                            for a in range(8):
                                ins = t.matmul(
                                    p[:, half * 512:(half + 1) * 512],
                                    atto[:, a * 512 + 128 * ic: a * 512 + 128 * (ic + 1)],
                                    Wo_sb[:, a * 1024 + half * 512: a * 1024 + (half + 1) * 512],
                                    start=(a == 0), stop=(a == 7))
                        s_mm.inc(ins)
                T(wo_t)

                def wo_ev(v, mm_wo_b=mm_wo_b):
                    for ic in range(4):
                        p = pa if ic % 2 == 0 else pb
                        v.wait_ge(s_mm.h, mm_wo_b + ic + 1)
                        if skip_wo:
                            s_ev.inc(v.tensor_copy(out=rd4[:, 1:2], in_=rd4[:, 1:2]))
                        else:
                            s_ev.inc(v.tensor_tensor(out=acc[:, ic * 1024:(ic + 1) * 1024],
                                                     in0=acc[:, ic * 1024:(ic + 1) * 1024],
                                                     in1=p[:, 0:1024], op=ALU.add))
                V(wo_ev)
                ev_wo = s_ev.n

                if is_dbg:
                    dump("accw", acc[:, :], [(s_ev, ev_wo)])

                mm_f1_b = s_mm.n
                if skip_ffn:
                    ev_f2 = s_ev.n

                def f1_t(t, wf1_base=wf1_base):
                    if skip_ffn:
                        t.wait_ge(s_ev.h, ev_wo)
                        for mc in range(32):
                            sl, nn = w1_ready_at[mc]
                            t.wait_ge(s_w1s[sl].h, nn)
                        for p_ in range(2):
                            for kc in range(32):
                                sl, nn = w2_ready_at[(p_, kc)]
                                t.wait_ge(s_w2s[sl].h, nn)
                        return
                    t.wait_ge(s_ev.h, ev_wo)
                    for mc in range(32):
                        p = pa if (mc % 4) < 2 else pb
                        off = 512 * (mc % 2)
                        sl, nn = w1_ready_at[mc]
                        t.wait_ge(s_w1s[sl].h, nn)
                        if mc >= 4:
                            t.wait_ge(s_wf1.h, wf1_base + (mc - 3))
                        ins = None
                        for kc in range(8):
                            ins = t.matmul(
                                p[:, off:off + 512],
                                w1b[:, (mc % 3) * 1024 + kc * 128: (mc % 3) * 1024 + (kc + 1) * 128],
                                xn_fm[:, kc * 512:(kc + 1) * 512],
                                start=(kc == 0), stop=(kc == 7))
                        s_mm.inc(ins)
                T(f1_t)

                def f1_ev(s, mm_f1_b=mm_f1_b):
                    if skip_ffn:
                        for mc in range(32):
                            s_wf1.inc(s.engine_nop() if False else s.copy(out=elu_t[:, 0:1], in_=elu_t[:, 0:1]))
                        return
                    for mc in range(32):
                        p = pa if (mc % 4) < 2 else pb
                        off = 512 * (mc % 2)
                        s.wait_ge(s_mm.h, mm_f1_b + mc + 1)
                        s_wf1.inc(s.activation(out=hsb[:, mc * 512:(mc + 1) * 512],
                                               in_=p[:, off:off + 512], func=AF.Gelu,
                                               bias=b1_sb[:, mc:mc + 1]))
                SC(f1_ev)
                wf1_done = s_wf1.n

                if is_dbg:
                    dump("hsb", hsb[:, 0:2048], [(s_wf1, wf1_done)])

                mm_f2_b = s_mm.n
                ev_f2_b = s_ev.n

                def f2_t(t, wf1_done=wf1_done):
                    if skip_ffn:
                        return
                    t.wait_ge(s_wf1.h, wf1_done)
                    for p_ in range(2):
                        if p_ == 1:
                            t.wait_ge(s_ev.h, ev_f2_b + 2)
                        for kc in range(32):
                            sl, nn = w2_ready_at[(p_, kc)]
                            t.wait_ge(s_w2s[sl].h, nn)
                            ins = None
                            for ic2 in range(2):
                                ic = 2 * p_ + ic2
                                p = pa if ic2 == 0 else pb
                                for half in range(2):
                                    ins = t.matmul(
                                        p[:, half * 512:(half + 1) * 512],
                                        hsb[:, kc * 512 + 128 * ic: kc * 512 + 128 * (ic + 1)],
                                        w2b[:, ((32 * p_ + kc) % 3) * 1024 + half * 512:
                                            ((32 * p_ + kc) % 3) * 1024 + (half + 1) * 512],
                                        start=(kc == 0), stop=(kc == 31))
                            s_wf2.inc(ins)
                T(f2_t)

                def f2_ev(v, wf2_base=wf2_base):
                    if skip_ffn:
                        ins = None
                        for i_ in range(64):
                            ins = v.tensor_copy(out=rd4[:, 0:1], in_=rd4[:, 0:1])
                            s_wf2.inc(ins)
                        return
                    for p_ in range(2):
                        v.wait_ge(s_wf2.h, wf2_base + 32 * (p_ + 1))
                        for ic2 in range(2):
                            ic = 2 * p_ + ic2
                            p = pa if ic2 == 0 else pb
                            s_ev.inc(v.tensor_tensor(
                                out=acc[:, ic * 1024:(ic + 1) * 1024],
                                in0=acc[:, ic * 1024:(ic + 1) * 1024],
                                in1=p[:, 0:1024], op=ALU.add))
                V(f2_ev)
                ev_f2 = s_ev.n

                if is_dbg:
                    dump("acc", acc[:, :], [(s_ev, ev_f2)])

                def store(sy, b=b, seg=seg, ev_f2=ev_f2):
                    sy.wait_ge(s_ev.h, ev_f2)
                    for ic in range(4):
                        s_stic[ic].inc(sy.dma_start(
                            out=ys[b, seg * 512 + 128 * ic: seg * 512 + 128 * (ic + 1), :],
                            in_=acc[:, ic * 1024:(ic + 1) * 1024]), 16)
                SY(store)

        def fin(sy):
            sy.wait_ge(s_st.h, s_st.n)
            for i in range(4):
                sy.wait_ge(s_stic[i].h, s_stic[i].n)
        SY(fin)

    return nc


def host_arrange_inputs(inp, nb=4, core=0):
    x = np.ascontiguousarray(inp["L0_attn_x"][core * nb:(core + 1) * nb])
    betas = np.ascontiguousarray(inp["L0_attn_betas"].reshape(H, DV))
    m0 = inp["L0_attn_memory0"].reshape(H, DK, DV)
    z0 = inp["L0_attn_z0"].reshape(H, DK)
    m0_arr = np.zeros((128, 512), dtype=np.float32)
    z0_arr = np.zeros((128, 8), dtype=np.float32)
    for g in range(8):
        for hp in range(2):
            h = 2 * g + hp
            m0_arr[64 * hp:64 * (hp + 1), g * 64:(g + 1) * 64] = m0[h]
            z0_arr[64 * hp:64 * (hp + 1), g] = z0[h]
    return {
        "xs": x, "Wq": inp["L0_attn_Wq"], "Wk": inp["L0_attn_Wk"],
        "Wv": inp["L0_attn_Wv"], "Wo": inp["L0_attn_Wo"],
        "w1": inp["L0_ffn_w1"], "b1": inp["L0_ffn_b1"],
        "w2": inp["L0_ffn_w2"], "b2": inp["L0_ffn_b2"],
        "betas": betas, "m0_arr": m0_arr, "z0_arr": z0_arr,
    }


_NCORES = 8


def kernel(**inputs):
    from concourse.bass_utils import run_bass_kernel_spmd
    in_maps = [host_arrange_inputs(inputs, nb=4, core=c) for c in range(_NCORES)]
    nc = build(nb=4, nsegs=4)
    res = run_bass_kernel_spmd(nc, in_maps, core_ids=list(range(_NCORES)))
    out = np.concatenate([res.results[c]["ys"] for c in range(_NCORES)], axis=0)
    return out.astype(np.float32)


# version 07
# speedup vs baseline: 1.2259x; 1.0393x over previous
import numpy as np
import contextlib
import concourse.bass as bass
import concourse.mybir as mybir

F32 = mybir.dt.float32
BF16 = mybir.dt.bfloat16
AF = mybir.ActivationFunctionType
ALU = mybir.AluOpType

H, DK, DV = 16, 64, 64
D = 1024
NHID = 4096
CEXP = 12.0
EPS = 1e-5


class Sem:
    def __init__(self, nc, st, name):
        self.h = st.enter_context(nc.semaphore(name))
        self.n = 0

    def inc(self, ins, k=1):
        ins.then_inc(self.h, k)
        self.n += k
        return ins


def host_constants():
    ident = np.eye(128, dtype=np.float32)
    m = np.zeros((128, 128), dtype=np.float32)
    for c2 in range(16):
        for t2 in range(8):
            for c1 in range(16):
                for t1 in range(8):
                    m[8 * c2 + t2, 8 * c1 + t1] = 1.0 if (16 * t2 + c2) <= (16 * t1 + c1) else 0.0
    return {"ident": ident, "diagmask": m}


def build(nb=4, nsegs=4, debug=(), dbg_tile=None, skip_ffn=False, skip_wo=False, stages=None, detect=False):
    nc = bass.Bass("TRN2", target_bir_lowering=False, detect_race_conditions=detect)
    S = 2048
    if dbg_tile is None:
        dbg_tile = (0, nsegs - 1)

    xs = nc.dram_tensor("xs", [nb, S, D], F32, kind="ExternalInput").ap()
    Wq = nc.dram_tensor("Wq", [D, D], F32, kind="ExternalInput").ap()
    Wk = nc.dram_tensor("Wk", [D, D], F32, kind="ExternalInput").ap()
    Wv = nc.dram_tensor("Wv", [D, D], F32, kind="ExternalInput").ap()
    Wo = nc.dram_tensor("Wo", [D, D], F32, kind="ExternalInput").ap()
    w1 = nc.dram_tensor("w1", [D, NHID], F32, kind="ExternalInput").ap()
    b1 = nc.dram_tensor("b1", [NHID], F32, kind="ExternalInput").ap()
    w2 = nc.dram_tensor("w2", [NHID, D], F32, kind="ExternalInput").ap()
    b2 = nc.dram_tensor("b2", [D], F32, kind="ExternalInput").ap()
    betas = nc.dram_tensor("betas", [H, DV], F32, kind="ExternalInput").ap()
    m0_arr = nc.dram_tensor("m0_arr", [128, 512], F32, kind="ExternalInput").ap()
    z0_arr = nc.dram_tensor("z0_arr", [128, 8], F32, kind="ExternalInput").ap()
    ys = nc.dram_tensor("ys", [nb, S, D], F32, kind="ExternalOutput").ap()

    consts = host_constants()
    ident_dram = nc.inline_tensor(consts["ident"], "identc")
    mask_dram = nc.inline_tensor(consts["diagmask"], "maskc")

    w1_scr = nc.dram_tensor("w1_scr", [128, 32, 8, 128], BF16, kind="Internal").ap()
    w2_scr = nc.dram_tensor("w2_scr", [32, 128, D], BF16, kind="Internal").ap()
    gate_scr = nc.dram_tensor("gate_scr", [H, 65], F32, kind="Internal").ap()
    omg_scr = nc.dram_tensor("omg_scr", [H, 65], F32, kind="Internal").ap()

    dbg_specs = {
        "xn_fm": ([128, 8 * 512], BF16), "qd2": ([128, 4096], BF16),
        "kd2": ([128, 4096], BF16), "vd2": ([128, 4096], BF16),
        "vposm": ([128, 4160], BF16), "vs": ([128, 4160], BF16),
        "skp": ([128, 4096], BF16), "pt": ([128, 4096], BF16),
        "att": ([128, 512], BF16), "atto": ([128, 4096], BF16),
        "acc": ([128, 4096], F32), "accw": ([128, 4096], F32), "zmf": ([128, 520], F32),
        "sq": ([128, 512], BF16), "hsb": ([128, 2048], BF16),
        "xn_tm": ([128, 4096], BF16),
    }
    dbg_drams = {k: nc.dram_tensor("dbg_" + k, *dbg_specs[k], kind="ExternalOutput").ap()
                 for k in debug}

    with contextlib.ExitStack() as st:
        e = st.enter_context

        def sb(name, shape, dt):
            return e(nc.sbuf_tensor(name, shape, dt))

        Wq_sb = sb("Wq_sb", [128, 8 * 1024], BF16)
        Wk_sb = sb("Wk_sb", [128, 8 * 1024], BF16)
        Wv_sb = sb("Wv_sb", [128, 8 * 1024], BF16)
        Wo_sb = sb("Wo_sb", [128, 8 * 1024], BF16)
        w1b = sb("w1b", [128, 3 * 1024], BF16)
        w2b = sb("w2b", [128, 3 * 1024], BF16)
        acc = sb("acc", [128, 4 * 1024], F32)
        xn_tm = sb("xn_tm", [128, 4 * 1024], BF16)
        xn_fm = sb("xn_fm", [128, 8 * 512], BF16)
        qd2 = sb("qd2", [128, 4096], BF16)
        kd2 = sb("kd2", [128, 4096], BF16)
        vd2 = sb("vd2", [128, 4096], BF16)
        vposm = sb("vposm", [128, 4160], BF16)
        vs = sb("vs", [128, 4160], BF16)
        skp = sb("skp", [128, 4096], BF16)
        pt = vd2
        att_p = sb("att_p", [128, 512], BF16)
        memn = sb("memn", [128, 512], BF16)
        sq_p = sb("sq_p", [128, 1024], BF16)
        elu_t = sb("elu_t", [128, 1024], BF16)
        atto = xn_tm
        hsb = sb("hsb", [128, 32 * 512], BF16)
        sscr = sb("sscr", [128, 65], F32)
        zmf = sb("zmf", [128, 8 * 65], F32)
        zmb = sb("zmb", [128, 8 * 65], BF16)
        gmp = sb("gmp", [128, 8 * 65], F32)
        omg130 = sb("omg130", [128, 8 * 130], BF16)
        b2bc = sb("b2bc", [128, 1024], F32)
        b1_sb = sb("b1_sb", [128, 32], F32)
        mask8 = sb("mask8", [128, 128], BF16)
        identb = sb("identb", [128, 128], BF16)
        gstage = sb("gstage", [H, 65], F32)
        stats = sb("stats", [128, 16], F32)
        rd4 = sb("rd4", [128, 16], F32)
        eps_sb = sb("eps_sb", [128, 1], F32)
        negc_sb = sb("negc_sb", [128, 1], F32)
        tiny_sb = sb("tiny_sb", [128, 1], F32)
        zero_sb = sb("zero_sb", [128, 1], F32)
        onesb = sb("onesb", [1, 1], F32)

        class _WStage:
            def __getitem__(self, sl):
                return None
        def wslice(kc, width):
            buf = qd2 if kc % 2 == 0 else kd2
            return buf[:, 0:width]

        pa = e(nc.psum_tensor("pa", [128, 1024], F32))
        pb = e(nc.psum_tensor("pb", [128, 1024], F32))
        pc = e(nc.psum_tensor("pc", [128, 1024], F32))
        pd = e(nc.psum_tensor("pd", [128, 1024], F32))
        pc_bf = pc.ap().bitcast(BF16)
        pd_bf = pd.ap().bitcast(BF16)
        pa_bf = pa.ap().bitcast(BF16)

        s_ld = Sem(nc, st, "s_ld")
        s_ldx = Sem(nc, st, "s_ldx")
        s_state = Sem(nc, st, "s_state")
        s_w1s = [Sem(nc, st, f"s_w1s{i}") for i in range(3)]
        s_w2s = [Sem(nc, st, f"s_w2s{i}") for i in range(3)]
        s_stg = [Sem(nc, st, f"s_stg{i}") for i in range(2)]
        s_sts = [Sem(nc, st, f"s_sts{i}") for i in range(2)]
        s_wf1 = Sem(nc, st, "s_wf1")
        s_wf2 = Sem(nc, st, "s_wf2")
        s_mm = Sem(nc, st, "s_mm")
        s_ev = Sem(nc, st, "s_ev")
        s_sc = Sem(nc, st, "s_sc")
        s_st = Sem(nc, st, "s_st")
        s_stic = [Sem(nc, st, f"s_stic{i}") for i in range(4)]
        block = e(nc.Block())

        T, V, SC, SY, G = block.tensor, block.vector, block.scalar, block.sync, block.gpsimd

        def dump(name, ap_src, waits):
            if name not in dbg_drams:
                return
            dst = dbg_drams[name]

            def _f(sy, dst=dst, ap_src=ap_src, waits=tuple(waits)):
                for sem, n in waits:
                    sy.wait_ge(sem.h, n)
                s_st.inc(sy.dma_start(out=dst[:, :], in_=ap_src), 16)
            SY(_f)

        def p0_gpsimd(g):
            for W, Wsb in ((Wq, Wq_sb), (Wk, Wk_sb), (Wv, Wv_sb), (Wo, Wo_sb)):
                for kc in range(8):
                    s_ld.inc(g.dma_start(out=Wsb[:, kc * 1024:(kc + 1) * 1024],
                                         in_=W[128 * kc:128 * (kc + 1), :]), 16)
            s_ld.inc(g.dma_start(out=identb[:, :], in_=ident_dram.ap()[:, :]), 16)
            s_ld.inc(g.dma_start(out=mask8[:, 0:128], in_=mask_dram.ap()[:, :]), 16)
            s_ld.inc(g.dma_start(out=b2bc[:, :],
                                 in_=b2.unsqueeze(0).partition_broadcast(128)), 16)
            s_ld.inc(g.dma_start(out=gstage[:, 1:65], in_=betas[:, :]), 16)
            s_ld.inc(g.dma_start(out=acc[0:1, 0:4096], in_=b1.unsqueeze(0)), 16)
        G(p0_gpsimd)
        ld0 = s_ld.n

        def p0_vector(v):
            v.memset(eps_sb[:, :], EPS)
            v.memset(negc_sb[:, :], -CEXP)
            v.memset(tiny_sb[:, :], 1e-35)
            v.memset(zero_sb[:, :], 0.0)
            v.memset(onesb[:, :], 1.0)
            v.memset(gstage[:, 0:1], 1.0)
            v.memset(vposm[:, :].rearrange("p (b c) -> p b c", c=65)[:, :, 0], 1.0)
            v.memset(vs[:, :].rearrange("p (b c) -> p b c", c=65)[:, :, 0], 1.0)
            v.wait_ge(s_ld.h, ld0)
            s_ev.inc(v.memset(rd4[:, 0:1], 0.0))
        V(p0_vector)
        ev0 = s_ev.n

        def p0_scalar(s):
            s.wait_ge(s_ld.h, ld0)
            s.wait_ge(s_ev.h, ev0)
            s_sc.inc(s.activation(out=gstage[:, 1:65], in_=gstage[:, 1:65], func=AF.Sigmoid))
        SC(p0_scalar)
        sc_gate = s_sc.n

        def p0_vec2(v):
            v.wait_ge(s_sc.h, sc_gate)
            v.tensor_scalar(out=sscr[0:H, 0:65], in0=gstage[:, 0:65],
                            scalar1=1.0, scalar2=-1.0, op0=ALU.subtract, op1=ALU.mult)
            v.drain()
            s_ev.inc(v.memset(sscr[0:H, 0:1], 1.0))
        V(p0_vec2)
        ev_omg = s_ev.n

        def p0_b1_t(t):
            t.wait_ge(s_ld.h, ld0)
            t.wait_ge(s_ev.h, ev0)
            ins = None
            for mc in range(32):
                ins = t.matmul(pa[0:128, mc:mc + 1], acc[0:1, 128 * mc:128 * (mc + 1)],
                               onesb[0:1, 0:1], start=True, stop=True)
            s_mm.inc(ins)
        T(p0_b1_t)
        mm_b1 = s_mm.n

        def p0_b1_v(v):
            v.wait_ge(s_mm.h, mm_b1)
            s_ev.inc(v.tensor_copy(out=b1_sb[:, :], in_=pa[:, 0:32]))
        V(p0_b1_v)
        ev_b1 = s_ev.n

        def p0_gates_sy(sy):
            sy.wait_ge(s_sc.h, sc_gate)
            s_st.inc(sy.dma_start(out=gate_scr[:, :], in_=gstage[:, :]), 16)
            sy.wait_ge(s_ev.h, ev_omg)
            s_st.inc(sy.dma_start(out=omg_scr[:, :], in_=sscr[0:H, 0:65]), 16)
            sy.wait_ge(s_st.h, s_st.n)
            for g_ in range(8):
                for hp in range(2):
                    s_state.inc(sy.dma_start(
                        out=gmp[64 * hp:64 * (hp + 1), g_ * 65:(g_ + 1) * 65],
                        in_=gate_scr[2 * g_ + hp:2 * g_ + hp + 1, :].partition_broadcast(64)), 16)
        SY(p0_gates_sy)

        def p0_omg_g(g):
            g.wait_ge(s_st.h, s_st.n)
            for g_ in range(8):
                for hp in range(2):
                    s_ld.inc(g.dma_start(
                        out=omg130[:, (g_ * 2 + hp) * 65:(g_ * 2 + hp + 1) * 65],
                        in_=omg_scr[2 * g_ + hp:2 * g_ + hp + 1, :].partition_broadcast(128)), 16)
        G(p0_omg_g)

        def p0_w1_g(g):
            for kc in range(8):
                if kc >= 2:
                    g.wait_ge(s_sts[kc % 2].h, 16 * (kc // 2))
                s_stg[kc % 2].inc(g.dma_start(
                    out=wslice(kc, 4096),
                    in_=w1[128 * kc:128 * (kc + 1), :]), 16)
        G(p0_w1_g)

        def p0_w1_sy(sy):
            for kc in range(8):
                sy.wait_ge(s_stg[kc % 2].h, 16 * (kc // 2 + 1))
                src = wslice(kc, 4096).rearrange("p (mc c) -> p mc c", mc=32)
                dst = w1_scr[:, :, kc, :]
                s_sts[kc % 2].inc(sy.dma_start(out=dst, in_=src), 16)
        SY(p0_w1_sy)

        w1_sts_base = [s.n for s in s_sts]
        w1_stg_base = [s.n for s in s_stg]

        def p0_w2_g(g):
            for kc in range(32):
                if kc == 0:
                    g.wait_ge(s_sts[0].h, w1_sts_base[0])
                    g.wait_ge(s_sts[1].h, w1_sts_base[1])
                if kc >= 2:
                    g.wait_ge(s_sts[kc % 2].h, w1_sts_base[kc % 2] + 16 * (kc // 2))
                s_stg[kc % 2].inc(g.dma_start(
                    out=wslice(kc, 1024),
                    in_=w2[128 * kc:128 * (kc + 1), :]), 16)
        G(p0_w2_g)

        def p0_w2_sy(sy):
            for kc in range(32):
                sy.wait_ge(s_stg[kc % 2].h, w1_stg_base[kc % 2] + 16 * (kc // 2 + 1))
                s_sts[kc % 2].inc(sy.dma_start(
                    out=w2_scr[kc, :, :],
                    in_=wslice(kc, 1024)), 16)
        SY(p0_w2_sy)

        P0_LD = s_ld.n
        P0_ST = s_st.n
        P0_MM = s_mm.n

        for b in range(nb):
            if stages == "xn":
                pass
            def init_state_sy(sy, b=b, mmn=s_mm.n, evn=s_ev.n):
                sy.wait_ge(s_st.h, P0_ST)
                if b > 0:
                    sy.wait_ge(s_mm.h, mmn)
                    sy.wait_ge(s_ev.h, evn)
                s_state.inc(sy.dma_start(
                    out=zmf[:, :].rearrange("p (g c) -> p g c", g=8)[:, :, 0],
                    in_=z0_arr[:, :], allow_slow_non_contiguous=True), 16)
                s_state.inc(sy.dma_start(
                    out=zmf[:, :].rearrange("p (g c) -> p g c", g=8)[:, :, 1:65],
                    in_=m0_arr[:, :].rearrange("p (g v) -> p g v", g=8)), 16)
            SY(init_state_sy)
            ld_state = s_state.n

            def init_state_v(v, b=b):
                v.wait_ge(s_state.h, ld_state)
                if b == 0:
                    v.wait_ge(s_ld.h, P0_LD)
                s_ev.inc(v.tensor_copy(out=zmb[:, :], in_=zmf[:, :]))
            V(init_state_v)
            ev_state0 = s_ev.n

            for seg in range(nsegs):
                is_dbg = (b, seg) == dbg_tile and dbg_drams

                def x_load(sy, b=b, seg=seg, stic=[x.n for x in s_stic], mmn=P0_MM):
                    if b == 0 and seg == 0:
                        sy.wait_ge(s_mm.h, mmn)
                    for ic in range(4):
                        sy.wait_ge(s_stic[ic].h, stic[ic])
                        s_ldx.inc(sy.dma_start(
                            out=acc[:, ic * 1024:(ic + 1) * 1024],
                            in_=xs[b, seg * 512 + 128 * ic: seg * 512 + 128 * (ic + 1), :]), 16)
                SY(x_load)
                ld_x = s_ldx.n

                wf1_base = s_wf1.n
                wf2_base = s_wf2.n
                w1_ready_at = {}
                w2_ready_at = {}

                def w_prefetch(sy, w1_ready_at=w1_ready_at, w2_ready_at=w2_ready_at):
                    for i in range(2):
                        sy.wait_ge(s_sts[i].h, s_sts[i].n)
                    for mc in range(32):
                        need = wf1_base + mc - 2
                        if need > 0:
                            sy.wait_ge(s_wf1.h, need)
                        slot = mc % 3
                        ins = sy.dma_start(
                            out=w1b[:, slot * 1024:(slot + 1) * 1024],
                            in_=w1_scr[:, mc, :, :])
                        s_w1s[slot].inc(ins, 16)
                        w1_ready_at[mc] = (slot, s_w1s[slot].n)
                    for p_ in range(2):
                        for kc in range(32):
                            need = wf2_base + 32 * p_ + kc - 2
                            if need > 0:
                                sy.wait_ge(s_wf2.h, need)
                            slot = (32 * p_ + kc) % 3
                            ins = sy.dma_start(
                                out=w2b[:, slot * 1024:(slot + 1) * 1024],
                                in_=w2_scr[kc, :, :])
                            s_w2s[slot].inc(ins, 16)
                            w2_ready_at[(p_, kc)] = (slot, s_w2s[slot].n)
                if stages != "xn":
                    SY(w_prefetch)

                if seg > 0:
                    def zmb_cast(v):
                        v.drain()
                        s_ev.inc(v.tensor_copy(out=zmb[:, :], in_=zmf[:, :]))
                    V(zmb_cast)

                def ln_v(v):
                    v.wait_ge(s_ldx.h, ld_x)
                    ins = None
                    for ic in range(4):
                        ins = v.reduce_sum(out=stats[:, ic:ic + 1],
                                           in_=acc[:, ic * 1024:(ic + 1) * 1024],
                                           axis=mybir.AxisListType.X)
                    s_ev.inc(ins)
                V(ln_v)
                ev_ln1 = s_ev.n

                def ln_s(s):
                    s.wait_ge(s_ldx.h, ld_x)
                    ins = None
                    for ic in range(4):
                        ins = s.activation(out=hsb[:, ic * 1024:(ic + 1) * 1024],
                                           in_=acc[:, ic * 1024:(ic + 1) * 1024],
                                           func=AF.Square, accum_out=stats[:, 4 + ic:5 + ic])
                    s_sc.inc(ins)
                SC(ln_s)
                sc_ln = s_sc.n

                def ln_v2(v):
                    v.wait_ge(s_sc.h, sc_ln)
                    v.wait_ge(s_ev.h, ev_ln1)
                    v.tensor_scalar(out=stats[:, 8:12], in0=stats[:, 0:4], scalar1=1.0 / D,
                                    scalar2=None, op0=ALU.mult)
                    v.tensor_scalar(out=stats[:, 4:8], in0=stats[:, 4:8], scalar1=1.0 / D,
                                    scalar2=None, op0=ALU.mult)
                    v.drain()
                    v.tensor_tensor(out=stats[:, 12:16], in0=stats[:, 8:12],
                                    in1=stats[:, 8:12], op=ALU.mult)
                    v.drain()
                    s_ev.inc(v.tensor_tensor(out=stats[:, 12:16], in0=stats[:, 4:8],
                                             in1=stats[:, 12:16], op=ALU.subtract))
                V(ln_v2)
                ev_var = s_ev.n

                def ln_s2(s):
                    s.wait_ge(s_ev.h, ev_var)
                    s.activation(out=stats[:, 12:16], in_=stats[:, 12:16],
                                 func=AF.Ln, bias=eps_sb[:, :])
                    s.drain()
                    s_sc.inc(s.activation(out=stats[:, 12:16], in_=stats[:, 12:16],
                                          func=AF.Exp, scale=-0.5))
                SC(ln_s2)
                sc_rstd = s_sc.n

                def ln_v3(v):
                    v.wait_ge(s_sc.h, sc_rstd)
                    ins = None
                    for ic in range(4):
                        ins = v.tensor_scalar(
                            out=xn_tm[:, ic * 1024:(ic + 1) * 1024],
                            in0=acc[:, ic * 1024:(ic + 1) * 1024],
                            scalar1=stats[:, 8 + ic:9 + ic], scalar2=stats[:, 12 + ic:13 + ic],
                            op0=ALU.subtract, op1=ALU.mult)
                        s_ev.inc(ins)
                V(ln_v3)
                ev_xn = s_ev.n

                def ln_b2(v):
                    v.drain()
                    ins = None
                    for ic in range(4):
                        ins = v.tensor_tensor(out=acc[:, ic * 1024:(ic + 1) * 1024],
                                              in0=acc[:, ic * 1024:(ic + 1) * 1024],
                                              in1=b2bc[:, :], op=ALU.add)
                    s_ev.inc(ins)
                V(ln_b2)
                ev_b2 = s_ev.n

                if is_dbg:
                    dump("xn_tm", xn_tm[:, :], [(s_ev, ev_xn)])

                mm_xnT = s_mm.n
                ev_xnT = s_ev.n

                sc_now = s_sc.n

                def xnT(t, sc_now=sc_now):
                    t.wait_ge(s_ev.h, ev_xn)
                    t.wait_ge(s_sc.h, sc_now)
                    t.wait_ge(s_ld.h, P0_LD)
                    for jc in range(8):
                        pb_t = pc_bf if jc < 4 else pd_bf
                        ins = None
                        for ic in range(4):
                            ins = t.transpose(
                                pb_t[:, (jc % 4) * 512 + ic * 128: (jc % 4) * 512 + (ic + 1) * 128],
                                xn_tm[:, ic * 1024 + 128 * jc: ic * 1024 + 128 * (jc + 1)],
                                identb[:, :])
                        s_mm.inc(ins)
                T(xnT)

                def xnT_ev(v):
                    v.wait_ge(s_mm.h, mm_xnT + 4)
                    s_ev.inc(v.tensor_copy(out=xn_fm[:, 0:2048], in_=pc_bf[:, 0:2048]))
                    v.wait_ge(s_mm.h, mm_xnT + 8)
                    s_ev.inc(v.tensor_copy(out=xn_fm[:, 2048:4096], in_=pd_bf[:, 0:2048]))
                V(xnT_ev)
                ev_xnfm = s_ev.n

                if is_dbg:
                    dump("xn_fm", xn_fm[:, :], [(s_ev, ev_xnfm)])
                if stages == "xn":
                    continue

                mm_proj = s_mm.n
                ev_proj_b = s_ev.n
                sc_proj_b = s_sc.n
                projs = [(Wv_sb, vd2), (Wk_sb, kd2), (Wq_sb, qd2)]

                def proj_t(t):
                    t.wait_ge(s_ev.h, ev_xnfm)
                    cnt = 0
                    for Wsb, dst in projs:
                        for jc in range(8):
                            p = pa if (cnt % 4) < 2 else pb
                            off = 512 * (cnt % 2)
                            if cnt >= 4:
                                t.wait_ge(s_ev.h, ev_proj_b + 2 * (cnt - 3))
                                t.wait_ge(s_sc.h, sc_proj_b + 2 * (cnt - 3))
                            ins = None
                            for kc in range(8):
                                ins = t.matmul(
                                    p[:, off:off + 512],
                                    Wsb[:, kc * 1024 + 128 * jc: kc * 1024 + 128 * (jc + 1)],
                                    xn_fm[:, kc * 512:(kc + 1) * 512],
                                    start=(kc == 0), stop=(kc == 7))
                            s_mm.inc(ins)
                            cnt += 1
                T(proj_t)

                def proj_ev(who, first=(b == 0 and seg == 0), sts_tot=[x.n for x in s_sts]):
                    def f(en):
                        if first:
                            en.wait_ge(s_sts[0].h, sts_tot[0])
                            en.wait_ge(s_sts[1].h, sts_tot[1])
                        cnt = 0
                        for Wsb, dst in projs:
                            for jc in range(8):
                                p = pa if (cnt % 4) < 2 else pb
                                off = 512 * (cnt % 2)
                                en.wait_ge(s_mm.h, mm_proj + cnt + 1)
                                cpar = who
                                for hp in range(2):
                                    src = p[64 * cpar:64 * (cpar + 1), off:off + 512].rearrange(
                                        "p (g two n tw) -> p g two n tw",
                                        g=8, two=2, n=4)[:, :, hp, :, :]
                                    dstv = dst[64 * hp:64 * (hp + 1), :].rearrange(
                                        "p (g n c tw) -> p g n c tw",
                                        g=8, n=4, c=16)[:, :, :, 2 * jc + cpar, :]
                                    if who == 0:
                                        ins = en.tensor_copy(out=dstv, in_=src)
                                        s_ev.inc(ins)
                                    else:
                                        ins = en.copy(out=dstv, in_=src)
                                        s_sc.inc(ins)
                                cnt += 1
                    return f
                V(proj_ev(0))
                SC(proj_ev(1))
                ev_proj = s_ev.n
                sc_proj = s_sc.n

                if is_dbg:
                    dump("qd2", qd2[:, :], [(s_ev, ev_proj), (s_sc, sc_proj)])
                    dump("kd2", kd2[:, :], [(s_ev, ev_proj), (s_sc, sc_proj)])
                    dump("vd2", vd2[:, :], [(s_ev, ev_proj), (s_sc, sc_proj)])

                mm_vpos = s_mm.n
                ev_vpos_b = s_ev.n
                sc_vpos_b = s_sc.n

                def vpos_t(t):
                    t.wait_ge(s_ev.h, ev_proj_b + 16)
                    t.wait_ge(s_sc.h, sc_proj_b + 16)
                    for g_ in range(8):
                        half = g_ % 2
                        if g_ >= 2:
                            t.wait_ge(s_ev.h, ev_vpos_b + (g_ - 1))
                            t.wait_ge(s_sc.h, sc_vpos_b + 4 * (g_ - 1))
                        ins = None
                        for n in range(4):
                            ins = t.transpose(
                                pc_bf[:, half * 1024 + n * 128: half * 1024 + (n + 1) * 128],
                                vd2[:, g_ * 512 + 128 * n: g_ * 512 + 128 * (n + 1)],
                                identb[:, :])
                        s_mm.inc(ins)
                T(vpos_t)

                def vpos_ev(v):
                    for g_ in range(8):
                        half = g_ % 2
                        v.wait_ge(s_mm.h, mm_vpos + g_ + 1)
                        src = pc_bf[:, half * 1024: half * 1024 + 512].rearrange(
                            "p (n hp v) -> p n hp v", n=4, hp=2)
                        dst = vposm[:, g_ * 520:(g_ + 1) * 520].rearrange(
                            "p (n hp c) -> p n hp c", n=4, hp=2)[:, :, :, 1:65]
                        s_ev.inc(v.tensor_copy(out=dst, in_=src))
                        for n in range(4):
                            src2 = pc_bf[:, half * 1024 + n * 128: half * 1024 + n * 128 + 128].rearrange(
                                "p (hp v) -> p hp v", hp=2)
                            dst2 = vs[:, g_ * 520 + n * 130: g_ * 520 + (n + 1) * 130].rearrange(
                                "p (hp c) -> p hp c", hp=2)[:, :, 1:65]
                            omg = omg130[:, :].rearrange(
                                "p (g2 hp c) -> p g2 hp c", g2=8, hp=2)[:, g_, :, 1:65]
                            s_sc.inc(v.tensor_tensor(out=dst2, in0=src2, in1=omg, op=ALU.mult))
                V(vpos_ev)
                ev_vpos = s_ev.n
                sc_vs = s_sc.n

                mm_kpos = s_mm.n
                ev_kpos_b = s_ev.n
                sc_kpos_b = s_sc.n

                def kpos_t(t):
                    t.wait_ge(s_ev.h, max(ev_vpos, ev_proj_b + 32))
                    t.wait_ge(s_sc.h, sc_proj_b + 32)
                    t.wait_ge(s_sc.h, sc_vs)
                    for g_ in range(8):
                        half = g_ % 2
                        if g_ >= 2:
                            t.wait_ge(s_ev.h, ev_kpos_b + 2 * (g_ - 1))
                        ins = None
                        for n in range(4):
                            ins = t.transpose(
                                pd_bf[:, half * 1024 + n * 128: half * 1024 + (n + 1) * 128],
                                kd2[:, g_ * 512 + 128 * n: g_ * 512 + 128 * (n + 1)],
                                identb[:, :])
                        s_mm.inc(ins)
                T(kpos_t)

                def kpos_v(v):
                    for g_ in range(8):
                        half = g_ % 2
                        v.wait_ge(s_mm.h, mm_kpos + g_ + 1)
                        el = elu_t[:, half * 512:(half + 1) * 512]
                        s_ev.inc(v.tensor_scalar(
                            out=el, in0=pd_bf[:, half * 1024:half * 1024 + 512],
                            scalar1=0.0, scalar2=None, op0=ALU.min))
                        v.wait_ge(s_sc.h, sc_kpos_b + g_ + 1)
                        s_ev.inc(v.scalar_tensor_tensor(
                            out=skp[:, g_ * 512:(g_ + 1) * 512],
                            in0=pd_bf[:, half * 1024:half * 1024 + 512],
                            scalar=0.0, in1=el, op0=ALU.max, op1=ALU.add))
                V(kpos_v)

                def kpos_s(s):
                    for g_ in range(8):
                        half = g_ % 2
                        el = elu_t[:, half * 512:(half + 1) * 512]
                        s.wait_ge(s_ev.h, ev_kpos_b + 2 * g_ + 1)
                        s_sc.inc(s.activation(out=el, in_=el, func=AF.Exp,
                                              bias=zero_sb[:, :], scale=1.0))
                SC(kpos_s)
                ev_kpos = s_ev.n
                sc_kpos = s_sc.n

                if is_dbg:
                    dump("vposm", vposm[:, :], [(s_ev, ev_kpos), (s_sc, sc_kpos)])
                    dump("vs", vs[:, :], [(s_ev, ev_kpos), (s_sc, sc_kpos)])
                    dump("skp", skp[:, :], [(s_ev, ev_kpos), (s_sc, sc_kpos)])

                last_ev_attO = None
                last_sc_attO = None
                attO_done = []
                ev_sq_of = {}
                mm_f1_of = {}
                wf1_seen = 0

                def f1_group(t, mcs, wf1_base=wf1_base):
                    for mc in mcs:
                        p = pa if (mc % 4) < 2 else pb
                        off = 512 * (mc % 2)
                        sl, nn = w1_ready_at[mc]
                        t.wait_ge(s_w1s[sl].h, nn)
                        if mc >= 4:
                            t.wait_ge(s_wf1.h, wf1_base + (mc - 3))
                        ins = None
                        for kc in range(8):
                            ins = t.matmul(
                                p[:, off:off + 512],
                                w1b[:, (mc % 3) * 1024 + kc * 128: (mc % 3) * 1024 + (kc + 1) * 128],
                                xn_fm[:, kc * 512:(kc + 1) * 512],
                                start=(kc == 0), stop=(kc == 7))
                        s_mm.inc(ins)
                        mm_f1_of[mc] = s_mm.n

                def f1_evac_group(v, mcs):
                    for mc in mcs:
                        p = pa if (mc % 4) < 2 else pb
                        off = 512 * (mc % 2)
                        v.wait_ge(s_mm.h, mm_f1_of[mc])
                        s_wf1.inc(v.tensor_scalar(
                            out=hsb[:, mc * 512:(mc + 1) * 512],
                            in0=p[:, off:off + 512],
                            scalar1=b1_sb[:, mc:mc + 1], scalar2=None, op0=ALU.add))

                def sq0_min(v):
                    s_ev.inc(v.tensor_scalar(out=elu_t[:, 0:512],
                                             in0=qd2[:, 0:512],
                                             scalar1=0.0, scalar2=None, op0=ALU.min))
                V(sq0_min)
                ev_sqmin0 = s_ev.n

                def sq0_exp(s, ev_sqmin0=ev_sqmin0):
                    s.wait_ge(s_ev.h, ev_sqmin0)
                    s_sc.inc(s.activation(out=elu_t[:, 0:512], in_=elu_t[:, 0:512],
                                          func=AF.Exp, bias=zero_sb[:, :], scale=1.0))
                SC(sq0_exp)
                sc_sqexp0 = s_sc.n

                def sq0_fin(v, sc_sqexp0=sc_sqexp0):
                    v.wait_ge(s_sc.h, sc_sqexp0)
                    s_ev.inc(v.scalar_tensor_tensor(
                        out=sq_p[:, 0:512], in0=qd2[:, 0:512],
                        scalar=0.0, in1=elu_t[:, 0:512], op0=ALU.max, op1=ALU.add))
                V(sq0_fin)
                ev_sq_of[0] = s_ev.n
                for g_ in range(8):
                    X, Y = pc, pd
                    X_bf = pc_bf
                    mm_sc_b = s_mm.n
                    sc_exp_b = s_sc.n

                    def pair_scores(t, g_=g_, first=(g_ == 0), sc_exp_b=sc_exp_b,
                                    X=X, Y=Y, attO_done=attO_done):
                        if first:
                            t.wait_ge(s_ev.h, ev_kpos)
                            t.wait_ge(s_sc.h, sc_kpos)
                        if g_ >= 1:
                            evn, scn = attO_done[g_ - 1]
                            t.wait_ge(s_ev.h, evn)
                            t.wait_ge(s_sc.h, scn)
                        for n in range(4):
                            psc = X if n % 2 == 0 else Y
                            if n >= 2:
                                t.wait_ge(s_sc.h, sc_exp_b + (n - 1))
                            ins = None
                            for hp in range(2):
                                ins = t.matmul(
                                    psc[:, hp * 512 + 128 * n: hp * 512 + 512],
                                    kd2[64 * hp:64 * (hp + 1),
                                        g_ * 512 + 128 * n: g_ * 512 + 128 * (n + 1)],
                                    qd2[64 * hp:64 * (hp + 1),
                                        g_ * 512 + 128 * n: (g_ + 1) * 512],
                                    start=True, stop=True, tile_position=(64 * hp, 0))
                            s_mm.inc(ins)
                    T(pair_scores)

                    def pair_exp(s, g_=g_, mm_sc_b=mm_sc_b, X=X, Y=Y):
                        for n in range(4):
                            psc = X if n % 2 == 0 else Y
                            s.wait_ge(s_mm.h, mm_sc_b + n + 1)
                            src = psc[:, :].rearrange("p (hp c) -> p hp c", hp=2)[:, :, 128 * n:512]
                            dst = pt[:, :].rearrange("p (hp n c) -> p hp n c",
                                                     hp=2, n=4)[:, :, n, 128 * n:512]
                            s_sc.inc(s.activation(out=dst, in_=src, func=AF.Exp,
                                                  bias=negc_sb[:, :], scale=0.125))
                    SC(pair_exp)
                    sc_exp = s_sc.n

                    def pair_mask(v, sc_exp=sc_exp):
                        v.wait_ge(s_sc.h, sc_exp)
                        ins = None
                        for hp in range(2):
                            for n in range(4):
                                base = hp * 2048 + n * 640
                                ins = v.tensor_tensor(
                                    out=pt[:, base:base + 128], in0=pt[:, base:base + 128],
                                    in1=mask8[:, 0:128], op=ALU.mult)
                        s_ev.inc(ins)
                    V(pair_mask)
                    ev_mask = s_ev.n

                    gn = g_ + 1
                    if gn < 8:
                        def pair_sq_min(v, gn=gn):
                            el = elu_t[:, (gn % 2) * 512:(gn % 2 + 1) * 512]
                            s_ev.inc(v.tensor_scalar(out=el,
                                                     in0=qd2[:, gn * 512:(gn + 1) * 512],
                                                     scalar1=0.0, scalar2=None, op0=ALU.min))
                        V(pair_sq_min)
                        ev_sqmin = s_ev.n

                        def pair_sq_exp(s, ev_sqmin=ev_sqmin, gn=gn):
                            el = elu_t[:, (gn % 2) * 512:(gn % 2 + 1) * 512]
                            s.wait_ge(s_ev.h, ev_sqmin)
                            s_sc.inc(s.activation(out=el, in_=el, func=AF.Exp,
                                                  bias=zero_sb[:, :], scale=1.0))
                        SC(pair_sq_exp)
                        sc_sqexp = s_sc.n

                        def pair_sq_fin(v, gn=gn, sc_sqexp=sc_sqexp):
                            el = elu_t[:, (gn % 2) * 512:(gn % 2 + 1) * 512]
                            v.wait_ge(s_sc.h, sc_sqexp)
                            s_ev.inc(v.scalar_tensor_tensor(
                                out=sq_p[:, (gn % 2) * 512:(gn % 2 + 1) * 512],
                                in0=qd2[:, gn * 512:(gn + 1) * 512],
                                scalar=0.0, in1=el, op0=ALU.max, op1=ALU.add))
                        V(pair_sq_fin)
                        ev_sq_of[gn] = s_ev.n
                    ev_sq = ev_sq_of[g_]

                    mm_mem_b = s_mm.n

                    def pair_mem(t, g_=g_, ev_sq=ev_sq, sc_exp=sc_exp, Y=Y):
                        t.wait_ge(s_ev.h, ev_sq)
                        t.wait_ge(s_sc.h, sc_exp)
                        for hp in range(2):
                            ins = None
                            for m in range(4):
                                ins = t.matmul(
                                    Y[:, hp * 512 + m * 65: hp * 512 + (m + 1) * 65],
                                    sq_p[64 * hp:64 * (hp + 1),
                                         (g_ % 2) * 512 + 128 * m: (g_ % 2) * 512 + 128 * (m + 1)],
                                    zmb[64 * hp:64 * (hp + 1), g_ * 65:(g_ + 1) * 65],
                                    start=True, stop=True, tile_position=(64 * hp, 0))
                            s_mm.inc(ins)
                    T(pair_mem)
                    mm_mem = s_mm.n

                    def pair_mupd(t, g_=g_, Y=Y):
                        for hp in range(2):
                            ins = None
                            for n in range(4):
                                ins = t.matmul(
                                    Y[64 * hp:64 * (hp + 1), 772:837],
                                    skp[:, g_ * 512 + n * 128 + hp * 64:
                                        g_ * 512 + n * 128 + (hp + 1) * 64],
                                    vposm[:, g_ * 520 + n * 130 + hp * 65:
                                          g_ * 520 + n * 130 + (hp + 1) * 65],
                                    start=(n == 0), stop=(n == 3))
                            s_mm.inc(ins)
                    T(pair_mupd)
                    mm_mupd = s_mm.n

                    mm_dot_b = s_mm.n

                    def pair_dot(t, g_=g_, ev_mask=ev_mask, X=X):
                        t.wait_ge(s_ev.h, ev_mask)
                        for hp in range(2):
                            for m in range(4):
                                ins = None
                                for n in range(m + 1):
                                    ins = t.matmul(
                                        X[:, hp * 512 + m * 65: hp * 512 + (m + 1) * 65],
                                        pt[:, hp * 2048 + n * 512 + 128 * m:
                                           hp * 2048 + n * 512 + 128 * (m + 1)],
                                        vs[:, g_ * 520 + n * 130 + hp * 65:
                                           g_ * 520 + n * 130 + (hp + 1) * 65],
                                        start=(n == 0), stop=(n == m))
                                s_mm.inc(ins)
                    T(pair_dot)
                    mm_dot = s_mm.n

                    T(lambda t, g_=g_: f1_group(t, range(4 * g_, 4 * g_ + 2)))

                    def pair_recip(v, mm_mem=mm_mem, mm_dot=mm_dot, X=X, Y=Y):
                        v.wait_ge(s_mm.h, mm_mem)
                        den = Y[:, :].rearrange("p (hp q) -> p hp q", hp=2)[:, :, 0:260]
                        den = den.rearrange("p hp (x c) -> p hp x c", c=65)[:, :, :, 0]
                        v.tensor_scalar(out=rd4[:, 0:8].rearrange("p (hp x) -> p hp x", hp=2),
                                        in0=den, scalar1=tiny_sb[:, :],
                                        scalar2=None, op0=ALU.add)
                        v.drain()
                        v.reciprocal(out=rd4[:, 0:8], in_=rd4[:, 0:8])
                        v.wait_ge(s_mm.h, mm_dot)
                        sden = X[:, :].rearrange("p (hp q) -> p hp q", hp=2)[:, :, 0:260]
                        sden = sden.rearrange("p hp (x c) -> p hp x c", c=65)[:, :, :, 0]
                        s_ev.inc(v.reciprocal(out=rd4[:, 8:16].rearrange("p (hp x) -> p hp x", hp=2),
                                              in_=sden))
                    V(pair_recip)
                    ev_recip = s_ev.n

                    def pair_memn(v, ev_recip=ev_recip, Y=Y):
                        v.drain()
                        ins = None
                        for hp in range(2):
                            for m in range(4):
                                x = hp * 4 + m
                                ins = v.tensor_scalar(
                                    out=memn[:, x * 64:(x + 1) * 64],
                                    in0=Y[:, hp * 512 + m * 65 + 1: hp * 512 + (m + 1) * 65],
                                    scalar1=rd4[:, x:x + 1], scalar2=None, op0=ALU.mult)
                        s_sc.inc(ins)
                    V(pair_memn)
                    sc_memn = s_sc.n

                    def pair_comb(v, sc_memn=sc_memn, X=X):
                        v.drain()
                        ins = None
                        for hp in range(2):
                            for m in range(4):
                                x = hp * 4 + m
                                ins = v.scalar_tensor_tensor(
                                    out=att_p[:, x * 64:(x + 1) * 64],
                                    in0=X[:, hp * 512 + m * 65 + 1: hp * 512 + (m + 1) * 65],
                                    scalar=rd4[:, 8 + x:9 + x],
                                    in1=memn[:, x * 64:(x + 1) * 64],
                                    op0=ALU.mult, op1=ALU.add)
                        s_ev.inc(ins)
                    V(pair_comb)
                    ev_att = s_ev.n

                    if is_dbg and g_ == 0:
                        dump("pt", pt[:, :], [(s_ev, ev_mask)])
                        dump("sq", sq_p[:, :], [(s_ev, ev_sq)])
                        dump("att", att_p[:, :], [(s_ev, ev_att)])

                    def pair_state(v, g_=g_, mm_mupd=mm_mupd, Y=Y):
                        v.wait_ge(s_mm.h, mm_mupd)
                        v.tensor_tensor(out=sscr[:, 0:65], in0=Y[:, 772:837],
                                        in1=gmp[:, g_ * 65:(g_ + 1) * 65], op=ALU.mult)
                        v.drain()
                        s_ev.inc(v.tensor_tensor(out=zmf[:, g_ * 65:(g_ + 1) * 65],
                                                 in0=zmf[:, g_ * 65:(g_ + 1) * 65],
                                                 in1=sscr[:, 0:65], op=ALU.add))
                    V(pair_state)
                    V(lambda v, g_=g_: f1_evac_group(v, [4 * g_]))

                    mm_attT_b = s_mm.n

                    def pair_attT(t, ev_att=ev_att, X_bf=X_bf):
                        t.wait_ge(s_ev.h, ev_att)
                        ins = None
                        for hp in range(2):
                            for m in range(4):
                                x = hp * 4 + m
                                ins = t.transpose(
                                    X_bf[64 * hp:64 * (hp + 1), 1280 + m * 128: 1280 + (m + 1) * 128],
                                    att_p[:, x * 64:(x + 1) * 64],
                                    identb[:, :])
                        s_mm.inc(ins)
                    T(pair_attT)
                    mm_attT = s_mm.n

                    T(lambda t, g_=g_: f1_group(t, range(4 * g_ + 2, 4 * g_ + 4)))

                    def pair_attO(is_v, g_=g_, mm_attT=mm_attT, X_bf=X_bf):
                        def f(en):
                            en.wait_ge(s_mm.h, mm_attT)
                            for hp in range(2):
                                h_ = 2 * g_ + hp
                                for cpar in range(2):
                                    if (hp + cpar) % 2 != (0 if is_v else 1):
                                        continue
                                    src = X_bf[64 * hp:64 * (hp + 1), 1280:1792].rearrange(
                                        "p (m a t2) -> p m a t2",
                                        m=4, a=8)[:, :, :, cpar * 8:(cpar + 1) * 8]
                                    src = src.transpose([0, 2, 1, 3])
                                    dst = atto[64 * cpar:64 * (cpar + 1), :].rearrange(
                                        "p (a i) -> p a i", a=8)[:, :, 32 * h_:32 * h_ + 32]
                                    dst = dst.rearrange("p a (m tw) -> p a m tw", m=4)
                                    ins = en.tensor_copy(out=dst, in_=src)
                                    (s_ev if is_v else s_sc).inc(ins)
                        return f
                    V(pair_attO(True))
                    V(pair_attO(False))
                    last_ev_attO = s_ev.n
                    last_sc_attO = s_sc.n
                    attO_done.append((last_ev_attO, last_sc_attO))

                    V(lambda v, g_=g_: f1_evac_group(v, range(4 * g_ + 1, 4 * g_ + 4)))

                if is_dbg:
                    dump("atto", atto[:, :], [(s_ev, last_ev_attO), (s_sc, last_sc_attO)])
                    dump("zmf", zmf[:, :], [(s_ev, s_ev.n)])

                wf1_done = s_wf1.n

                def f1_gelu(s, wf1_done=wf1_done):
                    s.wait_ge(s_wf1.h, wf1_done)
                    for q in range(4):
                        s_sc.inc(s.activation(out=hsb[:, q * 4096:(q + 1) * 4096],
                                              in_=hsb[:, q * 4096:(q + 1) * 4096],
                                              func=AF.Gelu, bias=zero_sb[:, :],
                                              scale=1.0))
                SC(f1_gelu)
                sc_gelu = s_sc.n

                mm_wo_b = s_mm.n
                ev_wo_b = s_ev.n

                def wo_t(t, a=None, b_=None):
                    t.wait_ge(s_ev.h, last_ev_attO)
                    t.wait_ge(s_sc.h, last_sc_attO)
                    t.wait_ge(s_wf1.h, wf1_done)
                    for ic in range(4):
                        p = pa if ic % 2 == 0 else pb
                        if ic >= 2:
                            t.wait_ge(s_ev.h, ev_wo_b + (ic - 1))
                        ins = None
                        for half in range(2):
                            for a in range(8):
                                ins = t.matmul(
                                    p[:, half * 512:(half + 1) * 512],
                                    atto[:, a * 512 + 128 * ic: a * 512 + 128 * (ic + 1)],
                                    Wo_sb[:, a * 1024 + half * 512: a * 1024 + (half + 1) * 512],
                                    start=(a == 0), stop=(a == 7))
                        s_mm.inc(ins)
                T(wo_t)

                def wo_ev(v, mm_wo_b=mm_wo_b):
                    for ic in range(4):
                        p = pa if ic % 2 == 0 else pb
                        v.wait_ge(s_mm.h, mm_wo_b + ic + 1)
                        if skip_wo:
                            s_ev.inc(v.tensor_copy(out=rd4[:, 1:2], in_=rd4[:, 1:2]))
                        else:
                            s_ev.inc(v.tensor_tensor(out=acc[:, ic * 1024:(ic + 1) * 1024],
                                                     in0=acc[:, ic * 1024:(ic + 1) * 1024],
                                                     in1=p[:, 0:1024], op=ALU.add))
                V(wo_ev)
                ev_wo = s_ev.n

                if is_dbg:
                    dump("accw", acc[:, :], [(s_ev, ev_wo)])

                mm_f2_b = s_mm.n
                ev_f2_b = s_ev.n

                def f2_t(t, sc_gelu=sc_gelu, ev_wo=ev_wo):
                    if skip_ffn:
                        return
                    t.wait_ge(s_sc.h, sc_gelu)
                    t.wait_ge(s_ev.h, ev_wo)
                    for p_ in range(2):
                        if p_ == 1:
                            t.wait_ge(s_ev.h, ev_f2_b + 2)
                        for kc in range(32):
                            sl, nn = w2_ready_at[(p_, kc)]
                            t.wait_ge(s_w2s[sl].h, nn)
                            ins = None
                            for ic2 in range(2):
                                ic = 2 * p_ + ic2
                                p = pa if ic2 == 0 else pb
                                for half in range(2):
                                    ins = t.matmul(
                                        p[:, half * 512:(half + 1) * 512],
                                        hsb[:, kc * 512 + 128 * ic: kc * 512 + 128 * (ic + 1)],
                                        w2b[:, ((32 * p_ + kc) % 3) * 1024 + half * 512:
                                            ((32 * p_ + kc) % 3) * 1024 + (half + 1) * 512],
                                        start=(kc == 0), stop=(kc == 31))
                            s_wf2.inc(ins)
                T(f2_t)

                def f2_ev(v, wf2_base=wf2_base):
                    if skip_ffn:
                        ins = None
                        for i_ in range(64):
                            ins = v.tensor_copy(out=rd4[:, 0:1], in_=rd4[:, 0:1])
                            s_wf2.inc(ins)
                        return
                    for p_ in range(2):
                        v.wait_ge(s_wf2.h, wf2_base + 32 * (p_ + 1))
                        for ic2 in range(2):
                            ic = 2 * p_ + ic2
                            p = pa if ic2 == 0 else pb
                            s_ev.inc(v.tensor_tensor(
                                out=acc[:, ic * 1024:(ic + 1) * 1024],
                                in0=acc[:, ic * 1024:(ic + 1) * 1024],
                                in1=p[:, 0:1024], op=ALU.add))
                V(f2_ev)
                ev_f2 = s_ev.n

                if is_dbg:
                    dump("acc", acc[:, :], [(s_ev, ev_f2)])

                def store(sy, b=b, seg=seg, ev_f2=ev_f2):
                    sy.wait_ge(s_ev.h, ev_f2)
                    for ic in range(4):
                        s_stic[ic].inc(sy.dma_start(
                            out=ys[b, seg * 512 + 128 * ic: seg * 512 + 128 * (ic + 1), :],
                            in_=acc[:, ic * 1024:(ic + 1) * 1024]), 16)
                SY(store)

        def fin(sy):
            sy.wait_ge(s_st.h, s_st.n)
            for i in range(4):
                sy.wait_ge(s_stic[i].h, s_stic[i].n)
        SY(fin)

    return nc


def host_arrange_inputs(inp, nb=4, core=0):
    x = np.ascontiguousarray(inp["L0_attn_x"][core * nb:(core + 1) * nb])
    betas = np.ascontiguousarray(inp["L0_attn_betas"].reshape(H, DV))
    m0 = inp["L0_attn_memory0"].reshape(H, DK, DV)
    z0 = inp["L0_attn_z0"].reshape(H, DK)
    m0_arr = np.zeros((128, 512), dtype=np.float32)
    z0_arr = np.zeros((128, 8), dtype=np.float32)
    for g in range(8):
        for hp in range(2):
            h = 2 * g + hp
            m0_arr[64 * hp:64 * (hp + 1), g * 64:(g + 1) * 64] = m0[h]
            z0_arr[64 * hp:64 * (hp + 1), g] = z0[h]
    return {
        "xs": x, "Wq": inp["L0_attn_Wq"], "Wk": inp["L0_attn_Wk"],
        "Wv": inp["L0_attn_Wv"], "Wo": inp["L0_attn_Wo"],
        "w1": inp["L0_ffn_w1"], "b1": inp["L0_ffn_b1"],
        "w2": inp["L0_ffn_w2"], "b2": inp["L0_ffn_b2"],
        "betas": betas, "m0_arr": m0_arr, "z0_arr": z0_arr,
    }


_NCORES = 8


def kernel(**inputs):
    from concourse.bass_utils import run_bass_kernel_spmd
    in_maps = [host_arrange_inputs(inputs, nb=4, core=c) for c in range(_NCORES)]
    nc = build(nb=4, nsegs=4)
    res = run_bass_kernel_spmd(nc, in_maps, core_ids=list(range(_NCORES)))
    out = np.concatenate([res.results[c]["ys"] for c in range(_NCORES)], axis=0)
    return out.astype(np.float32)
